# version 26
# baseline (speedup 1.0000x reference)
import contextlib
import numpy as np
import ml_dtypes
import concourse.bass as bass
import concourse.mybir as mybir
from concourse.bass_utils import run_bass_kernel_spmd

F32 = mybir.dt.float32
BF16 = mybir.dt.bfloat16
AX = mybir.AxisListType.X
ALU = mybir.AluOpType
AF = mybir.ActivationFunctionType

B = 8
T = 2048
NT = T // 128
D = 1024
ND = D // 128
H = 16
E = 8
DFF = 4096
CAP = 320
GCAP = 2560
EPS = 1e-5

USE_COLLECTIVE = True
DEBUG_DUMPS = False


class Plan:
    def __init__(self):
        self.ops = {k: [] for k in ("t", "v", "s", "g", "y")}
        self.n = {}
        self.sems = {}

    def w(self, eng, sem, val):
        if val is None or val <= 0:
            return
        h = self.sems[sem]
        self.ops[eng].append(lambda e, h=h, v=val: e.wait_ge(h, v))

    def op(self, eng, fn, inc=None, by=1):
        v, h = None, None
        if inc is not None:
            v = self.n[inc] = self.n.get(inc, 0) + by
            h = self.sems[inc]

        def run(e, fn=fn, h=h, by=by):
            inst = fn(e)
            if h is not None:
                inst.then_inc(h, by)
        self.ops[eng].append(run)
        return v

    def cur(self, sem):
        return self.n.get(sem, 0)

    def vself(self):
        self.w("v", "vs", self.cur("vs"))

    def sself(self):
        self.w("s", "ss", self.cur("ss"))


class Arena:
    def __init__(self, tensor, nelem):
        self.tensor = tensor
        self.free = [(0, nelem)]
        self.used = {}
        self.peak = 0
        self.curn = 0

    def alloc(self, name, elems):
        elems = (elems + 127) & ~127
        for i, (off, ln) in enumerate(self.free):
            if ln >= elems:
                self.used[name] = (off, elems)
                if ln == elems:
                    self.free.pop(i)
                else:
                    self.free[i] = (off + elems, ln - elems)
                self.curn += elems
                self.peak = max(self.peak, self.curn)
                return off
        raise MemoryError(f"arena OOM: {name} ({elems}) free={self.free}")

    def release(self, name):
        off, ln = self.used.pop(name)
        self.curn -= ln
        self.free.append((off, ln))
        self.free.sort()
        m = []
        for o, l in self.free:
            if m and m[-1][0] + m[-1][1] == o:
                m[-1] = (m[-1][0], m[-1][1] + l)
            else:
                m.append((o, l))
        self.free = m

    def ap(self, name, p0, p1, a, b):
        off, ln = self.used[name]
        assert 0 <= a <= b <= ln, (name, a, b, ln)
        return self.tensor[p0:p1, off + a: off + b]


def build_program():
    nc = bass.Bass("TRN2", target_bir_lowering=False)

    x0 = nc.dram_tensor("x0", [T, D], F32, kind="ExternalInput").ap()
    wqk = nc.dram_tensor("wqk", [D, 2048], F32, kind="ExternalInput").ap()
    bqkc = nc.dram_tensor("bqkc", [128, 16], F32, kind="ExternalInput").ap()
    wv = nc.dram_tensor("wv", [D, D], F32, kind="ExternalInput").ap()
    bvr = nc.dram_tensor("bvr", [1, D], F32, kind="ExternalInput").ap()
    wmsa = nc.dram_tensor("wmsa", [D, D], F32, kind="ExternalInput").ap()
    bmsar = nc.dram_tensor("bmsar", [1, D], F32, kind="ExternalInput").ap()
    gateT = nc.dram_tensor("gateT", [D, E], F32, kind="ExternalInput").ap()
    w1 = nc.dram_tensor("w1", [E, D, DFF], F32, kind="ExternalInput").ap()
    b1c = nc.dram_tensor("b1c", [E, 128, 32], F32, kind="ExternalInput").ap()
    w2 = nc.dram_tensor("w2", [E, DFF, D], F32, kind="ExternalInput").ap()
    b2 = nc.dram_tensor("b2", [E, D], F32, kind="ExternalInput").ap()
    prefix = nc.dram_tensor("prefix", [E, 1], F32, kind="ExternalInput").ap()
    c_ident = nc.dram_tensor("c_ident", [128, 128], F32, kind="ExternalInput").ap()
    c_iotaC = nc.dram_tensor("c_iotaC", [128, CAP], F32, kind="ExternalInput").ap()
    c_iotac3 = nc.dram_tensor("c_iotac3", [128, 3], F32, kind="ExternalInput").ap()
    c_ltriu = nc.dram_tensor("c_ltriu", [128, 128], F32, kind="ExternalInput").ap()
    c_iota8 = nc.dram_tensor("c_iota8", [128, 128], F32, kind="ExternalInput").ap()
    out = nc.dram_tensor("out", [T, D], F32, kind="ExternalOutput").ap()
    xlnd = nc.dram_tensor("xlnd", [T, D], F32)
    rbd = nc.dram_tensor("rbd", [2, 1024], BF16)
    rbd2 = nc.dram_tensor("rbd2", [2, 1024], F32)
    ped = nc.dram_tensor("ped", [128, 128], BF16)
    cc_in = nc.dram_tensor("cc_in", [1, E], F32)
    cc_out = nc.dram_tensor("cc_out", [E, E], F32)
    if DEBUG_DUMPS:
        dbg_qkT = nc.dram_tensor("dbg_qkT", [128, 16 * T], BF16)
        dbg_v = nc.dram_tensor("dbg_v", [128, NT * H * 65], BF16)
        dbg_x0T = nc.dram_tensor("dbg_x0T", [128, ND * T], BF16)

    pl = Plan()

    with contextlib.ExitStack() as ctx:
        EN = ctx.enter_context
        ARB_N = 82432
        ARF_N = 7168
        arb_t = EN(nc.sbuf_tensor("arb", [128, ARB_N], BF16))
        arf_t = EN(nc.sbuf_tensor("arf", [128, ARF_N], F32))
        arb = Arena(arb_t, ARB_N)
        arf = Arena(arf_t, ARF_N)

        s_id = EN(nc.sbuf_tensor("s_id", [128, 128], F32))
        s_iotaC = EN(nc.sbuf_tensor("s_iotaC", [128, CAP], F32))
        s_iotac3 = EN(nc.sbuf_tensor("s_iotac3", [128, 3], F32))
        s_iota8 = EN(nc.sbuf_tensor("s_iota8", [128, 128], F32))
        s_ltriuf = EN(nc.sbuf_tensor("s_ltriuf", [128, 128], F32))
        s_onesb = EN(nc.sbuf_tensor("s_onesb", [128, 128], BF16))
        s_onesf = EN(nc.sbuf_tensor("s_onesf", [1, 128], F32))
        s_onescf = EN(nc.sbuf_tensor("s_onescf", [128, 1], F32))
        s_bqkc = EN(nc.sbuf_tensor("s_bqkc", [128, 16], F32))
        s_bvr = EN(nc.sbuf_tensor("s_bvr", [1, D], BF16))
        s_bmsar = EN(nc.sbuf_tensor("s_bmsar", [1, D], BF16))
        s_gateT = EN(nc.sbuf_tensor("s_gateT", [128, ND * E], F32))
        s_b1c = EN(nc.sbuf_tensor("s_b1c", [128, E * 32], F32))
        s_b2bc = EN(nc.sbuf_tensor("s_b2bc", [128, D], F32))
        s_prefix = EN(nc.sbuf_tensor("s_prefix", [E, 1], F32))
        s_ag = EN(nc.sbuf_tensor("s_ag", [E, E], F32))
        s_toff = EN(nc.sbuf_tensor("s_toff", [1, E], F32))
        s_off = EN(nc.sbuf_tensor("s_off", [1, E], F32))
        s_offbc = EN(nc.sbuf_tensor("s_offbc", [128, E], F32))
        s_mx = EN(nc.sbuf_tensor("s_mx", [1, 16], F32))
        s_mxb = EN(nc.sbuf_tensor("s_mxb", [1, 1], BF16))
        s_negc = EN(nc.sbuf_tensor("s_negc", [128, 1], F32))
        s_M1 = EN(nc.sbuf_tensor("s_M1", [128, NT * E], BF16))
        s_M1f = EN(nc.sbuf_tensor("s_M1f", [128, NT * E], F32))
        s_ranks = EN(nc.sbuf_tensor("s_ranks", [128, NT * E], F32))
        s_cols = EN(nc.sbuf_tensor("s_cols", [128, NT * 4], F32))
        s_gs = EN(nc.sbuf_tensor("s_gs", [128, NT * 3], F32))
        s_lgtall = EN(nc.sbuf_tensor("s_lgtall", [128, NT * E], F32))
        s_ls = EN(nc.sbuf_tensor("s_ls", [128, NT * E], F32))
        s_les = EN(nc.sbuf_tensor("s_les", [128, NT * E], F32))
        s_gateTb = EN(nc.sbuf_tensor("s_gateTb", [128, 2 * ND * E], BF16))
        s_idb = EN(nc.sbuf_tensor("s_idb", [128, 128], BF16))
        s_stat = EN(nc.sbuf_tensor("s_stat", [128, 8], F32))
        s_pec = EN(nc.sbuf_tensor("s_pec", [128, 1], F32))
        s_eps = EN(nc.sbuf_tensor("s_eps", [128, 1], F32))
        s_cnt = EN(nc.sbuf_tensor("s_cnt", [1, E], F32))

        P0 = EN(nc.psum_tensor("P0", [128, 1024], F32))
        P1 = EN(nc.psum_tensor("P1", [128, 1024], F32))
        P2 = EN(nc.psum_tensor("P2", [128, 1024], F32))
        P3 = EN(nc.psum_tensor("P3", [128, 512], F32))
        P4 = EN(nc.psum_tensor("P4", [128, 512], F32))

        for nm in ("ts", "vs", "ss", "cs", "gs", "dC", "dT", "dB", "dXC",
                   "dQKa", "dQKb", "dWV", "dMSA", "dW1a", "dW1b", "dW1c", "dW2a",
                   "dW2b", "dW2c", "dSTG", "dSTGb", "dC2", "dB2", "dB2b", "dAG", "dx0c", "dx0d", "dx0a", "dx0b", "dxsa", "dxsb",
                   "dxla", "dxlb", "dOa", "dOb", "dBCa", "dBCb"):
            pl.sems[nm] = EN(nc.semaphore(nm))

        def yld(dst, src, sem):
            return pl.op("y", lambda e, d=dst, s=src: e.dma_start(out=d, in_=s),
                         inc=sem, by=16)

        def gld(dst, src, sem):
            return pl.op("g", lambda e, d=dst, s=src: e.dma_start(out=d, in_=s),
                         inc=sem, by=16)

        def MM(o, l, r, st, sp, inc=None):
            return pl.op("t", lambda e, o=o, l=l, r=r, st=st, sp=sp:
                         e.matmul(o, l, r, start=st, stop=sp), inc=inc)

        v_id = yld(s_id[:, :], c_ident, "dC2")

        arb.alloc("x0T", ND * T)
        arf.alloc("x0ld", 4 * 1024)
        X0SEM = ("dx0a", "dx0b", "dx0c", "dx0d")
        x0ld_v = {}
        for tt in range(4):
            x0ld_v[tt] = yld(
                arf.ap("x0ld", 0, 128, (tt % 4) * 1024, (tt % 4) * 1024 + 1024),
                x0[tt * 128:(tt + 1) * 128, :], X0SEM[tt % 4])

        yld(s_iotaC[:, :], c_iotaC, "dC")
        yld(s_iotac3[:, :], c_iotac3, "dC")
        yld(s_iota8[:, :], c_iota8, "dC")
        yld(s_ltriuf[:, :], c_ltriu, "dC")
        yld(s_bqkc[:, :], bqkc, "dC")
        yld(s_gateT.ap().rearrange("p (dj e) -> p dj e", e=E),
            gateT.rearrange("(dj p) e -> p dj e", p=128), "dC")
        yld(s_b1c.ap().rearrange("p (e m) -> p e m", m=32),
            b1c.rearrange("e p m -> p e m"), "dC")
        v_consts = yld(s_prefix[:, :], prefix, "dC")

        gld(s_bvr[:, :], bvr, "dB")
        gld(s_bmsar[:, :], bmsar, "dB")

        x0t_evac = {}
        pl.w("t", "dC2", v_id)
        for tt in range(NT):
            pl.w("t", X0SEM[tt % 4], x0ld_v[tt])
            for dj in range(ND):
                k2 = tt * ND + dj
                psl = (P3 if k2 % 2 == 0 else P4)[0:128, 0:128]
                if k2 >= 2:
                    pl.w("t", "vs", x0t_evac[k2 - 2])
                src_ap = arf.ap("x0ld", 0, 128, (tt % 4) * 1024 + dj * 128,
                                (tt % 4) * 1024 + dj * 128 + 128)
                tv = pl.op("t", lambda e, o=psl, s=src_ap:
                           e.transpose(o, s, s_id[:, :]), inc="ts")
                pl.w("v", "ts", tv)
                xdst = arb.ap("x0T", 0, 128, dj * T + tt * 128, dj * T + tt * 128 + 128)
                x0t_evac[k2] = pl.op("v", lambda e, d=xdst, p=psl:
                                     e.tensor_copy(d, p), inc="vs")
            if tt + 4 < NT:
                pl.w("y", "ts", pl.cur("ts"))
                x0ld_v[tt + 4] = yld(
                    arf.ap("x0ld", 0, 128, (tt % 4) * 1024, (tt % 4) * 1024 + 1024),
                    x0[(tt + 4) * 128:(tt + 5) * 128, :], X0SEM[tt % 4])
        v_x0T = pl.cur("vs")
        ts_x0t = pl.cur("ts")
        arf.release("x0ld")

        pl.w("v", "dC", v_consts)
        pl.op("v", lambda e: e.memset(s_onesb[:, :], 1.0))
        pl.op("v", lambda e: e.memset(s_onesf[:, :], 1.0))
        pl.op("v", lambda e: e.memset(s_onescf[:, :], 1.0))
        pl.op("v", lambda e: e.memset(s_eps[:, :], EPS))
        pl.op("v", lambda e: e.memset(s_toff[:, :], 0.0))
        pl.op("v", lambda e: e.tensor_copy(s_idb[:, :], s_id[:, :]))
        pl.op("v", lambda e: e.tensor_copy(s_gateTb[:, 0:64], s_gateT[:, :]), inc="vs")
        pl.vself()
        pl.op("v", lambda e: e.tensor_sub(s_ls[:, 0:64], s_gateT[:, :],
                                          s_gateTb[:, 0:64]), inc="vs")
        pl.vself()
        v_prep = pl.op("v", lambda e: e.tensor_copy(s_gateTb[:, 64:128],
                                                    s_ls[:, 0:64]), inc="vs")

        arb.alloc("qkT", 16 * T)
        arb.alloc("v", NT * H * 65)
        arb.alloc("wqkbuf", 2 * ND * 128)
        arb.alloc("wvbuf", ND * 1024)

        vones_ap = arb.ap("v", 0, 128, 0, NT * H * 65).rearrange(
            "p (t c) -> p t c", c=65)[:, :, 64:65]
        v_vones = pl.op("v", lambda e, a=vones_ap: e.memset(a, 1.0), inc="vs")

        ct_order = []
        for j in range(8):
            ct_order += [j, 8 + j]

        def plan_wqk_load(i):
            dst = arb.ap("wqkbuf", 0, 128, (i % 2) * 1024, (i % 2) * 1024 + 1024)
            dst = dst.rearrange("p (dj c) -> p dj c", c=128)
            ct = ct_order[i]
            src = wqk[:, ct * 128:(ct + 1) * 128].rearrange("(dj p) c -> p dj c", p=128)
            sem = "dQKa" if i % 2 == 0 else "dQKb"
            return (sem, gld(dst, src, sem))

        wqk_v = [plan_wqk_load(0), plan_wqk_load(1)]
        wv_v = []
        for dj in range(ND):
            wv_v.append(gld(arb.ap("wvbuf", 0, 128, dj * 1024, (dj + 1) * 1024),
                            wv[dj * 128:(dj + 1) * 128, :], "dWV"))

        qk_slotsets = [[(P0, 0), (P0, 512), (P1, 0), (P1, 512)],
                       [(P2, 0), (P2, 512), (P3, 0), (P4, 0)]]
        qk_evac = {}
        qk_mm_done = {}
        pl.w("t", "vs", v_x0T)
        for i, ct in enumerate(ct_order):
            pl.w("t", wqk_v[i][0], wqk_v[i][1])
            sset = qk_slotsets[i % 2]
            if i >= 2:
                for tch in range(4):
                    pl.w("t", "vs", qk_evac[(i - 2) * 4 + tch])
            for dj in range(ND):
                lhs = arb.ap("wqkbuf", 0, 128, (i % 2) * 1024 + dj * 128,
                             (i % 2) * 1024 + dj * 128 + 128)
                for tch in range(4):
                    pt, pc = sset[tch]
                    rhs = arb.ap("x0T", 0, 128, dj * T + tch * 512,
                                 dj * T + tch * 512 + 512)
                    MM(pt[:, pc:pc + 512], lhs, rhs, dj == 0, dj == ND - 1,
                       inc="ts" if (dj == ND - 1 and tch == 3) else None)
            v_mm = pl.cur("ts")
            pl.w("v", "ts", v_mm)
            for tch in range(4):
                pt, pc = sset[tch]
                dst = arb.ap("qkT", 0, 128, ct * T + tch * 512, ct * T + tch * 512 + 512)
                qk_evac[i * 4 + tch] = pl.op(
                    "v", lambda e, d=dst, p=pt[:, pc:pc + 512], b=s_bqkc[:, ct:ct + 1]:
                    e.tensor_scalar(d, p, b, None, ALU.add), inc="vs")
            qk_mm_done[i] = pl.cur("ts")
            if i + 2 < 16:
                pl.w("g", "ts", qk_mm_done[i])
                wqk_v.append(plan_wqk_load(i + 2))

        pl.w("t", "dWV", wv_v[-1])
        pl.w("t", "dB", 32)
        pl.w("t", "vs", v_vones)
        for tch in range(4):
            pl.w("t", "vs", qk_evac[14 * 4 + tch])
        v_evac = {}
        for tt in range(NT):
            vpt = P0 if tt % 2 == 0 else P1
            if tt >= 2:
                pl.w("t", "vs", v_evac[tt - 2])
            for dj in range(ND):
                lhs = arb.ap("x0T", 0, 128, dj * T + tt * 128, dj * T + tt * 128 + 128)
                for nch in range(2):
                    rhs = arb.ap("wvbuf", 0, 128, dj * 1024 + nch * 512,
                                 dj * 1024 + nch * 512 + 512)
                    MM(vpt[:, nch * 512:nch * 512 + 512], lhs, rhs, dj == 0, False)
            for nch in range(2):
                MM(vpt[:, nch * 512:nch * 512 + 512], s_onesb[0:1, :],
                   s_bvr[0:1, nch * 512:nch * 512 + 512], False, True,
                   inc="ts" if nch == 1 else None)
            v_mm = pl.cur("ts")
            pl.w("v", "ts", v_mm)
            vsrc = vpt.ap().rearrange("p (h c) -> p h c", c=64)
            vdst = arb.ap("v", 0, 128, tt * H * 65, (tt + 1) * H * 65).rearrange(
                "p (h c) -> p h c", c=65)[:, :, 0:64]
            v_evac[tt] = pl.op("v", lambda e, d=vdst, s=vsrc:
                               e.tensor_copy(d, s), inc="vs")
        v_vdone = pl.cur("vs")
        ts_vdone = pl.cur("ts")
        if DEBUG_DUMPS:
            pl.w("y", "ts", ts_vdone)
            pl.w("y", "vs", v_vdone)
            yld2 = lambda d, s: pl.op("y", lambda e, d=d, s=s: e.dma_start(out=d, in_=s),
                                      inc="dC", by=16)
            yld2(dbg_x0T.ap(), arb.ap("x0T", 0, 128, 0, ND * T))
            yld2(dbg_qkT.ap(), arb.ap("qkT", 0, 128, 0, 16 * T))
            yld2(dbg_v.ap(), arb.ap("v", 0, 128, 0, NT * H * 65))
            pl.w("v", "dC", pl.cur("dC"))
        arb.release("x0T")
        arb.release("wqkbuf")
        arb.release("wvbuf")

        arb.alloc("sq", 2 * T)
        norm_done = {}
        for grp, base in (("q", 0), ("k", 8)):
            for j in range(8):
                buf = j % 2
                if j >= 2:
                    pl.w("v", "ts", norm_done[(grp, j - 2)])
                elif grp == "k":
                    pl.w("v", "ts", norm_done[("q", 6 + j)])
                src = arb.ap("qkT", 0, 128, (base + j) * T, (base + j) * T + T)
                dst = arb.ap("sq", 0, 128, buf * T, buf * T + T)
                pl.w("v", "vs", qk_evac[ct_order.index(base + j) * 4 + 3])
                sqv = pl.op("v", lambda e, d=dst, s=src: e.tensor_mul(d, s, s), inc="vs")
                pl.w("t", "vs", sqv)
                chunks = ([(P0, 0), (P0, 512), (P1, 0), (P1, 512)] if grp == "q"
                          else [(P2, 0), (P2, 512), (P3, 0), (P4, 0)])
                for ci, (pt, pc) in enumerate(chunks):
                    rhs = arb.ap("sq", 0, 128, buf * T + ci * 512, buf * T + ci * 512 + 512)
                    MM(pt[0:1, pc:pc + 512], s_onesb[:, 0:1], rhs, j == 0, j == 7,
                       inc="ts" if ci == 3 else None)
                norm_done[(grp, j)] = pl.cur("ts")
        arb.release("sq")
        pl.w("v", "ts", norm_done[("k", 7)])
        pl.op("v", lambda e: e.reduce_max(s_mx[0:1, 0:1], P0[0:1, 0:1024], axis=AX))
        pl.op("v", lambda e: e.reduce_max(s_mx[0:1, 1:2], P1[0:1, 0:1024], axis=AX))
        pl.op("v", lambda e: e.reduce_max(s_mx[0:1, 2:3], P2[0:1, 0:1024], axis=AX))
        pl.op("v", lambda e: e.reduce_max(s_mx[0:1, 3:4], P3[0:1, 0:512], axis=AX))
        pl.op("v", lambda e: e.reduce_max(s_mx[0:1, 4:5], P4[0:1, 0:512], axis=AX),
              inc="vs")
        pl.vself()
        pl.op("v", lambda e: e.tensor_max(s_mx[0:1, 5:6], s_mx[0:1, 0:1], s_mx[0:1, 1:2]))
        pl.op("v", lambda e: e.tensor_max(s_mx[0:1, 6:7], s_mx[0:1, 3:4], s_mx[0:1, 4:5]),
              inc="vs")
        pl.vself()
        pl.op("v", lambda e: e.tensor_max(s_mx[0:1, 7:8], s_mx[0:1, 6:7], s_mx[0:1, 2:3]),
              inc="vs")
        pl.vself()
        pl.op("v", lambda e: e.tensor_mul(s_mx[0:1, 8:9], s_mx[0:1, 5:6], s_mx[0:1, 7:8]),
              inc="vs")
        v_prod = pl.cur("vs")
        pl.w("s", "vs", v_prod)
        s_sqr = pl.op("s", lambda e: e.activation(s_mx[0:1, 9:10], s_mx[0:1, 8:9], AF.Sqrt),
                      inc="ss")
        pl.w("v", "ss", s_sqr)
        pl.op("v", lambda e: e.tensor_scalar_mul(s_mx[0:1, 10:11], s_mx[0:1, 9:10],
                                                 -0.125), inc="vs")
        pl.vself()
        pl.op("v", lambda e: e.tensor_copy(s_mxb[0:1, 0:1], s_mx[0:1, 10:11]), inc="vs")
        v_mxb = pl.cur("vs")
        pl.w("t", "vs", v_mxb)
        t_negbc = MM(P3[0:128, 0:1], s_onesb[0:1, :], s_mxb[0:1, 0:1], True, True,
                     inc="ts")
        pl.w("v", "ts", t_negbc)
        v_negc = pl.op("v", lambda e: e.tensor_copy(s_negc[:, 0:1], P3[0:128, 0:1]),
                       inc="vs")

        arb.alloc("attnT", ND * T)
        arb.alloc("exp", 6 * 1024)
        arb.alloc("rbc", 2 * 1024)
        arb.alloc("recip", 2 * 128)

        w1_v = {}
        w2_v = {}

        def load_w1(e_, mg, buf, wait_ts=None):
            pl.w("g", "ts", wait_ts)
            dst = arb.ap("w1buf", 0, 128, buf * ND * 512, (buf + 1) * ND * 512)
            dst = dst.rearrange("p (dj h) -> p dj h", h=512)
            src = w1[e_, :, mg * 512:(mg + 1) * 512].rearrange("(dj p) h -> p dj h", p=128)
            sem = ("dW1a", "dW1b", "dW1c")[buf]
            w1_v[(e_, mg)] = (sem, gld(dst, src, sem))

        def load_w2(e_, kg, buf, wait_ts=None):
            pl.w("g", "ts", wait_ts)
            dst = arb.ap("w2buf", 0, 128, buf * 2048, (buf + 1) * 2048)
            dst = dst.rearrange("p (kj d) -> p kj d", d=1024)
            srcw = w2[e_, kg * 256:(kg + 1) * 256, :].rearrange("(kj p) d -> p kj d",
                                                                p=128)
            sem = ("dW2a", "dW2b", "dW2c")[buf]
            w2_v[(e_, kg)] = (sem, gld(dst, srcw, sem))

        arf.alloc("rsrow", 2 * 1024)
        arf.alloc("rs8", 2 * 128)
        exp_last = {}
        expv = {}
        evac_par = {}
        first_exp = [True]

        def emit_S2(slot, h, qb, kt):
            bp = 64 * slot
            pt = P1 if slot else P0
            if slot in exp_last:
                pl.w("t", "ss", exp_last[slot])
            for nch in range(2):
                lhs = arb.ap("qkT", bp, bp + 64, (8 + h // 2) * T + kt * 128,
                             (8 + h // 2) * T + kt * 128 + 128)
                rhs = arb.ap("qkT", bp, bp + 64,
                             (h // 2) * T + qb * 1024 + nch * 512,
                             (h // 2) * T + qb * 1024 + nch * 512 + 512)
                MM(pt[:, nch * 512:nch * 512 + 512], lhs, rhs, True, True,
                   inc="ts" if nch == 1 else None)
            return pl.cur("ts")

        def emit_exp2(slot, smm, kt):
            pt = P1 if slot else P0
            ebuf = 3 * slot + (kt % 3)
            edst = arb.ap("exp", 0, 128, ebuf * 1024, ebuf * 1024 + 1024)
            pl.w("s", "ts", smm)
            if first_exp[0]:
                pl.w("s", "vs", v_negc)
                first_exp[0] = False
            exp_last[slot] = pl.op(
                "s", lambda e, d=edst, p=pt[:, 0:1024]:
                e.activation(d, p, AF.Exp, bias=s_negc[:, 0:1], scale=0.125),
                inc="ss")
            expv[(slot, kt)] = exp_last[slot]

        def emit_PV2(slot, h, par, kt, flush):
            if flush:
                pl.w("t", "ss", expv[(slot, kt)])
            if kt == 0:
                pl.w("t", "vs", evac_par.get(par, v_vdone))
            ebuf = 3 * slot + (kt % 3)
            lhsv = arb.ap("v", 0, 128, kt * H * 65 + h * 65,
                          kt * H * 65 + h * 65 + 65)
            for nch in range(2):
                rhs = arb.ap("exp", 0, 128, ebuf * 1024 + nch * 512,
                             ebuf * 1024 + nch * 512 + 512)
                out = (P2[0:65, nch * 512:nch * 512 + 512] if par == 0
                       else (P3 if nch == 0 else P4)[0:65, 0:512])
                MM(out, lhsv, rhs, kt == 0, kt == NT - 1,
                   inc="ts" if (kt == NT - 1 and nch == 1) else None)
            return pl.cur("ts") if kt == NT - 1 else None

        def emit_tails_pair(A, B):
            chains = []
            for idx, (h, qb, par, pvd) in enumerate((A, B)):
                chains.append((h, qb, par, pvd, "y" if par == 0 else "g",
                               "dBCa" if par == 0 else "dBCb"))
            vrs, dmae, vrc, vbcv = {}, {}, {}, {}
            for (h, qb, par, pvd, eng, sem) in sorted(chains, key=lambda c: -c[2]):
                rsr = arf.ap("rsrow", 0, 1, par * 1024, par * 1024 + 1024)
                pl.w("v", "ts", pvd)
                pl.vself()
                if par == 0:
                    vrs[par] = pl.op("v", lambda e, d=rsr:
                                     e.tensor_copy(d, P2[64:65, 0:1024]), inc="vs")
                else:
                    rs0 = arf.ap("rsrow", 0, 1, par * 1024, par * 1024 + 512)
                    rs1 = arf.ap("rsrow", 0, 1, par * 1024 + 512, par * 1024 + 1024)
                    pl.op("v", lambda e, d=rs0: e.tensor_copy(d, P3[64:65, 0:512]))
                    vrs[par] = pl.op("v", lambda e, d=rs1:
                                     e.tensor_copy(d, P4[64:65, 0:512]), inc="vs")
            for (h, qb, par, pvd, eng, sem) in chains:
                rsr = arf.ap("rsrow", 0, 1, par * 1024, par * 1024 + 1024)
                r8src = rsr.rearrange("p (j c) -> p j c", j=8)
                rs8 = arf.ap("rs8", 0, 8, par * 128, par * 128 + 128)
                r8dst = rs8.rearrange("p (x c) -> p x c", x=1)
                pl.w(eng, "vs", vrs[par])
                pl.op(eng, lambda e, s=r8src, d=r8dst: e.dma_start(out=d, in_=s),
                      inc=sem, by=16)
                dmae[par] = pl.cur(sem)

            def _rc(e, d, p):
                with nc.allow_low_precision(reason="bf16 softmax normalizer"):
                    return e.reciprocal(d, p)
            for (h, qb, par, pvd, eng, sem) in sorted(chains, key=lambda c: c[2]):
                rs8 = arf.ap("rs8", 0, 8, par * 128, par * 128 + 128)
                rc8 = arb.ap("recip", 0, 8, par * 128, par * 128 + 128)
                pl.w("v", sem, dmae[par])
                pl.vself()
                vrc[par] = pl.op("v", lambda e, d=rc8, s=rs8, f=_rc: f(e, d, s),
                                 inc="vs")
            for (h, qb, par, pvd, eng, sem) in chains:
                rc8 = arb.ap("recip", 0, 8, par * 128, par * 128 + 128)
                pl.w(eng, "vs", vrc[par])
                r8dst = bass.AP(rbd, par * 1024, [[128, 8], [1, 128]])
                pl.op(eng, lambda e, s=rc8, d=r8dst: e.dma_start(out=d, in_=s),
                      inc=sem, by=16)
                pl.w(eng, sem, pl.cur(sem))
                bsrc = bass.AP(rbd, par * 1024, [[0, 64], [1, 1024]])
                rbcd = arb.ap("rbc", 0, 64, par * 1024, par * 1024 + 1024)
                pl.op(eng, lambda e, s=bsrc, d=rbcd: e.dma_start(out=d, in_=s),
                      inc=sem, by=16)
                vbcv[par] = pl.cur(sem)
            for (h, qb, par, pvd, eng, sem) in sorted(chains, key=lambda c: c[2]):
                bp = (h % 2) * 64
                rbcd = arb.ap("rbc", 0, 64, par * 1024, par * 1024 + 1024)
                pl.w("v", sem, vbcv[par])
                adst = arb.ap("attnT", bp, bp + 64, (h // 2) * T + qb * 1024,
                              (h // 2) * T + qb * 1024 + 1024)
                if par == 0:
                    evac_par[par] = pl.op(
                        "v", lambda e, d=adst, r=rbcd: e.tensor_mul(
                            d, P2[0:64, 0:1024], r), inc="vs")
                else:
                    rb0 = arb.ap("rbc", 0, 64, par * 1024, par * 1024 + 512)
                    rb1 = arb.ap("rbc", 0, 64, par * 1024 + 512, par * 1024 + 1024)
                    ad0 = arb.ap("attnT", bp, bp + 64, (h // 2) * T + qb * 1024,
                                 (h // 2) * T + qb * 1024 + 512)
                    ad1 = arb.ap("attnT", bp, bp + 64,
                                 (h // 2) * T + qb * 1024 + 512,
                                 (h // 2) * T + qb * 1024 + 1024)
                    pl.op("v", lambda e, d=ad0, r=rb0: e.tensor_mul(
                        d, P3[0:64, 0:512], r))
                    evac_par[par] = pl.op(
                        "v", lambda e, d=ad1, r=rb1: e.tensor_mul(
                            d, P4[0:64, 0:512], r), inc="vs")

        WMSA_EARLY = 8
        arb.alloc("wmsabA", WMSA_EARLY * 1024)

        def wmsab_ap(dj, a, b):
            if dj < WMSA_EARLY:
                return arb.ap("wmsabA", 0, 128, dj * 1024 + a, dj * 1024 + b)
            d2 = dj - WMSA_EARLY
            return arb.ap("wmsabB", 0, 128, d2 * 1024 + a, d2 * 1024 + b)
        wmsa_v = []
        pl.w("g", "ts", pl.cur("ts"))
        pl.w("g", "vs", pl.cur("vs"))
        pl.w("g", "ss", pl.cur("ss"))
        for dj in range(WMSA_EARLY):
            wmsa_v.append(gld(wmsab_ap(dj, 0, 1024),
                              wmsa[dj * 128:(dj + 1) * 128, :], "dMSA"))

        for jp in range(ND):
            for sub in range(2):
                hA, qbA = 2 * jp, sub
                hB, qbB = 2 * jp + 1, 1 - sub
                parA, parB = qbA, qbB
                pvo = [(0, hA, parA), (1, hB, parB)]
                pvo.sort(key=lambda c: c[2])
                for kt in range(NT):
                    if kt >= 3:
                        for (sl, hh, pp) in pvo:
                            emit_PV2(sl, hh, pp, kt - 3, False)
                    if kt == NT - 1:
                        for (sl, hh, pp) in pvo:
                            emit_PV2(sl, hh, pp, NT - 3, False)
                    sA = emit_S2(0, hA, qbA, kt)
                    sB = emit_S2(1, hB, qbB, kt)
                    emit_exp2(0, sA, kt)
                    emit_exp2(1, sB, kt)
                for (sl, hh, pp) in pvo:
                    emit_PV2(sl, hh, pp, NT - 2, False)
                pvd = {}
                for (sl, hh, pp) in pvo:
                    pvd[sl] = emit_PV2(sl, hh, pp, NT - 1, True)
                pvdA, pvdB = pvd[0], pvd[1]
                emit_tails_pair((hA, qbA, parA, pvdA), (hB, qbB, parB, pvdB))
        v_attn_all = pl.cur("vs")
        ts_attn_all = pl.cur("ts")
        arb.release("qkT")
        arb.release("v")
        arb.release("exp")
        arb.release("rbc")
        arb.release("recip")

        arb.alloc("xlnb", NT * 1024)
        arb.alloc("w1buf", 3 * ND * 512)
        arb.alloc("w2buf", 3 * 2048)
        if ND > WMSA_EARLY:
            arb.alloc("wmsabB", (ND - WMSA_EARLY) * 1024)
        arf.release("rsrow")
        arf.release("rs8")
        arf.alloc("x0st", 2 * 1024)
        arf.alloc("stgA", 2 * 1024)
        arf.alloc("stgB", 1024)
        arf.alloc("lgT", 2048)

        pl.w("g", "vs", v_attn_all)
        pl.w("g", "ts", ts_attn_all)
        for dj in range(WMSA_EARLY, ND):
            wmsa_v.append(gld(wmsab_ap(dj, 0, 1024),
                              wmsa[dj * 128:(dj + 1) * 128, :], "dMSA"))
        load_w1(0, 0, 0)
        load_w1(0, 1, 1)
        load_w1(0, 2, 2)
        load_w2(0, 0, 0)
        load_w2(0, 1, 1)
        load_w2(0, 2, 2)

        x0_v = {}
        pl.w("y", "ts", ts_x0t)
        pl.w("y", "dBCa", pl.cur("dBCa"))
        pl.w("y", "dBCb", pl.cur("dBCb"))
        for tt in range(2):
            x0_v[tt] = yld(arf.ap("x0st", 0, 128, (tt % 2) * 1024, (tt % 2) * 1024 + 1024),
                           x0[tt * 128:(tt + 1) * 128, :],
                           "dx0a" if tt % 2 == 0 else "dx0b")

        ln1_add = {}
        ln1_done = {}
        xln_store = {}
        pl.w("t", "dMSA", wmsa_v[-1])
        pl.w("t", "vs", v_attn_all)
        for tt in range(NT):
            pt = P0 if tt % 2 == 0 else P1
            if tt >= 2:
                pl.w("t", "vs", ln1_add[tt - 2])
            for nch in range(2):
                for dj in range(ND):
                    lhs = arb.ap("attnT", 0, 128, dj * T + tt * 128, dj * T + tt * 128 + 128)
                    rhs = wmsab_ap(dj, nch * 512, nch * 512 + 512)
                    MM(pt[:, nch * 512:nch * 512 + 512], lhs, rhs, dj == 0, False)
                MM(pt[:, nch * 512:nch * 512 + 512], s_onesb[0:1, :],
                   s_bmsar[0:1, nch * 512:nch * 512 + 512], False, True,
                   inc="ts" if nch == 1 else None)
            msa_mm = pl.cur("ts")

            stg = arf.ap("stgA", 0, 128, (tt % 2) * 1024, (tt % 2) * 1024 + 1024)
            stgb = arf.ap("stgB", 0, 128, 0, 1024)
            x0s = arf.ap("x0st", 0, 128, (tt % 2) * 1024, (tt % 2) * 1024 + 1024)
            pl.w("v", "ts", msa_mm)
            pl.w("v", "dx0a" if tt % 2 == 0 else "dx0b", x0_v[tt])
            if tt >= 2:
                pl.w("v", "dxsa" if tt % 2 == 0 else "dxsb", xln_store[tt - 2])
            pl.vself()
            ln1_add[tt] = pl.op("v", lambda e, d=stgb, p=pt[:, 0:1024], x=x0s:
                                e.tensor_add(d, p, x), inc="vs")
            if tt + 2 < NT:
                pl.w("y", "vs", ln1_add[tt])
                x0_v[tt + 2] = yld(
                    arf.ap("x0st", 0, 128, (tt % 2) * 1024, (tt % 2) * 1024 + 1024),
                    x0[(tt + 2) * 128:(tt + 3) * 128, :],
                    "dx0a" if tt % 2 == 0 else "dx0b")
            pl.w("s", "vs", ln1_add[tt])
            s_sq2 = pl.op("s", lambda e, o=stg, y=stgb: e.activation(o, y, AF.Square),
                          inc="ss")
            pl.vself()
            pl.op("v", lambda e, y=stgb: e.reduce_sum(s_stat[:, 0:1], y, axis=AX))
            pl.w("v", "ss", s_sq2)
            pl.op("v", lambda e, y=stg: e.reduce_sum(s_stat[:, 4:5], y, axis=AX),
                  inc="vs")
            pl.vself()
            pl.op("v", lambda e: e.tensor_scalar_mul(s_stat[:, 1:2], s_stat[:, 0:1],
                                                     1.0 / D))
            pl.op("v", lambda e: e.tensor_scalar_mul(s_stat[:, 3:4], s_stat[:, 4:5],
                                                     1.0 / D), inc="vs")
            pl.vself()
            pl.op("v", lambda e: e.tensor_mul(s_stat[:, 2:3], s_stat[:, 1:2],
                                              s_stat[:, 1:2]), inc="vs")
            pl.vself()
            pl.op("v", lambda e: e.tensor_sub(s_stat[:, 4:5], s_stat[:, 3:4],
                                              s_stat[:, 2:3]), inc="vs")
            v_var = pl.cur("vs")
            pl.w("s", "vs", v_var)
            s_std = pl.op("s", lambda e: e.activation(s_stat[:, 5:6], s_stat[:, 4:5],
                                                      AF.Sqrt, bias=s_eps[:, 0:1],
                                                      scale=1.0), inc="ss")
            pl.w("v", "ss", s_std)
            pl.op("v", lambda e: e.reciprocal(s_stat[:, 6:7], s_stat[:, 5:6]), inc="vs")
            pl.vself()
            pl.op("v", lambda e, y=stgb, o=stg: e.tensor_scalar(
                o, y, s_stat[:, 1:2], s_stat[:, 6:7], ALU.subtract, ALU.mult),
                inc="vs")
            v_xln = pl.cur("vs")
            pl.vself()
            xlb = arb.ap("xlnb", 0, 128, tt * 1024, (tt + 1) * 1024)
            ln1_done[tt] = pl.op("v", lambda e, d=xlb, s=stg: e.tensor_copy(d, s),
                                 inc="vs")
            pl.w("y", "vs", v_xln)
            xln_store[tt] = pl.op(
                "y", lambda e, s=stg, tt=tt: e.dma_start(
                    out=xlnd.ap()[tt * 128:(tt + 1) * 128, :], in_=s),
                inc="dxsa" if tt % 2 == 0 else "dxsb", by=16)

        ts_ln_all = pl.cur("ts")

        arb.alloc("xlnT", 2 * T)
        P2bf = P2.bitcast(BF16)
        lgchunks = [(P0, 0), (P0, 512), (P1, 0), (P1, 512)]
        xt_evac = {}
        lgmm_done = {}
        for dj in range(ND):
            xslot = dj % 2
            for tt in range(NT):
                k2 = dj * NT + tt
                psl = P2bf.ap()[0:128, (k2 % 2) * 1024:(k2 % 2) * 1024 + 128]
                if dj == 0:
                    pl.w("t", "vs", ln1_done[tt])
                    if tt >= NT - 2:
                        pl.w("t", "vs", ln1_add[tt])
                if k2 >= 2:
                    pl.w("t", "ss", xt_evac[k2 - 2])
                if tt == 0 and dj >= 2:
                    pl.w("s", "ts", lgmm_done[dj - 2])
                xsrc = arb.ap("xlnb", 0, 128, tt * 1024 + dj * 128,
                              tt * 1024 + dj * 128 + 128)
                tv = pl.op("t", lambda e, o=psl, s=xsrc:
                           e.transpose(o, s, s_idb[:, :]), inc="ts")
                pl.w("s", "ts", tv)
                xd = arb.ap("xlnT", 0, 128, xslot * T + tt * 128,
                            xslot * T + tt * 128 + 128)
                xt_evac[k2] = pl.op("s", lambda e, d=xd, p=psl: e.copy(d, p),
                                    inc="ss")
            pl.w("t", "ss", xt_evac[dj * NT + NT - 1])
            for half in range(2):
                lhs = s_gateTb[:, half * 64 + dj * 8:half * 64 + dj * 8 + 8]
                for ci, (lpt, lpc) in enumerate(lgchunks):
                    rhs = arb.ap("xlnT", 0, 128, xslot * T + ci * 512,
                                 xslot * T + ci * 512 + 512)
                    MM(lpt[0:8, lpc:lpc + 512], lhs, rhs,
                       dj == 0 and half == 0, dj == ND - 1 and half == 1,
                       inc="ts" if ci == 3 else None)
            lgmm_done[dj] = pl.cur("ts")
        arb.release("xlnT")

        pl.w("s", "ts", lgmm_done[ND - 1])
        for ci, (lpt, lpc) in enumerate(lgchunks):
            lgd = arf.ap("lgT", 0, 8, ci * 512, (ci + 1) * 512)
            pl.op("s", lambda e, d=lgd, p=lpt[0:8, lpc:lpc + 512]: e.copy(d, p),
                  inc="ss" if ci == 3 else None)
        s_lgev = pl.cur("ss")
        tm_evac = {}
        for tt in range(NT):
            slot = P2[0:128, 256:264] if tt % 2 == 0 else P2[0:128, 768:776]
            if tt == 0:
                pl.w("t", "ss", s_lgev)
            if tt >= 2:
                pl.w("t", "ss", tm_evac[tt - 2])
            lgs = arf.ap("lgT", 0, 8, tt * 128, (tt + 1) * 128)
            tv = pl.op("t", lambda e, o=slot, s=lgs:
                       e.transpose(o, s, s_id[0:8, 0:8]), inc="ts")
            pl.w("s", "ts", tv)
            tm_evac[tt] = pl.op("s", lambda e, p=slot, tt=tt:
                                e.copy(s_lgtall[:, tt * 8:(tt + 1) * 8], p), inc="ss")

        pl.w("v", "ss", tm_evac[NT - 1])
        pl.vself()
        for tt in range(NT):
            pl.op("v", lambda e, tt=tt: e.reduce_max(
                s_gs[:, tt:tt + 1], s_lgtall[:, tt * 8:(tt + 1) * 8], axis=AX),
                inc="vs" if tt == NT - 1 else None)
        pl.vself()
        for tt in range(NT):
            pl.op("v", lambda e, tt=tt: e.tensor_scalar(
                s_ls[:, tt * 8:(tt + 1) * 8], s_lgtall[:, tt * 8:(tt + 1) * 8],
                s_gs[:, tt:tt + 1], None, ALU.subtract),
                inc="vs" if tt == NT - 1 else None)
        v_lsall = pl.cur("vs")
        pl.w("s", "vs", v_lsall)
        pl.sself()
        s_exw = pl.op("s", lambda e: e.activation(s_les[:, :], s_ls[:, :], AF.Exp),
                      inc="ss")
        pl.w("v", "ss", s_exw)
        for tt in range(NT):
            pl.op("v", lambda e, tt=tt: e.reduce_sum(
                s_gs[:, 16 + tt:17 + tt], s_les[:, tt * 8:(tt + 1) * 8], axis=AX),
                inc="vs" if tt == NT - 1 else None)
        pl.vself()
        pl.op("v", lambda e: e.reciprocal(s_cols[:, 32:48], s_gs[:, 16:32]), inc="vs")
        pl.op("v", lambda e: e.tensor_scalar(s_M1f[:, :], s_ls[:, :], 0.0, None,
                                             ALU.is_ge), inc="vs")
        pl.vself()
        pl.op("v", lambda e: e.scalar_tensor_tensor(
            s_ls[:, :], s_iota8[:, :], 9.0, s_M1f[:, :], ALU.subtract, ALU.mult),
            inc="vs")
        pl.vself()
        pl.op("v", lambda e: e.tensor_scalar_add(s_les[:, :], s_ls[:, :], 9.0),
              inc="vs")
        pl.vself()
        for tt in range(NT):
            pl.op("v", lambda e, tt=tt: e.tensor_reduce(
                s_cols[:, 16 + tt:17 + tt], s_les[:, tt * 8:(tt + 1) * 8],
                axis=AX, op=ALU.min), inc="vs" if tt == NT - 1 else None)
        pl.vself()
        for tt in range(NT):
            pl.op("v", lambda e, tt=tt: e.tensor_scalar(
                s_M1f[:, tt * 8:(tt + 1) * 8], s_iota8[:, tt * 8:(tt + 1) * 8],
                s_cols[:, 16 + tt:17 + tt], None, ALU.is_equal),
                inc="vs" if tt == NT - 1 else None)
        pl.vself()
        v_m1all = pl.op("v", lambda e: e.tensor_copy(s_M1[:, :], s_M1f[:, :]),
                        inc="vs")

        rp = P2[:, 768:776]
        cp = P2[0:1, 776:784]
        toff_v = {-1: v_m1all}
        pl.w("t", "vs", v_m1all)
        for tt in range(NT):
            pl.w("t", "vs", toff_v[tt - 1])
            pl.op("t", lambda e, tt=tt: e.matmul(rp, s_ltriuf[:, :],
                                                 s_M1f[:, tt * 8:tt * 8 + 8],
                                                 start=True, stop=False))
            pl.op("t", lambda e: e.matmul(rp, s_onesf[0:1, :], s_toff[0:1, :],
                                          start=False, stop=True), inc="ts")
            t_cnt = pl.op("t", lambda e, tt=tt: e.matmul(
                cp, s_onescf[:, 0:1], s_M1f[:, tt * 8:tt * 8 + 8],
                start=True, stop=True), inc="ts")
            pl.w("s", "ts", t_cnt)
            s_rke = pl.op("s", lambda e, tt=tt:
                          e.copy(s_ranks[:, tt * 8:tt * 8 + 8], rp), inc="ss")
            pl.w("v", "ss", s_rke)
            pl.w("v", "ts", t_cnt)
            pl.op("v", lambda e, tt=tt: e.tensor_mul(
                s_ranks[:, tt * 8:tt * 8 + 8], s_ranks[:, tt * 8:tt * 8 + 8],
                s_M1f[:, tt * 8:tt * 8 + 8]), inc="vs")
            pl.vself()
            pl.op("v", lambda e, tt=tt: e.reduce_sum(
                s_cols[:, tt:tt + 1], s_ranks[:, tt * 8:tt * 8 + 8], axis=AX))
            toff_v[tt] = pl.op("v", lambda e: e.tensor_add(
                s_toff[0:1, :], s_toff[0:1, :], cp), inc="vs")
        v_gate_all = pl.cur("vs")

        pl.vself()
        v_cntc = pl.op("v", lambda e: e.tensor_copy(s_cnt[0:1, :], s_toff[0:1, :]),
                       inc="vs")
        pl.w("g", "vs", v_cntc)
        g_in = pl.op("g", lambda e: e.dma_start(out=cc_in.ap(), in_=s_cnt[0:1, :]),
                     inc="dAG", by=16)
        pl.w("g", "dAG", g_in)
        if USE_COLLECTIVE:
            pl.op("g", lambda e: e.collective_compute(
                "AllGather", ALU.bypass, replica_groups=[list(range(B))],
                ins=[cc_in.ap()], outs=[cc_out.ap()]), inc="cs", by=1)
        else:
            for r in range(E):
                pl.op("g", lambda e, r=r: e.dma_start(out=cc_out.ap()[r:r + 1, :],
                                                      in_=cc_in.ap()),
                      inc="cs", by=16 if r == E - 1 else 16)
        pl.w("g", "cs", pl.cur("cs"))
        g_ag = pl.op("g", lambda e: e.dma_start(out=s_ag[:, :], in_=cc_out.ap()),
                     inc="dAG", by=16)

        ts_msa_all = pl.cur("ts")
        arb.release("attnT")
        arb.release("wmsabA")
        if ND > WMSA_EARLY:
            arb.release("wmsabB")
        arf.release("x0st")
        arf.release("lgT")
        arf.alloc("b2r", 2 * 1024)

        arb.alloc("EO", 24 * 1024)
        arb.alloc("Ht", 32 * CAP)
        arb.alloc("G", NT * CAP)
        arb.alloc("D0", ND * CAP)
        arb.alloc("D1", ND * CAP)

        dslots = [(P0, 0), (P0, 512), (P1, 0), (P1, 512), (P2, 0), (P2, 512),
                  (P3, 0), (P4, 0)]
        g_built = {}
        d_evac = {}
        h_evac = {}
        eo_evac = {}
        disp_last = {}
        v_b2 = None

        def emit_G(ee):
            for tt in range(NT):
                if tt == 0:
                    if ee == 0:
                        pl.w("v", "vs", v_gate_all)
                        pl.w("v", "ts", ts_msa_all)
                    else:
                        pl.w("v", "ts", disp_last[ee - 1])
                    pl.vself()
                gdst = arb.ap("G", 0, 128, tt * CAP, (tt + 1) * CAP)
                g_built[(ee, tt)] = pl.op(
                    "v", lambda e, d=gdst, tt=tt, ee=ee: e.tensor_scalar(
                        d, s_iotaC[:, :], s_cols[:, tt:tt + 1],
                        s_M1f[:, tt * 8 + ee:tt * 8 + ee + 1], ALU.is_equal, ALU.mult),
                    inc="vs")

        emit_G(0)
        for e_ in range(E):
            dbuf = "D0" if e_ % 2 == 0 else "D1"
            for dj in range(ND):
                pt, pc = dslots[dj]
                if e_ >= 1:
                    pl.w("t", "vs", d_evac[(e_ - 1, dj)])
                    if dj == 0:
                        pl.w("t", "vs", eo_evac[(e_ - 1, 2)])
                for tt in range(NT):
                    if dj == 0:
                        pl.w("t", "vs", g_built[(e_, tt)])
                        if e_ == 0 and tt == 0:
                            pl.w("t", "vs", ln1_done[NT - 1])
                    lhs = arb.ap("xlnb", 0, 128, tt * 1024 + dj * 128,
                                 tt * 1024 + dj * 128 + 128)
                    rhs = arb.ap("G", 0, 128, tt * CAP, (tt + 1) * CAP)
                    MM(pt[:, pc:pc + CAP], lhs, rhs, tt == 0, tt == NT - 1,
                       inc="ts" if tt == NT - 1 else None)
                dmm = pl.cur("ts")
                pl.w("v", "ts", dmm)
                ddst = arb.ap(dbuf, 0, 128, dj * CAP, (dj + 1) * CAP)
                d_evac[(e_, dj)] = pl.op(
                    "v", lambda e, d=ddst, p=pt[:, pc:pc + CAP]: e.tensor_copy(d, p),
                    inc="vs")
            disp_last[e_] = pl.cur("ts")

            hsets = [[(P0, 0), (P0, 512), (P1, 0), (P1, 512)],
                     [(P2, 0), (P2, 512), (P3, 0), (P4, 0)]]
            for mg in range(8):
                sset = hsets[mg % 2]
                gmg = e_ * 8 + mg
                pl.w("t", w1_v[(e_, mg)][0], w1_v[(e_, mg)][1])
                if mg >= 2:
                    for mi in range(4):
                        pl.w("t", "ss", h_evac[(e_, mg - 2, mi)])
                for mi in range(4):
                    pt, pc = sset[mi]
                    for dj in range(ND):
                        if mg == 0 and mi == 0:
                            pl.w("t", "vs", d_evac[(e_, dj)])
                        lhs = arb.ap("w1buf", 0, 128,
                                     (gmg % 3) * ND * 512 + dj * 512 + mi * 128,
                                     (gmg % 3) * ND * 512 + dj * 512 + mi * 128 + 128)
                        rhs = arb.ap(dbuf, 0, 128, dj * CAP, (dj + 1) * CAP)
                        MM(pt[:, pc:pc + CAP], lhs, rhs, dj == 0, dj == ND - 1,
                           inc="ts" if dj == ND - 1 else None)
                    hmm = pl.cur("ts")
                    m_t = mg * 4 + mi
                    hdst = arb.ap("Ht", 0, 128, m_t * CAP, (m_t + 1) * CAP)
                    pl.w("s", "ts", hmm)
                    h_evac[(e_, mg, mi)] = pl.op(
                        "s", lambda e, d=hdst, p=pt[:, pc:pc + CAP],
                        b=s_b1c[:, e_ * 32 + m_t:e_ * 32 + m_t + 1]:
                        e.activation(d, p, AF.Relu, bias=b, scale=1.0), inc="ss")
                nxt = gmg + 3
                if nxt < E * 8:
                    load_w1(nxt // 8, nxt % 8, gmg % 3, wait_ts=pl.cur("ts"))

            if e_ == 0:
                b2_v = {}
                for ee in range(2):
                    b2_v[ee] = yld(arf.ap("b2r", 0, 1, (ee % 2) * 1024,
                                          (ee % 2) * 1024 + 1024),
                                   b2[ee:ee + 1, :], "dB2" if ee % 2 == 0 else "dB2b")
            par = e_ % 2
            pl.w("t", "dB2" if par == 0 else "dB2b", b2_v[e_])
            pl.w("t", "ss", h_evac[(e_, 7, 3)])
            if e_ >= 1:
                pl.w("t", "vs", v_b2)
                pl.w("t", "vs", eo_evac[(e_ - 1, 2)])
            MM(P3[0:128, 0:512], s_onesf[0:1, :],
               arf.ap("b2r", 0, 1, par * 1024, par * 1024 + 512), True, True)
            t_b2 = MM(P4[0:128, 0:512], s_onesf[0:1, :],
                      arf.ap("b2r", 0, 1, par * 1024 + 512, par * 1024 + 1024),
                      True, True, inc="ts")
            if e_ + 2 < E:
                pl.w("y", "ts", t_b2)
                b2_v[e_ + 2] = yld(arf.ap("b2r", 0, 1, par * 1024, par * 1024 + 1024),
                                   b2[e_ + 2:e_ + 3, :],
                                   "dB2" if par == 0 else "dB2b")
            pl.w("v", "ts", t_b2)
            pl.op("v", lambda e: e.tensor_copy(s_b2bc[:, 0:512], P3[0:128, 0:512]))
            v_b2 = pl.op("v", lambda e: e.tensor_copy(s_b2bc[:, 512:1024],
                                                      P4[0:128, 0:512]), inc="vs")

            if e_ + 1 < E:
                emit_G(e_ + 1)

            eopt = [(P0, 128), (P1, 128), (P2, 64)]
            pl.w("t", "ss", h_evac[(e_, 7, 3)])
            if e_ >= 1:
                for ci in range(3):
                    pl.w("t", "vs", eo_evac[(e_ - 1, ci)])
            for kg in range(16):
                gkg = e_ * 16 + kg
                pl.w("t", w2_v[(e_, kg)][0], w2_v[(e_, kg)][1])
                for kj in range(2):
                    kt = kg * 2 + kj
                    for ci in range(3):
                        pt, crows = eopt[ci]
                        lhs = arb.ap("Ht", 0, 128, kt * CAP + ci * 128,
                                     kt * CAP + ci * 128 + crows)
                        for nch in range(2):
                            rhs = arb.ap("w2buf", 0, 128,
                                         (gkg % 3) * 2048 + kj * 1024 + nch * 512,
                                         (gkg % 3) * 2048 + kj * 1024 + nch * 512 + 512)
                            MM(pt[0:crows, nch * 512:nch * 512 + 512], lhs, rhs,
                               kt == 0, kt == 31,
                               inc="ts" if (kj == 1 and ci == 2 and nch == 1) else None)
                nxt = gkg + 3
                if nxt < E * 16:
                    load_w2(nxt // 16, nxt % 16, gkg % 3, wait_ts=pl.cur("ts"))
            eomm = pl.cur("ts")
            for ci in range(3):
                pt, crows = eopt[ci]
                pl.w("v", "ts", eomm)
                pl.w("v", "vs", v_b2)
                edst = arb.ap("EO", 0, crows, (e_ * 3 + ci) * 1024,
                              (e_ * 3 + ci) * 1024 + 1024)
                eo_evac[(e_, ci)] = pl.op(
                    "v", lambda e, d=edst, p=pt[0:crows, 0:1024], cr=crows:
                    e.tensor_add(d, p, s_b2bc[0:cr, 0:1024]), inc="vs")

        ts_moe_done = pl.cur("ts")
        arb.release("G")
        arb.release("D0")
        arb.release("D1")
        arb.release("Ht")
        arb.release("w1buf")
        arb.release("w2buf")
        arb.release("xlnb")
        arb.alloc("pebc", E * T)
        arb.alloc("CW", 3 * 24 * 128)
        arb.alloc("gsall", 128)
        arb.alloc("petr", 128)
        arf.release("b2r")
        arf.alloc("rrow", 2048)
        arf.alloc("rbcf", 2 * 1024)

        pl.w("t", "dAG", g_ag)
        pl.w("t", "vs", eo_evac[(E - 1, 2)])
        t_om = MM(P3[0:1, 0:8], s_prefix[:, :], s_ag[:, :], True, True, inc="ts")
        pl.w("v", "ts", t_om)
        pl.vself()
        v_off = pl.op("v", lambda e: e.tensor_copy(s_off[0:1, :], P3[0:1, 0:8]), inc="vs")
        pl.w("t", "vs", v_off)
        t_ob = MM(P3[0:128, 8:16], s_onesf[0:1, :], s_off[0:1, :], True, True, inc="ts")
        pl.w("v", "ts", t_ob)
        v_offbc = pl.op("v", lambda e: e.tensor_copy(s_offbc[:, 0:8], P3[0:128, 8:16]),
                        inc="vs")

        pl.vself()
        for tt in range(NT):
            pl.op("v", lambda e, tt=tt: e.tensor_mul(
                s_les[:, tt * 8:(tt + 1) * 8], s_offbc[:, 0:8],
                s_M1f[:, tt * 8:(tt + 1) * 8]), inc="vs" if tt == NT - 1 else None)
        pl.vself()
        for tt in range(NT):
            pl.op("v", lambda e, tt=tt: e.reduce_sum(
                s_gs[:, 32 + tt:33 + tt], s_les[:, tt * 8:(tt + 1) * 8], axis=AX),
                inc="vs" if tt == NT - 1 else None)
        pl.vself()
        pl.op("v", lambda e: e.tensor_add(s_ls[:, 0:16], s_cols[:, 0:16],
                                          s_gs[:, 32:48]), inc="vs")
        pl.vself()
        pl.op("v", lambda e: e.tensor_scalar(s_ls[:, 16:32], s_ls[:, 0:16],
                                             float(GCAP), None, ALU.is_lt), inc="vs")
        pl.vself()
        pl.op("v", lambda e: e.tensor_mul(s_cols[:, 48:64], s_cols[:, 32:48],
                                          s_ls[:, 16:32]), inc="vs")
        v_pk_all = pl.cur("vs")

        tr_evac = {}
        for tt in range(NT):
            psl = P2[0:1, (tt % 2) * 512:(tt % 2) * 512 + 128]
            if tt >= 2:
                pl.w("t", "ss", tr_evac[tt - 2])
            pl.w("t", "vs", v_pk_all if tt == 0 else None)
            tv = pl.op("t", lambda e, o=psl, tt=tt: e.transpose(
                o, s_cols[:, tt:tt + 1], s_id[:, :]), inc="ts")
            pl.w("s", "ts", tv)
            tr_evac[tt] = pl.op("s", lambda e, p=psl,
                                d=arf.ap("rrow", 0, 1, tt * 128, (tt + 1) * 128):
                                e.copy(d, p), inc="ss")
        pl.w("t", "ss", tr_evac[NT - 1])
        pl.w("t", "ss", tr_evac[NT - 2])
        rb_slots = [(P2, 0), (P2, 512), (P3, 0), (P4, 0)]
        for ci, (pt, pc) in enumerate(rb_slots):
            MM(pt[0:128, pc:pc + 512], s_onesf[0:1, :],
               arf.ap("rrow", 0, 1, ci * 512, (ci + 1) * 512), True, True,
               inc="ts" if ci == 3 else None)
        t_rbc = pl.cur("ts")
        pl.w("v", "ts", t_rbc)
        for ci, (pt, pc) in enumerate(rb_slots):
            pl.op("v", lambda e, p=pt[0:128, pc:pc + 512], ci=ci: e.tensor_copy(
                arf.ap("rbcf", 0, 128, ci * 512, (ci + 1) * 512), p),
                inc="vs" if ci == 3 else None)
        v_rbcf = pl.cur("vs")

        for ee in range(E):
            for tt in range(NT):
                col = ee * 16 + tt
                pl.op("v", lambda e, col=col, tt=tt, ee=ee: e.tensor_mul(
                    arb.ap("gsall", 0, 128, col, col + 1),
                    s_cols[:, 48 + tt:49 + tt],
                    s_M1f[:, tt * 8 + ee:tt * 8 + ee + 1]),
                    inc="vs" if col == 127 else None)
        v_gs = pl.cur("vs")
        pl.w("t", "vs", v_gs)
        t_petr = pl.op("t", lambda e: e.transpose(
            P2bf.ap()[0:128, 0:128], arb.ap("gsall", 0, 128, 0, 128),
            s_idb[:, :]), inc="ts")
        pl.w("s", "ts", t_petr)
        s_petr = pl.op("s", lambda e: e.copy(
            arb.ap("petr", 0, 128, 0, 128), P2bf.ap()[0:128, 0:128]), inc="ss")
        pl.w("y", "ss", s_petr)
        pdout = yld(ped.ap(), arb.ap("petr", 0, 128, 0, 128), "dBCa")
        pl.w("y", "dBCa", pdout)
        pl.w("g", "dBCa", pdout)
        pl.w("g", "ts", pl.cur("ts"))
        for ee in range(E):
            eng = "y" if ee % 2 == 0 else "g"
            sem = "dBCa" if ee % 2 == 0 else "dBCb"
            bsrc = bass.AP(ped, ee * 2048, [[0, 128], [1, 2048]])
            bdst = arb.ap("pebc", 0, 128, ee * T, ee * T + 2048)
            pl.op(eng, lambda e, s=bsrc, d=bdst: e.dma_start(out=d, in_=s),
                  inc=sem, by=16)
        pbca = pl.cur("dBCa")
        pbcb = pl.cur("dBCb")

        arf.release("rrow")
        arf.alloc("xlnld", 2 * 1024)
        xln_v = {}
        pl.w("y", "ts", t_rbc)
        for tt in range(2):
            pl.w("y", "dxsa" if tt % 2 == 0 else "dxsb", xln_store[tt])
            xln_v[tt] = yld(arf.ap("xlnld", 0, 128, (tt % 2) * 1024, (tt % 2) * 1024 + 1024),
                            xlnd.ap()[tt * 128:(tt + 1) * 128, :],
                            "dxla" if tt % 2 == 0 else "dxlb")

        cw_built = {}
        cw_read = {}
        ln2_add = {}
        out_v = {}

        def emit_cw_and_mm(tt):
            if tt >= 3:
                pl.w("v", "ts", cw_read[tt - 3])
            pl.vself()
            if tt == 0:
                pl.w("v", "dBCa", pbca)
                pl.w("v", "dBCb", pbcb)
            slot = (tt % 3) * 24 * 128
            for e_ in range(E):
                for ci in range(3):
                    idx = e_ * 3 + ci
                    crows = 128 if ci < 2 else 64
                    cdst = arb.ap("CW", 0, crows, slot + idx * 128,
                                  slot + idx * 128 + 128)
                    pin = arb.ap("pebc", 0, crows, e_ * T + tt * 128,
                                 e_ * T + (tt + 1) * 128)
                    rin = arf.ap("rbcf", 0, crows, tt * 128, (tt + 1) * 128)
                    pl.op("v", lambda e, d=cdst, r=rin, p=pin, ci=ci, cr=crows:
                          e.scalar_tensor_tensor(d, r, s_iotac3[0:cr, ci:ci + 1],
                                                 p, ALU.is_equal, ALU.mult),
                          inc="vs" if idx == 23 else None)
            cw_built[tt] = pl.cur("vs")

            pt = P0 if tt % 2 == 0 else P1
            pl.w("t", "vs", cw_built[tt])
            if tt >= 2:
                pl.w("t", "vs", ln2_add[tt - 2])
            first = True
            for e_ in range(E):
                for ci in range(3):
                    crows = 128 if ci < 2 else 64
                    lhs = arb.ap("CW", 0, crows, slot + (e_ * 3 + ci) * 128,
                                 slot + (e_ * 3 + ci) * 128 + 128)
                    last = (e_ == E - 1 and ci == 2)
                    for nch in range(2):
                        rhs = arb.ap("EO", 0, crows, (e_ * 3 + ci) * 1024 + nch * 512,
                                     (e_ * 3 + ci) * 1024 + nch * 512 + 512)
                        MM(pt[:, nch * 512:nch * 512 + 512], lhs, rhs, first,
                           last, inc="ts" if (last and nch == 1) else None)
                    first = False
            cw_read[tt] = pl.cur("ts")
            return pl.cur("ts")

        ans_mm = {}

        def emit_epilogue(tt):
            pt = P0 if tt % 2 == 0 else P1
            stg = arf.ap("stgA", 0, 128, (tt % 2) * 1024, (tt % 2) * 1024 + 1024)
            stgb = arf.ap("stgB", 0, 128, 0, 1024)
            xl = arf.ap("xlnld", 0, 128, (tt % 2) * 1024, (tt % 2) * 1024 + 1024)
            pl.w("v", "ts", ans_mm[tt])
            pl.w("v", "dxla" if tt % 2 == 0 else "dxlb", xln_v[tt])
            if tt == 0:
                pl.w("v", "dxsa", xln_store[NT - 2])
                pl.w("v", "dxsb", xln_store[NT - 1])
            if tt >= 2:
                pl.w("v", "dOa" if tt % 2 == 0 else "dOb", 16 * (tt // 2))
            pl.vself()
            ln2_add[tt] = pl.op("v", lambda e, d=stgb, p=pt[:, 0:1024], x=xl:
                                e.tensor_add(d, p, x), inc="vs")
            if tt + 2 < NT:
                pl.w("y", "vs", ln2_add[tt])
                pl.w("y", "dxsa" if tt % 2 == 0 else "dxsb", xln_store[tt + 2])
                xln_v[tt + 2] = yld(
                    arf.ap("xlnld", 0, 128, (tt % 2) * 1024, (tt % 2) * 1024 + 1024),
                    xlnd.ap()[(tt + 2) * 128:(tt + 3) * 128, :],
                    "dxla" if tt % 2 == 0 else "dxlb")
            pl.w("s", "vs", ln2_add[tt])
            s_sq2 = pl.op("s", lambda e, o=stg, y=stgb: e.activation(o, y, AF.Square),
                          inc="ss")
            pl.vself()
            pl.op("v", lambda e, y=stgb: e.reduce_sum(s_stat[:, 0:1], y, axis=AX))
            pl.w("v", "ss", s_sq2)
            pl.op("v", lambda e, y=stg: e.reduce_sum(s_stat[:, 4:5], y, axis=AX),
                  inc="vs")
            pl.vself()
            pl.op("v", lambda e: e.tensor_scalar_mul(s_stat[:, 1:2], s_stat[:, 0:1],
                                                     1.0 / D))
            pl.op("v", lambda e: e.tensor_scalar_mul(s_stat[:, 3:4], s_stat[:, 4:5],
                                                     1.0 / D), inc="vs")
            pl.vself()
            pl.op("v", lambda e: e.tensor_mul(s_stat[:, 2:3], s_stat[:, 1:2],
                                              s_stat[:, 1:2]), inc="vs")
            pl.vself()
            pl.op("v", lambda e: e.tensor_sub(s_stat[:, 4:5], s_stat[:, 3:4],
                                              s_stat[:, 2:3]), inc="vs")
            v_var = pl.cur("vs")
            pl.w("s", "vs", v_var)
            s_std = pl.op("s", lambda e: e.activation(s_stat[:, 5:6], s_stat[:, 4:5],
                                                      AF.Sqrt, bias=s_eps[:, 0:1],
                                                      scale=1.0), inc="ss")
            pl.w("v", "ss", s_std)
            pl.op("v", lambda e: e.reciprocal(s_stat[:, 6:7], s_stat[:, 5:6]), inc="vs")
            pl.vself()
            v_out = pl.op("v", lambda e, y=stgb, o=stg: e.tensor_scalar(
                o, y, s_stat[:, 1:2], s_stat[:, 6:7], ALU.subtract, ALU.mult),
                inc="vs")
            pl.w("y", "vs", v_out)
            out_v[tt] = pl.op("y", lambda e, s=stg, tt=tt: e.dma_start(
                out=out[tt * 128:(tt + 1) * 128, :], in_=s),
                inc="dOa" if tt % 2 == 0 else "dOb", by=16)

        for tt in range(NT):
            ans_mm[tt] = emit_cw_and_mm(tt)
            if tt >= 1:
                emit_epilogue(tt - 1)
        emit_epilogue(NT - 1)

        pl.w("y", "dOa", 16 * (NT // 2))
        pl.w("y", "dOb", 16 * (NT // 2))

        if arb.peak > ARB_N or arf.peak > ARF_N:
            raise MemoryError(f"arena peaks {arb.peak} {arf.peak}")
        blk = EN(nc.Block())

        @blk.tensor
        def _(t):
            for f in pl.ops["t"]:
                f(t)

        @blk.vector
        def _(v):
            for f in pl.ops["v"]:
                f(v)

        @blk.scalar
        def _(s):
            for f in pl.ops["s"]:
                f(s)

        @blk.gpsimd
        def _(g):
            for f in pl.ops["g"]:
                f(g)

        @blk.sync
        def _(y):
            for f in pl.ops["y"]:
                f(y)

    return nc


def make_host_inputs(L0_attn_W_qkv, L0_attn_b_qkv, L0_attn_W_msa, L0_attn_b_msa,
                     L0_ffn_gate_w, L0_ffn_w1, L0_ffn_b1, L0_ffn_w2, L0_ffn_b2):
    W = np.asarray(L0_attn_W_qkv, dtype=np.float32)
    bq = np.asarray(L0_attn_b_qkv, dtype=np.float32)
    wqk = np.concatenate([W[:, h * 192:h * 192 + 64] for h in range(H)]
                         + [W[:, h * 192 + 64:h * 192 + 128] for h in range(H)], axis=1)
    wv = np.concatenate([W[:, h * 192 + 128:h * 192 + 192] for h in range(H)], axis=1)
    bqk = np.concatenate([bq[h * 192:h * 192 + 64] for h in range(H)]
                         + [bq[h * 192 + 64:h * 192 + 128] for h in range(H)])
    bqkc = np.ascontiguousarray(bqk.reshape(16, 128).T)
    bvr = np.concatenate([bq[h * 192 + 128:h * 192 + 192] for h in range(H)])[None, :]
    wmsa = np.asarray(L0_attn_W_msa, dtype=np.float32)
    bmsar = np.asarray(L0_attn_b_msa, dtype=np.float32)[None, :]
    gateT = np.ascontiguousarray(np.asarray(L0_ffn_gate_w, dtype=np.float32).T)
    w1 = np.asarray(L0_ffn_w1, dtype=np.float32)
    b1 = np.asarray(L0_ffn_b1, dtype=np.float32)
    b1c = np.ascontiguousarray(b1.reshape(E, 32, 128).transpose(0, 2, 1))
    w2 = np.asarray(L0_ffn_w2, dtype=np.float32)
    b2 = np.asarray(L0_ffn_b2, dtype=np.float32)
    consts = {
        "c_ident": np.eye(128, dtype=np.float32),
        "c_iotaC": np.broadcast_to(np.arange(CAP, dtype=np.float32)[None, :],
                                   (128, CAP)).copy(),
        "c_iotac3": np.stack([np.arange(128, dtype=np.float32) + 128 * ci
                              for ci in range(3)], axis=1).astype(np.float32),
        "c_ltriu": np.triu(np.ones((128, 128), dtype=np.float32), 1),
        "c_iota8": np.broadcast_to(np.tile(np.arange(8, dtype=np.float32), 16)[None, :],
                                   (128, 128)).copy(),
    }
    shared = {"wqk": np.ascontiguousarray(wqk), "bqkc": bqkc,
              "wv": np.ascontiguousarray(wv),
              "bvr": np.ascontiguousarray(bvr),
              "wmsa": np.ascontiguousarray(wmsa),
              "bmsar": np.ascontiguousarray(bmsar),
              "gateT": gateT, "w1": np.ascontiguousarray(w1), "b1c": b1c,
              "w2": np.ascontiguousarray(w2), "b2": np.ascontiguousarray(b2)}
    shared.update(consts)
    return shared


def kernel(L0_attn_z, L0_attn_W_qkv, L0_attn_b_qkv, L0_attn_W_msa, L0_attn_b_msa,
           L0_ffn_gate_w, L0_ffn_w1, L0_ffn_b1, L0_ffn_w2, L0_ffn_b2):
    z = np.asarray(L0_attn_z, dtype=np.float32)
    shared = make_host_inputs(L0_attn_W_qkv, L0_attn_b_qkv, L0_attn_W_msa,
                              L0_attn_b_msa, L0_ffn_gate_w, L0_ffn_w1, L0_ffn_b1,
                              L0_ffn_w2, L0_ffn_b2)
    nc = build_program()
    in_maps = []
    for c in range(B):
        m = dict(shared)
        m["x0"] = np.ascontiguousarray(z[c])
        pref = np.zeros((E, 1), dtype=np.float32)
        pref[:c, 0] = 1.0
        m["prefix"] = pref
        in_maps.append(m)
    res = run_bass_kernel_spmd(nc, in_maps, core_ids=list(range(B)))
    outs = [np.asarray(res.results[c]["out"], dtype=np.float32) for c in range(B)]
    return np.stack(outs, axis=0)


# version 27
# speedup vs baseline: 1.0162x; 1.0162x over previous
import contextlib
import numpy as np
import ml_dtypes
import concourse.bass as bass
import concourse.mybir as mybir
from concourse.bass_utils import run_bass_kernel_spmd

F32 = mybir.dt.float32
BF16 = mybir.dt.bfloat16
AX = mybir.AxisListType.X
ALU = mybir.AluOpType
AF = mybir.ActivationFunctionType

B = 8
T = 2048
NT = T // 128
D = 1024
ND = D // 128
H = 16
E = 8
DFF = 4096
CAP = 320
GCAP = 2560
EPS = 1e-5

USE_COLLECTIVE = True
DEBUG_DUMPS = False


class Plan:
    def __init__(self):
        self.ops = {k: [] for k in ("t", "v", "s", "g", "y")}
        self.n = {}
        self.sems = {}

    def w(self, eng, sem, val):
        if val is None or val <= 0:
            return
        h = self.sems[sem]
        self.ops[eng].append(lambda e, h=h, v=val: e.wait_ge(h, v))

    def op(self, eng, fn, inc=None, by=1):
        v, h = None, None
        if inc is not None:
            v = self.n[inc] = self.n.get(inc, 0) + by
            h = self.sems[inc]

        def run(e, fn=fn, h=h, by=by):
            inst = fn(e)
            if h is not None:
                inst.then_inc(h, by)
        self.ops[eng].append(run)
        return v

    def cur(self, sem):
        return self.n.get(sem, 0)

    def vself(self):
        self.w("v", "vs", self.cur("vs"))

    def sself(self):
        self.w("s", "ss", self.cur("ss"))


class Arena:
    def __init__(self, tensor, nelem):
        self.tensor = tensor
        self.free = [(0, nelem)]
        self.used = {}
        self.peak = 0
        self.curn = 0

    def alloc(self, name, elems):
        elems = (elems + 127) & ~127
        for i, (off, ln) in enumerate(self.free):
            if ln >= elems:
                self.used[name] = (off, elems)
                if ln == elems:
                    self.free.pop(i)
                else:
                    self.free[i] = (off + elems, ln - elems)
                self.curn += elems
                self.peak = max(self.peak, self.curn)
                return off
        raise MemoryError(f"arena OOM: {name} ({elems}) free={self.free}")

    def release(self, name):
        off, ln = self.used.pop(name)
        self.curn -= ln
        self.free.append((off, ln))
        self.free.sort()
        m = []
        for o, l in self.free:
            if m and m[-1][0] + m[-1][1] == o:
                m[-1] = (m[-1][0], m[-1][1] + l)
            else:
                m.append((o, l))
        self.free = m

    def ap(self, name, p0, p1, a, b):
        off, ln = self.used[name]
        assert 0 <= a <= b <= ln, (name, a, b, ln)
        return self.tensor[p0:p1, off + a: off + b]


def build_program():
    nc = bass.Bass("TRN2", target_bir_lowering=False)

    x0 = nc.dram_tensor("x0", [T, D], F32, kind="ExternalInput").ap()
    wqk = nc.dram_tensor("wqk", [D, 2048], F32, kind="ExternalInput").ap()
    bqkc = nc.dram_tensor("bqkc", [128, 16], F32, kind="ExternalInput").ap()
    wv = nc.dram_tensor("wv", [D, D], F32, kind="ExternalInput").ap()
    bvr = nc.dram_tensor("bvr", [1, D], F32, kind="ExternalInput").ap()
    wmsa = nc.dram_tensor("wmsa", [D, D], F32, kind="ExternalInput").ap()
    bmsar = nc.dram_tensor("bmsar", [1, D], F32, kind="ExternalInput").ap()
    gateT = nc.dram_tensor("gateT", [D, E], F32, kind="ExternalInput").ap()
    w1 = nc.dram_tensor("w1", [E, D, DFF], F32, kind="ExternalInput").ap()
    b1c = nc.dram_tensor("b1c", [E, 128, 32], F32, kind="ExternalInput").ap()
    w2 = nc.dram_tensor("w2", [E, DFF, D], F32, kind="ExternalInput").ap()
    b2 = nc.dram_tensor("b2", [E, D], F32, kind="ExternalInput").ap()
    prefix = nc.dram_tensor("prefix", [E, 1], F32, kind="ExternalInput").ap()
    c_ident = nc.dram_tensor("c_ident", [128, 128], F32, kind="ExternalInput").ap()
    c_iotaC = nc.dram_tensor("c_iotaC", [128, CAP], F32, kind="ExternalInput").ap()
    c_iotac3 = nc.dram_tensor("c_iotac3", [128, 3], F32, kind="ExternalInput").ap()
    c_ltriu = nc.dram_tensor("c_ltriu", [128, 128], F32, kind="ExternalInput").ap()
    c_iota8 = nc.dram_tensor("c_iota8", [128, 128], F32, kind="ExternalInput").ap()
    out = nc.dram_tensor("out", [T, D], F32, kind="ExternalOutput").ap()
    xlnd = nc.dram_tensor("xlnd", [T, D], F32)
    rbd = nc.dram_tensor("rbd", [2, 1024], BF16)
    rbd2 = nc.dram_tensor("rbd2", [2, 1024], F32)
    ped = nc.dram_tensor("ped", [128, 128], BF16)
    cc_in = nc.dram_tensor("cc_in", [1, E], F32)
    cc_out = nc.dram_tensor("cc_out", [E, E], F32)
    if DEBUG_DUMPS:
        dbg_qkT = nc.dram_tensor("dbg_qkT", [128, 16 * T], BF16)
        dbg_v = nc.dram_tensor("dbg_v", [128, NT * H * 65], BF16)
        dbg_x0T = nc.dram_tensor("dbg_x0T", [128, ND * T], BF16)

    pl = Plan()

    with contextlib.ExitStack() as ctx:
        EN = ctx.enter_context
        ARB_N = 82432
        ARF_N = 7168
        arb_t = EN(nc.sbuf_tensor("arb", [128, ARB_N], BF16))
        arf_t = EN(nc.sbuf_tensor("arf", [128, ARF_N], F32))
        arb = Arena(arb_t, ARB_N)
        arf = Arena(arf_t, ARF_N)

        s_id = EN(nc.sbuf_tensor("s_id", [128, 128], F32))
        s_iotaC = EN(nc.sbuf_tensor("s_iotaC", [128, CAP], F32))
        s_iotac3 = EN(nc.sbuf_tensor("s_iotac3", [128, 3], F32))
        s_iota8 = EN(nc.sbuf_tensor("s_iota8", [128, 128], F32))
        s_ltriuf = EN(nc.sbuf_tensor("s_ltriuf", [128, 128], F32))
        s_onesb = EN(nc.sbuf_tensor("s_onesb", [128, 128], BF16))
        s_onesf = EN(nc.sbuf_tensor("s_onesf", [1, 128], F32))
        s_onescf = EN(nc.sbuf_tensor("s_onescf", [128, 1], F32))
        s_bqkc = EN(nc.sbuf_tensor("s_bqkc", [128, 16], F32))
        s_bvr = EN(nc.sbuf_tensor("s_bvr", [1, D], BF16))
        s_bmsar = EN(nc.sbuf_tensor("s_bmsar", [1, D], BF16))
        s_gateT = EN(nc.sbuf_tensor("s_gateT", [128, ND * E], F32))
        s_b1c = EN(nc.sbuf_tensor("s_b1c", [128, E * 32], F32))
        s_b2bc = EN(nc.sbuf_tensor("s_b2bc", [128, D], F32))
        s_prefix = EN(nc.sbuf_tensor("s_prefix", [E, 1], F32))
        s_ag = EN(nc.sbuf_tensor("s_ag", [E, E], F32))
        s_toff = EN(nc.sbuf_tensor("s_toff", [1, E], F32))
        s_off = EN(nc.sbuf_tensor("s_off", [1, E], F32))
        s_offbc = EN(nc.sbuf_tensor("s_offbc", [128, E], F32))
        s_mx = EN(nc.sbuf_tensor("s_mx", [1, 16], F32))
        s_mxb = EN(nc.sbuf_tensor("s_mxb", [1, 1], BF16))
        s_negc = EN(nc.sbuf_tensor("s_negc", [128, 1], F32))
        s_M1 = EN(nc.sbuf_tensor("s_M1", [128, NT * E], BF16))
        s_M1f = EN(nc.sbuf_tensor("s_M1f", [128, NT * E], F32))
        s_ranks = EN(nc.sbuf_tensor("s_ranks", [128, NT * E], F32))
        s_cols = EN(nc.sbuf_tensor("s_cols", [128, NT * 4], F32))
        s_gs = EN(nc.sbuf_tensor("s_gs", [128, NT * 3], F32))
        s_lgtall = EN(nc.sbuf_tensor("s_lgtall", [128, NT * E], F32))
        s_ls = EN(nc.sbuf_tensor("s_ls", [128, NT * E], F32))
        s_les = EN(nc.sbuf_tensor("s_les", [128, NT * E], F32))
        s_gateTb = EN(nc.sbuf_tensor("s_gateTb", [128, 2 * ND * E], BF16))
        s_idb = EN(nc.sbuf_tensor("s_idb", [128, 128], BF16))
        s_stat = EN(nc.sbuf_tensor("s_stat", [128, 8], F32))
        s_pec = EN(nc.sbuf_tensor("s_pec", [128, 1], F32))
        s_eps = EN(nc.sbuf_tensor("s_eps", [128, 1], F32))
        s_cnt = EN(nc.sbuf_tensor("s_cnt", [1, E], F32))

        P0 = EN(nc.psum_tensor("P0", [128, 1024], F32))
        P1 = EN(nc.psum_tensor("P1", [128, 1024], F32))
        P2 = EN(nc.psum_tensor("P2", [128, 1024], F32))
        P3 = EN(nc.psum_tensor("P3", [128, 512], F32))
        P4 = EN(nc.psum_tensor("P4", [128, 512], F32))

        for nm in ("ts", "vs", "ss", "cs", "gs", "dC", "dT", "dB", "dXC",
                   "dQKa", "dQKb", "dWV", "dMSA", "dW1a", "dW1b", "dW1c", "dW2a",
                   "dW2b", "dW2c", "dSTG", "dSTGb", "dC2", "dB2", "dB2b", "dAG", "dx0c", "dx0d", "dx0a", "dx0b", "dxsa", "dxsb",
                   "dxla", "dxlb", "dOa", "dOb", "dBCa", "dBCb"):
            pl.sems[nm] = EN(nc.semaphore(nm))

        def yld(dst, src, sem):
            return pl.op("y", lambda e, d=dst, s=src: e.dma_start(out=d, in_=s),
                         inc=sem, by=16)

        def gld(dst, src, sem):
            return pl.op("g", lambda e, d=dst, s=src: e.dma_start(out=d, in_=s),
                         inc=sem, by=16)

        def MM(o, l, r, st, sp, inc=None):
            return pl.op("t", lambda e, o=o, l=l, r=r, st=st, sp=sp:
                         e.matmul(o, l, r, start=st, stop=sp), inc=inc)

        v_id = yld(s_id[:, :], c_ident, "dC2")

        arb.alloc("x0T", ND * T)
        arf.alloc("x0ld", 4 * 1024)
        X0SEM = ("dx0a", "dx0b", "dx0c", "dx0d")
        x0ld_v = {}
        for tt in range(4):
            x0ld_v[tt] = yld(
                arf.ap("x0ld", 0, 128, (tt % 4) * 1024, (tt % 4) * 1024 + 1024),
                x0[tt * 128:(tt + 1) * 128, :], X0SEM[tt % 4])

        yld(s_iotaC[:, :], c_iotaC, "dC")
        yld(s_iotac3[:, :], c_iotac3, "dC")
        yld(s_iota8[:, :], c_iota8, "dC")
        yld(s_ltriuf[:, :], c_ltriu, "dC")
        yld(s_bqkc[:, :], bqkc, "dC")
        yld(s_gateT.ap().rearrange("p (dj e) -> p dj e", e=E),
            gateT.rearrange("(dj p) e -> p dj e", p=128), "dC")
        yld(s_b1c.ap().rearrange("p (e m) -> p e m", m=32),
            b1c.rearrange("e p m -> p e m"), "dC")
        v_consts = yld(s_prefix[:, :], prefix, "dC")

        gld(s_bvr[:, :], bvr, "dB")
        gld(s_bmsar[:, :], bmsar, "dB")

        x0t_evac = {}
        pl.w("t", "dC2", v_id)
        for tt in range(NT):
            pl.w("t", X0SEM[tt % 4], x0ld_v[tt])
            for dj in range(ND):
                k2 = tt * ND + dj
                psl = (P3 if k2 % 2 == 0 else P4)[0:128, 0:128]
                if k2 >= 2:
                    pl.w("t", "vs", x0t_evac[k2 - 2])
                src_ap = arf.ap("x0ld", 0, 128, (tt % 4) * 1024 + dj * 128,
                                (tt % 4) * 1024 + dj * 128 + 128)
                tv = pl.op("t", lambda e, o=psl, s=src_ap:
                           e.transpose(o, s, s_id[:, :]), inc="ts")
                pl.w("v", "ts", tv)
                xdst = arb.ap("x0T", 0, 128, dj * T + tt * 128, dj * T + tt * 128 + 128)
                x0t_evac[k2] = pl.op("v", lambda e, d=xdst, p=psl:
                                     e.tensor_copy(d, p), inc="vs")
            if tt + 4 < NT:
                pl.w("y", "ts", pl.cur("ts"))
                x0ld_v[tt + 4] = yld(
                    arf.ap("x0ld", 0, 128, (tt % 4) * 1024, (tt % 4) * 1024 + 1024),
                    x0[(tt + 4) * 128:(tt + 5) * 128, :], X0SEM[tt % 4])
        v_x0T = pl.cur("vs")
        ts_x0t = pl.cur("ts")
        arf.release("x0ld")

        pl.w("v", "dC", v_consts)
        pl.op("v", lambda e: e.memset(s_onesb[:, :], 1.0))
        pl.op("v", lambda e: e.memset(s_onesf[:, :], 1.0))
        pl.op("v", lambda e: e.memset(s_onescf[:, :], 1.0))
        pl.op("v", lambda e: e.memset(s_eps[:, :], EPS))
        pl.op("v", lambda e: e.memset(s_toff[:, :], 0.0))
        pl.op("v", lambda e: e.tensor_copy(s_idb[:, :], s_id[:, :]))
        pl.op("v", lambda e: e.tensor_copy(s_gateTb[:, 0:64], s_gateT[:, :]), inc="vs")
        pl.vself()
        pl.op("v", lambda e: e.tensor_sub(s_ls[:, 0:64], s_gateT[:, :],
                                          s_gateTb[:, 0:64]), inc="vs")
        pl.vself()
        v_prep = pl.op("v", lambda e: e.tensor_copy(s_gateTb[:, 64:128],
                                                    s_ls[:, 0:64]), inc="vs")

        arb.alloc("qkT", 16 * T)
        arb.alloc("v", NT * H * 65)
        arb.alloc("wqkbuf", 2 * ND * 128)
        arb.alloc("wvbuf", ND * 1024)

        vones_ap = arb.ap("v", 0, 128, 0, NT * H * 65).rearrange(
            "p (t c) -> p t c", c=65)[:, :, 64:65]
        v_vones = pl.op("v", lambda e, a=vones_ap: e.memset(a, 1.0), inc="vs")

        ct_order = []
        for j in range(8):
            ct_order += [j, 8 + j]

        def plan_wqk_load(i):
            dst = arb.ap("wqkbuf", 0, 128, (i % 2) * 1024, (i % 2) * 1024 + 1024)
            dst = dst.rearrange("p (dj c) -> p dj c", c=128)
            ct = ct_order[i]
            src = wqk[:, ct * 128:(ct + 1) * 128].rearrange("(dj p) c -> p dj c", p=128)
            sem = "dQKa" if i % 2 == 0 else "dQKb"
            return (sem, gld(dst, src, sem))

        wqk_v = [plan_wqk_load(0), plan_wqk_load(1)]
        wv_v = []
        for dj in range(ND):
            wv_v.append(gld(arb.ap("wvbuf", 0, 128, dj * 1024, (dj + 1) * 1024),
                            wv[dj * 128:(dj + 1) * 128, :], "dWV"))

        qk_slotsets = [[(P0, 0), (P0, 512), (P1, 0), (P1, 512)],
                       [(P2, 0), (P2, 512), (P3, 0), (P4, 0)]]
        qk_evac = {}
        qk_mm_done = {}
        pl.w("t", "vs", v_x0T)
        for i, ct in enumerate(ct_order):
            pl.w("t", wqk_v[i][0], wqk_v[i][1])
            sset = qk_slotsets[i % 2]
            if i >= 2:
                for tch in range(4):
                    pl.w("t", "vs", qk_evac[(i - 2) * 4 + tch])
            for dj in range(ND):
                lhs = arb.ap("wqkbuf", 0, 128, (i % 2) * 1024 + dj * 128,
                             (i % 2) * 1024 + dj * 128 + 128)
                for tch in range(4):
                    pt, pc = sset[tch]
                    rhs = arb.ap("x0T", 0, 128, dj * T + tch * 512,
                                 dj * T + tch * 512 + 512)
                    MM(pt[:, pc:pc + 512], lhs, rhs, dj == 0, dj == ND - 1,
                       inc="ts" if (dj == ND - 1 and tch == 3) else None)
            v_mm = pl.cur("ts")
            pl.w("v", "ts", v_mm)
            for tch in range(4):
                pt, pc = sset[tch]
                dst = arb.ap("qkT", 0, 128, ct * T + tch * 512, ct * T + tch * 512 + 512)
                qk_evac[i * 4 + tch] = pl.op(
                    "v", lambda e, d=dst, p=pt[:, pc:pc + 512], b=s_bqkc[:, ct:ct + 1]:
                    e.tensor_scalar(d, p, b, None, ALU.add), inc="vs")
            qk_mm_done[i] = pl.cur("ts")
            if i + 2 < 16:
                pl.w("g", "ts", qk_mm_done[i])
                wqk_v.append(plan_wqk_load(i + 2))

        pl.w("t", "dWV", wv_v[-1])
        pl.w("t", "dB", 32)
        pl.w("t", "vs", v_vones)
        for tch in range(4):
            pl.w("t", "vs", qk_evac[14 * 4 + tch])
        v_evac = {}
        for tt in range(NT):
            vpt = P0 if tt % 2 == 0 else P1
            if tt >= 2:
                pl.w("t", "vs", v_evac[tt - 2])
            for dj in range(ND):
                lhs = arb.ap("x0T", 0, 128, dj * T + tt * 128, dj * T + tt * 128 + 128)
                for nch in range(2):
                    rhs = arb.ap("wvbuf", 0, 128, dj * 1024 + nch * 512,
                                 dj * 1024 + nch * 512 + 512)
                    MM(vpt[:, nch * 512:nch * 512 + 512], lhs, rhs, dj == 0, False)
            for nch in range(2):
                MM(vpt[:, nch * 512:nch * 512 + 512], s_onesb[0:1, :],
                   s_bvr[0:1, nch * 512:nch * 512 + 512], False, True,
                   inc="ts" if nch == 1 else None)
            v_mm = pl.cur("ts")
            pl.w("v", "ts", v_mm)
            vsrc = vpt.ap().rearrange("p (h c) -> p h c", c=64)
            vdst = arb.ap("v", 0, 128, tt * H * 65, (tt + 1) * H * 65).rearrange(
                "p (h c) -> p h c", c=65)[:, :, 0:64]
            v_evac[tt] = pl.op("v", lambda e, d=vdst, s=vsrc:
                               e.tensor_copy(d, s), inc="vs")
        v_vdone = pl.cur("vs")
        ts_vdone = pl.cur("ts")
        if DEBUG_DUMPS:
            pl.w("y", "ts", ts_vdone)
            pl.w("y", "vs", v_vdone)
            yld2 = lambda d, s: pl.op("y", lambda e, d=d, s=s: e.dma_start(out=d, in_=s),
                                      inc="dC", by=16)
            yld2(dbg_x0T.ap(), arb.ap("x0T", 0, 128, 0, ND * T))
            yld2(dbg_qkT.ap(), arb.ap("qkT", 0, 128, 0, 16 * T))
            yld2(dbg_v.ap(), arb.ap("v", 0, 128, 0, NT * H * 65))
            pl.w("v", "dC", pl.cur("dC"))
        arb.release("x0T")
        arb.release("wqkbuf")
        arb.release("wvbuf")

        arb.alloc("sq", 2 * T)
        norm_done = {}
        for grp, base in (("q", 0), ("k", 8)):
            for j in range(8):
                buf = j % 2
                if j >= 2:
                    pl.w("v", "ts", norm_done[(grp, j - 2)])
                elif grp == "k":
                    pl.w("v", "ts", norm_done[("q", 6 + j)])
                src = arb.ap("qkT", 0, 128, (base + j) * T, (base + j) * T + T)
                dst = arb.ap("sq", 0, 128, buf * T, buf * T + T)
                pl.w("v", "vs", qk_evac[ct_order.index(base + j) * 4 + 3])
                sqv = pl.op("v", lambda e, d=dst, s=src: e.tensor_mul(d, s, s), inc="vs")
                pl.w("t", "vs", sqv)
                chunks = ([(P0, 0), (P0, 512), (P1, 0), (P1, 512)] if grp == "q"
                          else [(P2, 0), (P2, 512), (P3, 0), (P4, 0)])
                for ci, (pt, pc) in enumerate(chunks):
                    rhs = arb.ap("sq", 0, 128, buf * T + ci * 512, buf * T + ci * 512 + 512)
                    MM(pt[0:1, pc:pc + 512], s_onesb[:, 0:1], rhs, j == 0, j == 7,
                       inc="ts" if ci == 3 else None)
                norm_done[(grp, j)] = pl.cur("ts")
        arb.release("sq")
        pl.w("v", "ts", norm_done[("k", 7)])
        pl.op("v", lambda e: e.reduce_max(s_mx[0:1, 0:1], P0[0:1, 0:1024], axis=AX))
        pl.op("v", lambda e: e.reduce_max(s_mx[0:1, 1:2], P1[0:1, 0:1024], axis=AX))
        pl.op("v", lambda e: e.reduce_max(s_mx[0:1, 2:3], P2[0:1, 0:1024], axis=AX))
        pl.op("v", lambda e: e.reduce_max(s_mx[0:1, 3:4], P3[0:1, 0:512], axis=AX))
        pl.op("v", lambda e: e.reduce_max(s_mx[0:1, 4:5], P4[0:1, 0:512], axis=AX),
              inc="vs")
        pl.vself()
        pl.op("v", lambda e: e.tensor_max(s_mx[0:1, 5:6], s_mx[0:1, 0:1], s_mx[0:1, 1:2]))
        pl.op("v", lambda e: e.tensor_max(s_mx[0:1, 6:7], s_mx[0:1, 3:4], s_mx[0:1, 4:5]),
              inc="vs")
        pl.vself()
        pl.op("v", lambda e: e.tensor_max(s_mx[0:1, 7:8], s_mx[0:1, 6:7], s_mx[0:1, 2:3]),
              inc="vs")
        pl.vself()
        pl.op("v", lambda e: e.tensor_mul(s_mx[0:1, 8:9], s_mx[0:1, 5:6], s_mx[0:1, 7:8]),
              inc="vs")
        v_prod = pl.cur("vs")
        pl.w("s", "vs", v_prod)
        s_sqr = pl.op("s", lambda e: e.activation(s_mx[0:1, 9:10], s_mx[0:1, 8:9], AF.Sqrt),
                      inc="ss")
        pl.w("v", "ss", s_sqr)
        pl.op("v", lambda e: e.tensor_scalar_mul(s_mx[0:1, 10:11], s_mx[0:1, 9:10],
                                                 -0.125), inc="vs")
        pl.vself()
        pl.op("v", lambda e: e.tensor_copy(s_mxb[0:1, 0:1], s_mx[0:1, 10:11]), inc="vs")
        v_mxb = pl.cur("vs")
        pl.w("t", "vs", v_mxb)
        t_negbc = MM(P3[0:128, 0:1], s_onesb[0:1, :], s_mxb[0:1, 0:1], True, True,
                     inc="ts")
        pl.w("v", "ts", t_negbc)
        v_negc = pl.op("v", lambda e: e.tensor_copy(s_negc[:, 0:1], P3[0:128, 0:1]),
                       inc="vs")

        arb.alloc("attnT", ND * T)
        arb.alloc("exp", 6 * 1024)
        arb.alloc("rbc", 2 * 1024)
        arb.alloc("recip", 2 * 128)

        w1_v = {}
        w2_v = {}

        def load_w1(e_, mg, buf, wait_ts=None):
            pl.w("g", "ts", wait_ts)
            dst = arb.ap("w1buf", 0, 128, buf * ND * 512, (buf + 1) * ND * 512)
            dst = dst.rearrange("p (dj h) -> p dj h", h=512)
            src = w1[e_, :, mg * 512:(mg + 1) * 512].rearrange("(dj p) h -> p dj h", p=128)
            sem = ("dW1a", "dW1b", "dW1c")[buf]
            w1_v[(e_, mg)] = (sem, gld(dst, src, sem))

        def load_w2(e_, kg, buf, wait_ts=None):
            pl.w("g", "ts", wait_ts)
            dst = arb.ap("w2buf", 0, 128, buf * 2048, (buf + 1) * 2048)
            dst = dst.rearrange("p (kj d) -> p kj d", d=1024)
            srcw = w2[e_, kg * 256:(kg + 1) * 256, :].rearrange("(kj p) d -> p kj d",
                                                                p=128)
            sem = ("dW2a", "dW2b", "dW2c")[buf]
            w2_v[(e_, kg)] = (sem, gld(dst, srcw, sem))

        arf.alloc("rsrow", 2 * 1024)
        arf.alloc("rs8", 2 * 128)
        exp_last = {}
        expv = {}
        evac_par = {}
        first_exp = [True]

        def emit_S2(slot, h, qb, kt):
            bp = 64 * slot
            pt = P1 if slot else P0
            if slot in exp_last:
                pl.w("t", "ss", exp_last[slot])
            for nch in range(2):
                lhs = arb.ap("qkT", bp, bp + 64, (8 + h // 2) * T + kt * 128,
                             (8 + h // 2) * T + kt * 128 + 128)
                rhs = arb.ap("qkT", bp, bp + 64,
                             (h // 2) * T + qb * 1024 + nch * 512,
                             (h // 2) * T + qb * 1024 + nch * 512 + 512)
                MM(pt[:, nch * 512:nch * 512 + 512], lhs, rhs, True, True,
                   inc="ts" if nch == 1 else None)
            return pl.cur("ts")

        def emit_exp2(slot, smm, kt):
            pt = P1 if slot else P0
            ebuf = 3 * slot + (kt % 3)
            edst = arb.ap("exp", 0, 128, ebuf * 1024, ebuf * 1024 + 1024)
            pl.w("s", "ts", smm)
            if first_exp[0]:
                pl.w("s", "vs", v_negc)
                first_exp[0] = False
            exp_last[slot] = pl.op(
                "s", lambda e, d=edst, p=pt[:, 0:1024]:
                e.activation(d, p, AF.Exp, bias=s_negc[:, 0:1], scale=0.125),
                inc="ss")
            expv[(slot, kt)] = exp_last[slot]

        def emit_PV2(slot, h, par, kt, flush):
            if flush:
                pl.w("t", "ss", expv[(slot, kt)])
            if kt == 0:
                pl.w("t", "vs", evac_par.get(par, v_vdone))
            ebuf = 3 * slot + (kt % 3)
            lhsv = arb.ap("v", 0, 128, kt * H * 65 + h * 65,
                          kt * H * 65 + h * 65 + 65)
            for nch in range(2):
                rhs = arb.ap("exp", 0, 128, ebuf * 1024 + nch * 512,
                             ebuf * 1024 + nch * 512 + 512)
                out = (P2[0:65, nch * 512:nch * 512 + 512] if par == 0
                       else (P3 if nch == 0 else P4)[0:65, 0:512])
                MM(out, lhsv, rhs, kt == 0, kt == NT - 1,
                   inc="ts" if (kt == NT - 1 and nch == 1) else None)
            return pl.cur("ts") if kt == NT - 1 else None

        def emit_tails_pair(A, B):
            chains = []
            for idx, (h, qb, par, pvd) in enumerate((A, B)):
                chains.append((h, qb, par, pvd, "y" if par == 0 else "g",
                               "dBCa" if par == 0 else "dBCb"))
            vrs, dmae, vrc, vbcv = {}, {}, {}, {}
            for (h, qb, par, pvd, eng, sem) in sorted(chains, key=lambda c: -c[2]):
                rsr = arf.ap("rsrow", 0, 1, par * 1024, par * 1024 + 1024)
                pl.w("v", "ts", pvd)
                pl.vself()
                if par == 0:
                    vrs[par] = pl.op("v", lambda e, d=rsr:
                                     e.tensor_copy(d, P2[64:65, 0:1024]), inc="vs")
                else:
                    rs0 = arf.ap("rsrow", 0, 1, par * 1024, par * 1024 + 512)
                    rs1 = arf.ap("rsrow", 0, 1, par * 1024 + 512, par * 1024 + 1024)
                    pl.op("v", lambda e, d=rs0: e.tensor_copy(d, P3[64:65, 0:512]))
                    vrs[par] = pl.op("v", lambda e, d=rs1:
                                     e.tensor_copy(d, P4[64:65, 0:512]), inc="vs")
            for (h, qb, par, pvd, eng, sem) in chains:
                rsr = arf.ap("rsrow", 0, 1, par * 1024, par * 1024 + 1024)
                r8src = rsr.rearrange("p (j c) -> p j c", j=8)
                rs8 = arf.ap("rs8", 0, 8, par * 128, par * 128 + 128)
                r8dst = rs8.rearrange("p (x c) -> p x c", x=1)
                pl.w(eng, "vs", vrs[par])
                pl.op(eng, lambda e, s=r8src, d=r8dst: e.dma_start(out=d, in_=s),
                      inc=sem, by=16)
                dmae[par] = pl.cur(sem)

            def _rc(e, d, p):
                with nc.allow_low_precision(reason="bf16 softmax normalizer"):
                    return e.reciprocal(d, p)
            for (h, qb, par, pvd, eng, sem) in sorted(chains, key=lambda c: c[2]):
                rs8 = arf.ap("rs8", 0, 8, par * 128, par * 128 + 128)
                rc8 = arb.ap("recip", 0, 8, par * 128, par * 128 + 128)
                pl.w("v", sem, dmae[par])
                pl.vself()
                vrc[par] = pl.op("v", lambda e, d=rc8, s=rs8, f=_rc: f(e, d, s),
                                 inc="vs")
            for (h, qb, par, pvd, eng, sem) in chains:
                rc8 = arb.ap("recip", 0, 8, par * 128, par * 128 + 128)
                pl.w(eng, "vs", vrc[par])
                r8dst = bass.AP(rbd, par * 1024, [[128, 8], [1, 128]])
                pl.op(eng, lambda e, s=rc8, d=r8dst: e.dma_start(out=d, in_=s),
                      inc=sem, by=16)
                pl.w(eng, sem, pl.cur(sem))
                bsrc = bass.AP(rbd, par * 1024, [[0, 64], [1, 1024]])
                rbcd = arb.ap("rbc", 0, 64, par * 1024, par * 1024 + 1024)
                pl.op(eng, lambda e, s=bsrc, d=rbcd: e.dma_start(out=d, in_=s),
                      inc=sem, by=16)
                vbcv[par] = pl.cur(sem)
            for (h, qb, par, pvd, eng, sem) in sorted(chains, key=lambda c: c[2]):
                bp = (h % 2) * 64
                rbcd = arb.ap("rbc", 0, 64, par * 1024, par * 1024 + 1024)
                pl.w("v", sem, vbcv[par])
                adst = arb.ap("attnT", bp, bp + 64, (h // 2) * T + qb * 1024,
                              (h // 2) * T + qb * 1024 + 1024)
                if par == 0:
                    evac_par[par] = pl.op(
                        "v", lambda e, d=adst, r=rbcd: e.tensor_mul(
                            d, P2[0:64, 0:1024], r), inc="vs")
                else:
                    rb0 = arb.ap("rbc", 0, 64, par * 1024, par * 1024 + 512)
                    rb1 = arb.ap("rbc", 0, 64, par * 1024 + 512, par * 1024 + 1024)
                    ad0 = arb.ap("attnT", bp, bp + 64, (h // 2) * T + qb * 1024,
                                 (h // 2) * T + qb * 1024 + 512)
                    ad1 = arb.ap("attnT", bp, bp + 64,
                                 (h // 2) * T + qb * 1024 + 512,
                                 (h // 2) * T + qb * 1024 + 1024)
                    pl.op("v", lambda e, d=ad0, r=rb0: e.tensor_mul(
                        d, P3[0:64, 0:512], r))
                    evac_par[par] = pl.op(
                        "v", lambda e, d=ad1, r=rb1: e.tensor_mul(
                            d, P4[0:64, 0:512], r), inc="vs")

        WMSA_EARLY = 8
        arb.alloc("wmsabA", WMSA_EARLY * 1024)

        def wmsab_ap(dj, a, b):
            if dj < WMSA_EARLY:
                return arb.ap("wmsabA", 0, 128, dj * 1024 + a, dj * 1024 + b)
            d2 = dj - WMSA_EARLY
            return arb.ap("wmsabB", 0, 128, d2 * 1024 + a, d2 * 1024 + b)
        wmsa_v = []
        pl.w("g", "ts", pl.cur("ts"))
        pl.w("g", "vs", pl.cur("vs"))
        pl.w("g", "ss", pl.cur("ss"))
        for dj in range(WMSA_EARLY):
            wmsa_v.append(gld(wmsab_ap(dj, 0, 1024),
                              wmsa[dj * 128:(dj + 1) * 128, :], "dMSA"))

        for jp in range(ND):
            for sub in range(2):
                hA, qbA = 2 * jp, sub
                hB, qbB = 2 * jp + 1, 1 - sub
                parA, parB = qbA, qbB
                pvo = [(0, hA, parA), (1, hB, parB)]
                pvo.sort(key=lambda c: c[2])
                for kt in range(NT):
                    if kt >= 3:
                        for (sl, hh, pp) in pvo:
                            emit_PV2(sl, hh, pp, kt - 3, False)
                    if kt == NT - 1:
                        for (sl, hh, pp) in pvo:
                            emit_PV2(sl, hh, pp, NT - 3, False)
                    sA = emit_S2(0, hA, qbA, kt)
                    sB = emit_S2(1, hB, qbB, kt)
                    emit_exp2(0, sA, kt)
                    emit_exp2(1, sB, kt)
                for (sl, hh, pp) in pvo:
                    emit_PV2(sl, hh, pp, NT - 2, False)
                pvd = {}
                for (sl, hh, pp) in pvo:
                    pvd[sl] = emit_PV2(sl, hh, pp, NT - 1, True)
                pvdA, pvdB = pvd[0], pvd[1]
                emit_tails_pair((hA, qbA, parA, pvdA), (hB, qbB, parB, pvdB))
        v_attn_all = pl.cur("vs")
        ts_attn_all = pl.cur("ts")
        arb.release("qkT")
        arb.release("v")
        arb.release("exp")
        arb.release("rbc")
        arb.release("recip")

        arb.alloc("xlnb", NT * 1024)
        arb.alloc("w1buf", 3 * ND * 512)
        arb.alloc("w2buf", 3 * 2048)
        if ND > WMSA_EARLY:
            arb.alloc("wmsabB", (ND - WMSA_EARLY) * 1024)
        arf.release("rsrow")
        arf.release("rs8")
        arf.alloc("x0st", 2 * 1024)
        arf.alloc("stgA", 2 * 1024)
        arf.alloc("stgB", 1024)
        arf.alloc("lgT", 2048)

        pl.w("g", "vs", v_attn_all)
        pl.w("g", "ts", ts_attn_all)
        for dj in range(WMSA_EARLY, ND):
            wmsa_v.append(gld(wmsab_ap(dj, 0, 1024),
                              wmsa[dj * 128:(dj + 1) * 128, :], "dMSA"))
        load_w1(0, 0, 0)
        load_w1(0, 1, 1)
        load_w1(0, 2, 2)
        load_w2(0, 0, 0)
        load_w2(0, 1, 1)
        load_w2(0, 2, 2)

        x0_v = {}
        pl.w("y", "ts", ts_x0t)
        pl.w("y", "dBCa", pl.cur("dBCa"))
        pl.w("y", "dBCb", pl.cur("dBCb"))
        for tt in range(2):
            x0_v[tt] = yld(arf.ap("x0st", 0, 128, (tt % 2) * 1024, (tt % 2) * 1024 + 1024),
                           x0[tt * 128:(tt + 1) * 128, :],
                           "dx0a" if tt % 2 == 0 else "dx0b")

        ln1_add = {}
        ln1_done = {}
        xln_store = {}
        pl.w("t", "dMSA", wmsa_v[-1])
        pl.w("t", "vs", v_attn_all)
        for tt in range(NT):
            pt = P0 if tt % 2 == 0 else P1
            if tt >= 2:
                pl.w("t", "vs", ln1_add[tt - 2])
            for nch in range(2):
                for dj in range(ND):
                    lhs = arb.ap("attnT", 0, 128, dj * T + tt * 128, dj * T + tt * 128 + 128)
                    rhs = wmsab_ap(dj, nch * 512, nch * 512 + 512)
                    MM(pt[:, nch * 512:nch * 512 + 512], lhs, rhs, dj == 0, False)
                MM(pt[:, nch * 512:nch * 512 + 512], s_onesb[0:1, :],
                   s_bmsar[0:1, nch * 512:nch * 512 + 512], False, True,
                   inc="ts" if nch == 1 else None)
            msa_mm = pl.cur("ts")

            stg = arf.ap("stgA", 0, 128, (tt % 2) * 1024, (tt % 2) * 1024 + 1024)
            stgb = arf.ap("stgB", 0, 128, 0, 1024)
            x0s = arf.ap("x0st", 0, 128, (tt % 2) * 1024, (tt % 2) * 1024 + 1024)
            pl.w("v", "ts", msa_mm)
            pl.w("v", "dx0a" if tt % 2 == 0 else "dx0b", x0_v[tt])
            if tt >= 2:
                pl.w("v", "dxsa" if tt % 2 == 0 else "dxsb", xln_store[tt - 2])
            pl.vself()
            ln1_add[tt] = pl.op("v", lambda e, d=stgb, p=pt[:, 0:1024], x=x0s:
                                e.tensor_add(d, p, x), inc="vs")
            if tt + 2 < NT:
                pl.w("y", "vs", ln1_add[tt])
                x0_v[tt + 2] = yld(
                    arf.ap("x0st", 0, 128, (tt % 2) * 1024, (tt % 2) * 1024 + 1024),
                    x0[(tt + 2) * 128:(tt + 3) * 128, :],
                    "dx0a" if tt % 2 == 0 else "dx0b")
            pl.w("s", "vs", ln1_add[tt])
            s_sq2 = pl.op("s", lambda e, o=stg, y=stgb: e.activation(o, y, AF.Square),
                          inc="ss")
            pl.vself()
            pl.op("v", lambda e, y=stgb: e.reduce_sum(s_stat[:, 0:1], y, axis=AX))
            pl.w("v", "ss", s_sq2)
            pl.op("v", lambda e, y=stg: e.reduce_sum(s_stat[:, 4:5], y, axis=AX),
                  inc="vs")
            pl.vself()
            pl.op("v", lambda e: e.tensor_scalar_mul(s_stat[:, 1:2], s_stat[:, 0:1],
                                                     1.0 / D))
            pl.op("v", lambda e: e.tensor_scalar_mul(s_stat[:, 3:4], s_stat[:, 4:5],
                                                     1.0 / D), inc="vs")
            pl.vself()
            pl.op("v", lambda e: e.tensor_mul(s_stat[:, 2:3], s_stat[:, 1:2],
                                              s_stat[:, 1:2]), inc="vs")
            pl.vself()
            pl.op("v", lambda e: e.tensor_sub(s_stat[:, 4:5], s_stat[:, 3:4],
                                              s_stat[:, 2:3]), inc="vs")
            v_var = pl.cur("vs")
            pl.w("s", "vs", v_var)
            s_std = pl.op("s", lambda e: e.activation(s_stat[:, 5:6], s_stat[:, 4:5],
                                                      AF.Sqrt, bias=s_eps[:, 0:1],
                                                      scale=1.0), inc="ss")
            pl.w("v", "ss", s_std)
            pl.op("v", lambda e: e.reciprocal(s_stat[:, 6:7], s_stat[:, 5:6]), inc="vs")
            pl.vself()
            pl.op("v", lambda e, y=stgb, o=stg: e.tensor_scalar(
                o, y, s_stat[:, 1:2], s_stat[:, 6:7], ALU.subtract, ALU.mult),
                inc="vs")
            v_xln = pl.cur("vs")
            pl.vself()
            xlb = arb.ap("xlnb", 0, 128, tt * 1024, (tt + 1) * 1024)
            ln1_done[tt] = pl.op("v", lambda e, d=xlb, s=stg: e.tensor_copy(d, s),
                                 inc="vs")
            pl.w("y", "vs", v_xln)
            xln_store[tt] = pl.op(
                "y", lambda e, s=stg, tt=tt: e.dma_start(
                    out=xlnd.ap()[tt * 128:(tt + 1) * 128, :], in_=s),
                inc="dxsa" if tt % 2 == 0 else "dxsb", by=16)

        ts_ln_all = pl.cur("ts")

        arb.alloc("xlnT", 2 * T)
        P2bf = P2.bitcast(BF16)
        lgchunks = [(P0, 0), (P0, 512), (P1, 0), (P1, 512)]
        xt_evac = {}
        lgmm_done = {}
        for dj in range(ND):
            xslot = dj % 2
            for tt in range(NT):
                k2 = dj * NT + tt
                psl = P2bf.ap()[0:128, (k2 % 2) * 1024:(k2 % 2) * 1024 + 128]
                if dj == 0:
                    pl.w("t", "vs", ln1_done[tt])
                    if tt >= NT - 2:
                        pl.w("t", "vs", ln1_add[tt])
                if k2 >= 2:
                    pl.w("t", "ss", xt_evac[k2 - 2])
                if tt == 0 and dj >= 2:
                    pl.w("s", "ts", lgmm_done[dj - 2])
                xsrc = arb.ap("xlnb", 0, 128, tt * 1024 + dj * 128,
                              tt * 1024 + dj * 128 + 128)
                tv = pl.op("t", lambda e, o=psl, s=xsrc:
                           e.transpose(o, s, s_idb[:, :]), inc="ts")
                pl.w("s", "ts", tv)
                xd = arb.ap("xlnT", 0, 128, xslot * T + tt * 128,
                            xslot * T + tt * 128 + 128)
                xt_evac[k2] = pl.op("s", lambda e, d=xd, p=psl: e.copy(d, p),
                                    inc="ss")
            pl.w("t", "ss", xt_evac[dj * NT + NT - 1])
            for half in range(2):
                lhs = s_gateTb[:, half * 64 + dj * 8:half * 64 + dj * 8 + 8]
                for ci, (lpt, lpc) in enumerate(lgchunks):
                    rhs = arb.ap("xlnT", 0, 128, xslot * T + ci * 512,
                                 xslot * T + ci * 512 + 512)
                    MM(lpt[0:8, lpc:lpc + 512], lhs, rhs,
                       dj == 0 and half == 0, dj == ND - 1 and half == 1,
                       inc="ts" if ci == 3 else None)
            lgmm_done[dj] = pl.cur("ts")
        arb.release("xlnT")

        pl.w("s", "ts", lgmm_done[ND - 1])
        for ci, (lpt, lpc) in enumerate(lgchunks):
            lgd = arf.ap("lgT", 0, 8, ci * 512, (ci + 1) * 512)
            pl.op("s", lambda e, d=lgd, p=lpt[0:8, lpc:lpc + 512]: e.copy(d, p),
                  inc="ss" if ci == 3 else None)
        s_lgev = pl.cur("ss")
        tm_evac = {}
        for tt in range(NT):
            slot = P2[0:128, 256:264] if tt % 2 == 0 else P2[0:128, 768:776]
            if tt == 0:
                pl.w("t", "ss", s_lgev)
            if tt >= 2:
                pl.w("t", "ss", tm_evac[tt - 2])
            lgs = arf.ap("lgT", 0, 8, tt * 128, (tt + 1) * 128)
            tv = pl.op("t", lambda e, o=slot, s=lgs:
                       e.transpose(o, s, s_id[0:8, 0:8]), inc="ts")
            pl.w("s", "ts", tv)
            tm_evac[tt] = pl.op("s", lambda e, p=slot, tt=tt:
                                e.copy(s_lgtall[:, tt * 8:(tt + 1) * 8], p), inc="ss")

        pl.w("v", "ss", tm_evac[NT - 1])
        pl.vself()
        for tt in range(NT):
            pl.op("v", lambda e, tt=tt: e.reduce_max(
                s_gs[:, tt:tt + 1], s_lgtall[:, tt * 8:(tt + 1) * 8], axis=AX),
                inc="vs" if tt == NT - 1 else None)
        pl.vself()
        for tt in range(NT):
            pl.op("v", lambda e, tt=tt: e.tensor_scalar(
                s_ls[:, tt * 8:(tt + 1) * 8], s_lgtall[:, tt * 8:(tt + 1) * 8],
                s_gs[:, tt:tt + 1], None, ALU.subtract),
                inc="vs" if tt == NT - 1 else None)
        v_lsall = pl.cur("vs")
        pl.w("s", "vs", v_lsall)
        pl.sself()
        s_exw = pl.op("s", lambda e: e.activation(s_les[:, :], s_ls[:, :], AF.Exp),
                      inc="ss")
        pl.w("v", "ss", s_exw)
        for tt in range(NT):
            pl.op("v", lambda e, tt=tt: e.reduce_sum(
                s_gs[:, 16 + tt:17 + tt], s_les[:, tt * 8:(tt + 1) * 8], axis=AX),
                inc="vs" if tt == NT - 1 else None)
        pl.vself()
        pl.op("v", lambda e: e.reciprocal(s_cols[:, 32:48], s_gs[:, 16:32]), inc="vs")
        pl.op("v", lambda e: e.tensor_scalar(s_M1f[:, :], s_ls[:, :], 0.0, None,
                                             ALU.is_ge), inc="vs")
        pl.vself()
        pl.op("v", lambda e: e.scalar_tensor_tensor(
            s_ls[:, :], s_iota8[:, :], 9.0, s_M1f[:, :], ALU.subtract, ALU.mult),
            inc="vs")
        pl.vself()
        pl.op("v", lambda e: e.tensor_scalar_add(s_les[:, :], s_ls[:, :], 9.0),
              inc="vs")
        pl.vself()
        for tt in range(NT):
            pl.op("v", lambda e, tt=tt: e.tensor_reduce(
                s_cols[:, 16 + tt:17 + tt], s_les[:, tt * 8:(tt + 1) * 8],
                axis=AX, op=ALU.min), inc="vs" if tt == NT - 1 else None)
        pl.vself()
        for tt in range(NT):
            pl.op("v", lambda e, tt=tt: e.tensor_scalar(
                s_M1f[:, tt * 8:(tt + 1) * 8], s_iota8[:, tt * 8:(tt + 1) * 8],
                s_cols[:, 16 + tt:17 + tt], None, ALU.is_equal),
                inc="vs" if tt == NT - 1 else None)
        pl.vself()
        v_m1all = pl.op("v", lambda e: e.tensor_copy(s_M1[:, :], s_M1f[:, :]),
                        inc="vs")

        rp = P2[:, 768:776]
        cp = P2[0:1, 776:784]
        toff_v = {-1: v_m1all}
        pl.w("t", "vs", v_m1all)
        for tt in range(NT):
            pl.w("t", "vs", toff_v[tt - 1])
            pl.op("t", lambda e, tt=tt: e.matmul(rp, s_ltriuf[:, :],
                                                 s_M1f[:, tt * 8:tt * 8 + 8],
                                                 start=True, stop=False))
            pl.op("t", lambda e: e.matmul(rp, s_onesf[0:1, :], s_toff[0:1, :],
                                          start=False, stop=True), inc="ts")
            t_cnt = pl.op("t", lambda e, tt=tt: e.matmul(
                cp, s_onescf[:, 0:1], s_M1f[:, tt * 8:tt * 8 + 8],
                start=True, stop=True), inc="ts")
            pl.w("s", "ts", t_cnt)
            s_rke = pl.op("s", lambda e, tt=tt:
                          e.copy(s_ranks[:, tt * 8:tt * 8 + 8], rp), inc="ss")
            pl.w("v", "ss", s_rke)
            pl.w("v", "ts", t_cnt)
            pl.op("v", lambda e, tt=tt: e.tensor_mul(
                s_ranks[:, tt * 8:tt * 8 + 8], s_ranks[:, tt * 8:tt * 8 + 8],
                s_M1f[:, tt * 8:tt * 8 + 8]), inc="vs")
            pl.vself()
            pl.op("v", lambda e, tt=tt: e.reduce_sum(
                s_cols[:, tt:tt + 1], s_ranks[:, tt * 8:tt * 8 + 8], axis=AX))
            toff_v[tt] = pl.op("v", lambda e: e.tensor_add(
                s_toff[0:1, :], s_toff[0:1, :], cp), inc="vs")
        v_gate_all = pl.cur("vs")

        pl.vself()
        v_cntc = pl.op("v", lambda e: e.tensor_copy(s_cnt[0:1, :], s_toff[0:1, :]),
                       inc="vs")
        pl.w("g", "vs", v_cntc)
        g_in = pl.op("g", lambda e: e.dma_start(out=cc_in.ap(), in_=s_cnt[0:1, :]),
                     inc="dAG", by=16)
        pl.w("g", "dAG", g_in)
        if USE_COLLECTIVE:
            pl.op("g", lambda e: e.collective_compute(
                "AllGather", ALU.bypass, replica_groups=[list(range(B))],
                ins=[cc_in.ap()], outs=[cc_out.ap()]), inc="cs", by=1)
        else:
            for r in range(E):
                pl.op("g", lambda e, r=r: e.dma_start(out=cc_out.ap()[r:r + 1, :],
                                                      in_=cc_in.ap()),
                      inc="cs", by=16 if r == E - 1 else 16)
        pl.w("g", "cs", pl.cur("cs"))
        g_ag = pl.op("g", lambda e: e.dma_start(out=s_ag[:, :], in_=cc_out.ap()),
                     inc="dAG", by=16)

        ts_msa_all = pl.cur("ts")
        arb.release("attnT")
        arb.release("wmsabA")
        if ND > WMSA_EARLY:
            arb.release("wmsabB")
        arf.release("x0st")
        arf.release("lgT")
        arf.alloc("b2r", 2 * 1024)

        arb.alloc("EO", 24 * 1024)
        arb.alloc("Ht", 32 * CAP)
        arb.alloc("G", NT * CAP)
        arb.alloc("D0", ND * CAP)
        arb.alloc("D1", ND * CAP)

        dslots = [(P0, 0), (P0, 512), (P1, 0), (P1, 512), (P2, 0), (P2, 512),
                  (P3, 0), (P4, 0)]
        g_built = {}
        d_evac = {}
        h_evac = {}
        eo_evac = {}
        disp_last = {}
        v_b2 = None

        def emit_G(ee):
            for tt in range(NT):
                if tt == 0:
                    if ee == 0:
                        pl.w("v", "vs", v_gate_all)
                        pl.w("v", "ts", ts_msa_all)
                    else:
                        pl.w("v", "ts", disp_last[ee - 1])
                    pl.vself()
                gdst = arb.ap("G", 0, 128, tt * CAP, (tt + 1) * CAP)
                g_built[(ee, tt)] = pl.op(
                    "v", lambda e, d=gdst, tt=tt, ee=ee: e.tensor_scalar(
                        d, s_iotaC[:, :], s_cols[:, tt:tt + 1],
                        s_M1f[:, tt * 8 + ee:tt * 8 + ee + 1], ALU.is_equal, ALU.mult),
                    inc="vs")

        emit_G(0)
        for e_ in range(E):
            dbuf = "D0" if e_ % 2 == 0 else "D1"
            for dj in range(ND):
                pt, pc = dslots[dj]
                if e_ >= 1:
                    pl.w("t", "vs", d_evac[(e_ - 1, dj)])
                    if dj == 0:
                        pl.w("t", "vs", eo_evac[(e_ - 1, 2)])
                for tt in range(NT):
                    if dj == 0:
                        pl.w("t", "vs", g_built[(e_, tt)])
                        if e_ == 0 and tt == 0:
                            pl.w("t", "vs", ln1_done[NT - 1])
                    lhs = arb.ap("xlnb", 0, 128, tt * 1024 + dj * 128,
                                 tt * 1024 + dj * 128 + 128)
                    rhs = arb.ap("G", 0, 128, tt * CAP, (tt + 1) * CAP)
                    MM(pt[:, pc:pc + CAP], lhs, rhs, tt == 0, tt == NT - 1,
                       inc="ts" if tt == NT - 1 else None)
                dmm = pl.cur("ts")
                pl.w("v", "ts", dmm)
                ddst = arb.ap(dbuf, 0, 128, dj * CAP, (dj + 1) * CAP)
                d_evac[(e_, dj)] = pl.op(
                    "v", lambda e, d=ddst, p=pt[:, pc:pc + CAP]: e.tensor_copy(d, p),
                    inc="vs")
            disp_last[e_] = pl.cur("ts")

            hsets = [[(P0, 0), (P0, 512), (P1, 0), (P1, 512)],
                     [(P2, 0), (P2, 512), (P3, 0), (P4, 0)]]
            for mg in range(8):
                sset = hsets[mg % 2]
                gmg = e_ * 8 + mg
                pl.w("t", w1_v[(e_, mg)][0], w1_v[(e_, mg)][1])
                if mg >= 2:
                    for mi in range(4):
                        pl.w("t", "ss", h_evac[(e_, mg - 2, mi)])
                for mi in range(4):
                    pt, pc = sset[mi]
                    for dj in range(ND):
                        if mg == 0 and mi == 0:
                            pl.w("t", "vs", d_evac[(e_, dj)])
                        lhs = arb.ap("w1buf", 0, 128,
                                     (gmg % 3) * ND * 512 + dj * 512 + mi * 128,
                                     (gmg % 3) * ND * 512 + dj * 512 + mi * 128 + 128)
                        rhs = arb.ap(dbuf, 0, 128, dj * CAP, (dj + 1) * CAP)
                        MM(pt[:, pc:pc + CAP], lhs, rhs, dj == 0, dj == ND - 1,
                           inc="ts" if dj == ND - 1 else None)
                    hmm = pl.cur("ts")
                    m_t = mg * 4 + mi
                    hdst = arb.ap("Ht", 0, 128, m_t * CAP, (m_t + 1) * CAP)
                    pl.w("s", "ts", hmm)
                    h_evac[(e_, mg, mi)] = pl.op(
                        "s", lambda e, d=hdst, p=pt[:, pc:pc + CAP],
                        b=s_b1c[:, e_ * 32 + m_t:e_ * 32 + m_t + 1]:
                        e.activation(d, p, AF.Relu, bias=b, scale=1.0), inc="ss")
                nxt = gmg + 3
                if nxt < E * 8:
                    load_w1(nxt // 8, nxt % 8, gmg % 3, wait_ts=pl.cur("ts"))

            if e_ == 0:
                b2_v = {}
                for ee in range(2):
                    b2_v[ee] = yld(arf.ap("b2r", 0, 1, (ee % 2) * 1024,
                                          (ee % 2) * 1024 + 1024),
                                   b2[ee:ee + 1, :], "dB2" if ee % 2 == 0 else "dB2b")
            par = e_ % 2
            pl.w("t", "dB2" if par == 0 else "dB2b", b2_v[e_])
            pl.w("t", "ss", h_evac[(e_, 7, 3)])
            if e_ >= 1:
                pl.w("t", "vs", v_b2)
                pl.w("t", "vs", eo_evac[(e_ - 1, 2)])
            MM(P3[0:128, 0:512], s_onesf[0:1, :],
               arf.ap("b2r", 0, 1, par * 1024, par * 1024 + 512), True, True)
            t_b2 = MM(P4[0:128, 0:512], s_onesf[0:1, :],
                      arf.ap("b2r", 0, 1, par * 1024 + 512, par * 1024 + 1024),
                      True, True, inc="ts")
            if e_ + 2 < E:
                pl.w("y", "ts", t_b2)
                b2_v[e_ + 2] = yld(arf.ap("b2r", 0, 1, par * 1024, par * 1024 + 1024),
                                   b2[e_ + 2:e_ + 3, :],
                                   "dB2" if par == 0 else "dB2b")
            pl.w("v", "ts", t_b2)
            pl.op("v", lambda e: e.tensor_copy(s_b2bc[:, 0:512], P3[0:128, 0:512]))
            v_b2 = pl.op("v", lambda e: e.tensor_copy(s_b2bc[:, 512:1024],
                                                      P4[0:128, 0:512]), inc="vs")

            if e_ + 1 < E:
                emit_G(e_ + 1)

            eopt = [(P0, 128), (P1, 128), (P2, 64)]
            pl.w("t", "ss", h_evac[(e_, 7, 3)])
            if e_ >= 1:
                for ci in range(3):
                    pl.w("t", "vs", eo_evac[(e_ - 1, ci)])
            for kg in range(16):
                gkg = e_ * 16 + kg
                pl.w("t", w2_v[(e_, kg)][0], w2_v[(e_, kg)][1])
                for kj in range(2):
                    kt = kg * 2 + kj
                    for ci in range(3):
                        pt, crows = eopt[ci]
                        lhs = arb.ap("Ht", 0, 128, kt * CAP + ci * 128,
                                     kt * CAP + ci * 128 + crows)
                        for nch in range(2):
                            rhs = arb.ap("w2buf", 0, 128,
                                         (gkg % 3) * 2048 + kj * 1024 + nch * 512,
                                         (gkg % 3) * 2048 + kj * 1024 + nch * 512 + 512)
                            MM(pt[0:crows, nch * 512:nch * 512 + 512], lhs, rhs,
                               kt == 0, kt == 31,
                               inc="ts" if (kj == 1 and ci == 2 and nch == 1) else None)
                nxt = gkg + 3
                if nxt < E * 16:
                    load_w2(nxt // 16, nxt % 16, gkg % 3, wait_ts=pl.cur("ts"))
            eomm = pl.cur("ts")
            for ci in range(3):
                pt, crows = eopt[ci]
                pl.w("v", "ts", eomm)
                pl.w("v", "vs", v_b2)
                edst = arb.ap("EO", 0, crows, (e_ * 3 + ci) * 1024,
                              (e_ * 3 + ci) * 1024 + 1024)
                eo_evac[(e_, ci)] = pl.op(
                    "v", lambda e, d=edst, p=pt[0:crows, 0:1024], cr=crows:
                    e.tensor_add(d, p, s_b2bc[0:cr, 0:1024]), inc="vs")

        ts_moe_done = pl.cur("ts")
        arb.release("G")
        arb.release("D0")
        arb.release("D1")
        arb.release("Ht")
        arb.release("w1buf")
        arb.release("w2buf")
        arb.release("xlnb")
        arb.alloc("pebc", E * T)
        arb.alloc("CW", 3 * 24 * 128)
        arb.alloc("gsall", 128)
        arb.alloc("petr", 128)
        arf.release("b2r")
        arf.alloc("rrow", 2048)
        arf.alloc("rbcf", 2 * 1024)

        pl.w("t", "dAG", g_ag)
        pl.w("t", "vs", eo_evac[(E - 1, 2)])
        t_om = MM(P3[0:1, 0:8], s_prefix[:, :], s_ag[:, :], True, True, inc="ts")
        pl.w("v", "ts", t_om)
        pl.vself()
        v_off = pl.op("v", lambda e: e.tensor_copy(s_off[0:1, :], P3[0:1, 0:8]), inc="vs")
        pl.w("t", "vs", v_off)
        t_ob = MM(P3[0:128, 8:16], s_onesf[0:1, :], s_off[0:1, :], True, True, inc="ts")
        pl.w("v", "ts", t_ob)
        v_offbc = pl.op("v", lambda e: e.tensor_copy(s_offbc[:, 0:8], P3[0:128, 8:16]),
                        inc="vs")

        pl.vself()
        for tt in range(NT):
            pl.op("v", lambda e, tt=tt: e.tensor_mul(
                s_les[:, tt * 8:(tt + 1) * 8], s_offbc[:, 0:8],
                s_M1f[:, tt * 8:(tt + 1) * 8]), inc="vs" if tt == NT - 1 else None)
        pl.vself()
        for tt in range(NT):
            pl.op("v", lambda e, tt=tt: e.reduce_sum(
                s_gs[:, 32 + tt:33 + tt], s_les[:, tt * 8:(tt + 1) * 8], axis=AX),
                inc="vs" if tt == NT - 1 else None)
        pl.vself()
        pl.op("v", lambda e: e.tensor_add(s_ls[:, 0:16], s_cols[:, 0:16],
                                          s_gs[:, 32:48]), inc="vs")
        pl.vself()
        pl.op("v", lambda e: e.tensor_scalar(s_ls[:, 16:32], s_ls[:, 0:16],
                                             float(GCAP), None, ALU.is_lt), inc="vs")
        pl.vself()
        pl.op("v", lambda e: e.tensor_mul(s_cols[:, 48:64], s_cols[:, 32:48],
                                          s_ls[:, 16:32]), inc="vs")
        v_pk_all = pl.cur("vs")

        tr_evac = {}
        for tt in range(NT):
            psl = P2[0:1, (tt % 2) * 512:(tt % 2) * 512 + 128]
            if tt >= 2:
                pl.w("t", "ss", tr_evac[tt - 2])
            pl.w("t", "vs", v_pk_all if tt == 0 else None)
            tv = pl.op("t", lambda e, o=psl, tt=tt: e.transpose(
                o, s_cols[:, tt:tt + 1], s_id[:, :]), inc="ts")
            pl.w("s", "ts", tv)
            tr_evac[tt] = pl.op("s", lambda e, p=psl,
                                d=arf.ap("rrow", 0, 1, tt * 128, (tt + 1) * 128):
                                e.copy(d, p), inc="ss")
        pl.w("t", "ss", tr_evac[NT - 1])
        pl.w("t", "ss", tr_evac[NT - 2])
        rb_slots = [(P2, 0), (P2, 512), (P3, 0), (P4, 0)]
        for ci, (pt, pc) in enumerate(rb_slots):
            MM(pt[0:128, pc:pc + 512], s_onesf[0:1, :],
               arf.ap("rrow", 0, 1, ci * 512, (ci + 1) * 512), True, True,
               inc="ts" if ci == 3 else None)
        t_rbc = pl.cur("ts")
        pl.w("v", "ts", t_rbc)
        for ci, (pt, pc) in enumerate(rb_slots):
            pl.op("v", lambda e, p=pt[0:128, pc:pc + 512], ci=ci: e.tensor_copy(
                arf.ap("rbcf", 0, 128, ci * 512, (ci + 1) * 512), p),
                inc="vs" if ci == 3 else None)
        v_rbcf = pl.cur("vs")

        for ee in range(E):
            for tt in range(NT):
                col = ee * 16 + tt
                pl.op("v", lambda e, col=col, tt=tt, ee=ee: e.tensor_mul(
                    arb.ap("gsall", 0, 128, col, col + 1),
                    s_cols[:, 48 + tt:49 + tt],
                    s_M1f[:, tt * 8 + ee:tt * 8 + ee + 1]),
                    inc="vs" if col == 127 else None)
        v_gs = pl.cur("vs")
        pl.w("t", "vs", v_gs)
        t_petr = pl.op("t", lambda e: e.transpose(
            P2bf.ap()[0:128, 0:128], arb.ap("gsall", 0, 128, 0, 128),
            s_idb[:, :]), inc="ts")
        pl.w("s", "ts", t_petr)
        s_petr = pl.op("s", lambda e: e.copy(
            arb.ap("petr", 0, 128, 0, 128), P2bf.ap()[0:128, 0:128]), inc="ss")
        pl.w("y", "ss", s_petr)
        pdout = yld(ped.ap(), arb.ap("petr", 0, 128, 0, 128), "dBCa")
        pl.w("y", "dBCa", pdout)
        pl.w("g", "dBCa", pdout)
        pl.w("g", "ts", pl.cur("ts"))
        for ee in range(E):
            eng = "y" if ee % 2 == 0 else "g"
            sem = "dBCa" if ee % 2 == 0 else "dBCb"
            bsrc = bass.AP(ped, ee * 2048, [[0, 128], [1, 2048]])
            bdst = arb.ap("pebc", 0, 128, ee * T, ee * T + 2048)
            pl.op(eng, lambda e, s=bsrc, d=bdst: e.dma_start(out=d, in_=s),
                  inc=sem, by=16)
        pbca = pl.cur("dBCa")
        pbcb = pl.cur("dBCb")

        arf.release("rrow")
        arf.alloc("xlnld", 2 * 1024)
        xln_v = {}
        pl.w("y", "ts", t_rbc)
        for tt in range(2):
            pl.w("y", "dxsa" if tt % 2 == 0 else "dxsb", xln_store[tt])
            xln_v[tt] = yld(arf.ap("xlnld", 0, 128, (tt % 2) * 1024, (tt % 2) * 1024 + 1024),
                            xlnd.ap()[tt * 128:(tt + 1) * 128, :],
                            "dxla" if tt % 2 == 0 else "dxlb")

        cw_built = {}
        cw_read = {}
        ln2_add = {}
        out_v = {}

        def emit_cw_and_mm(tt):
            if tt >= 3:
                pl.w("v", "ts", cw_read[tt - 3])
            pl.vself()
            if tt == 0:
                pl.w("v", "dBCa", pbca)
                pl.w("v", "dBCb", pbcb)
            slot = (tt % 3) * 24 * 128
            for e_ in range(E):
                for ci in range(3):
                    idx = e_ * 3 + ci
                    crows = 128 if ci < 2 else 64
                    cdst = arb.ap("CW", 0, crows, slot + idx * 128,
                                  slot + idx * 128 + 128)
                    pin = arb.ap("pebc", 0, crows, e_ * T + tt * 128,
                                 e_ * T + (tt + 1) * 128)
                    rin = arf.ap("rbcf", 0, crows, tt * 128, (tt + 1) * 128)
                    pl.op("v", lambda e, d=cdst, r=rin, p=pin, ci=ci, cr=crows:
                          e.scalar_tensor_tensor(d, r, s_iotac3[0:cr, ci:ci + 1],
                                                 p, ALU.is_equal, ALU.mult),
                          inc="vs" if idx == 23 else None)
            cw_built[tt] = pl.cur("vs")

            pt = P0 if tt % 2 == 0 else P1
            pl.w("t", "vs", cw_built[tt])
            if tt >= 2:
                pl.w("t", "vs", ln2_add[tt - 2])
            first = True
            for e_ in range(E):
                for ci in range(3):
                    crows = 128 if ci < 2 else 64
                    lhs = arb.ap("CW", 0, crows, slot + (e_ * 3 + ci) * 128,
                                 slot + (e_ * 3 + ci) * 128 + 128)
                    last = (e_ == E - 1 and ci == 2)
                    for nch in range(2):
                        rhs = arb.ap("EO", 0, crows, (e_ * 3 + ci) * 1024 + nch * 512,
                                     (e_ * 3 + ci) * 1024 + nch * 512 + 512)
                        MM(pt[:, nch * 512:nch * 512 + 512], lhs, rhs, first,
                           last, inc="ts" if (last and nch == 1) else None)
                    first = False
            cw_read[tt] = pl.cur("ts")
            return pl.cur("ts")

        ans_mm = {}

        def emit_epilogue(tt):
            pt = P0 if tt % 2 == 0 else P1
            stg = arf.ap("stgA", 0, 128, (tt % 2) * 1024, (tt % 2) * 1024 + 1024)
            stgb = arf.ap("stgB", 0, 128, 0, 1024)
            xl = arf.ap("xlnld", 0, 128, (tt % 2) * 1024, (tt % 2) * 1024 + 1024)
            pl.w("v", "ts", ans_mm[tt])
            pl.w("v", "dxla" if tt % 2 == 0 else "dxlb", xln_v[tt])
            if tt == 0:
                pl.w("v", "dxsa", xln_store[NT - 2])
                pl.w("v", "dxsb", xln_store[NT - 1])
            if tt >= 2:
                pl.w("v", "dOa" if tt % 2 == 0 else "dOb", 16 * (tt // 2))
            pl.vself()
            ln2_add[tt] = pl.op("v", lambda e, d=stgb, p=pt[:, 0:1024], x=xl:
                                e.tensor_add(d, p, x), inc="vs")
            if tt + 2 < NT:
                pl.w("y", "vs", ln2_add[tt])
                pl.w("y", "dxsa" if tt % 2 == 0 else "dxsb", xln_store[tt + 2])
                xln_v[tt + 2] = yld(
                    arf.ap("xlnld", 0, 128, (tt % 2) * 1024, (tt % 2) * 1024 + 1024),
                    xlnd.ap()[(tt + 2) * 128:(tt + 3) * 128, :],
                    "dxla" if tt % 2 == 0 else "dxlb")
            pl.w("s", "vs", ln2_add[tt])
            s_sq2 = pl.op("s", lambda e, o=stg, y=stgb: e.activation(o, y, AF.Square),
                          inc="ss")
            pl.vself()
            pl.op("v", lambda e, y=stgb: e.reduce_sum(s_stat[:, 0:1], y, axis=AX))
            pl.w("v", "ss", s_sq2)
            pl.op("v", lambda e, y=stg: e.reduce_sum(s_stat[:, 4:5], y, axis=AX),
                  inc="vs")
            pl.vself()
            pl.op("v", lambda e: e.tensor_scalar_mul(s_stat[:, 1:2], s_stat[:, 0:1],
                                                     1.0 / D))
            pl.op("v", lambda e: e.tensor_scalar_mul(s_stat[:, 3:4], s_stat[:, 4:5],
                                                     1.0 / D), inc="vs")
            pl.vself()
            pl.op("v", lambda e: e.tensor_mul(s_stat[:, 2:3], s_stat[:, 1:2],
                                              s_stat[:, 1:2]), inc="vs")
            pl.vself()
            pl.op("v", lambda e: e.tensor_sub(s_stat[:, 4:5], s_stat[:, 3:4],
                                              s_stat[:, 2:3]), inc="vs")
            v_var = pl.cur("vs")
            pl.w("s", "vs", v_var)
            s_std = pl.op("s", lambda e: e.activation(s_stat[:, 5:6], s_stat[:, 4:5],
                                                      AF.Sqrt, bias=s_eps[:, 0:1],
                                                      scale=1.0), inc="ss")
            pl.w("v", "ss", s_std)
            pl.op("v", lambda e: e.reciprocal(s_stat[:, 6:7], s_stat[:, 5:6]), inc="vs")
            pl.vself()
            v_out = pl.op("v", lambda e, y=stgb, o=stg: e.tensor_scalar(
                o, y, s_stat[:, 1:2], s_stat[:, 6:7], ALU.subtract, ALU.mult),
                inc="vs")
            oeng = "y" if tt % 2 == 0 else "g"
            pl.w(oeng, "vs", v_out)
            out_v[tt] = pl.op(oeng, lambda e, s=stg, tt=tt: e.dma_start(
                out=out[tt * 128:(tt + 1) * 128, :], in_=s),
                inc="dOa" if tt % 2 == 0 else "dOb", by=16)

        for tt in range(NT):
            ans_mm[tt] = emit_cw_and_mm(tt)
            if tt >= 1:
                emit_epilogue(tt - 1)
        emit_epilogue(NT - 1)

        pl.w("y", "dOa", 16 * (NT // 2))
        pl.w("y", "dOb", 16 * (NT // 2))

        if arb.peak > ARB_N or arf.peak > ARF_N:
            raise MemoryError(f"arena peaks {arb.peak} {arf.peak}")
        blk = EN(nc.Block())

        @blk.tensor
        def _(t):
            for f in pl.ops["t"]:
                f(t)

        @blk.vector
        def _(v):
            for f in pl.ops["v"]:
                f(v)

        @blk.scalar
        def _(s):
            for f in pl.ops["s"]:
                f(s)

        @blk.gpsimd
        def _(g):
            for f in pl.ops["g"]:
                f(g)

        @blk.sync
        def _(y):
            for f in pl.ops["y"]:
                f(y)

    return nc


def make_host_inputs(L0_attn_W_qkv, L0_attn_b_qkv, L0_attn_W_msa, L0_attn_b_msa,
                     L0_ffn_gate_w, L0_ffn_w1, L0_ffn_b1, L0_ffn_w2, L0_ffn_b2):
    W = np.asarray(L0_attn_W_qkv, dtype=np.float32)
    bq = np.asarray(L0_attn_b_qkv, dtype=np.float32)
    wqk = np.concatenate([W[:, h * 192:h * 192 + 64] for h in range(H)]
                         + [W[:, h * 192 + 64:h * 192 + 128] for h in range(H)], axis=1)
    wv = np.concatenate([W[:, h * 192 + 128:h * 192 + 192] for h in range(H)], axis=1)
    bqk = np.concatenate([bq[h * 192:h * 192 + 64] for h in range(H)]
                         + [bq[h * 192 + 64:h * 192 + 128] for h in range(H)])
    bqkc = np.ascontiguousarray(bqk.reshape(16, 128).T)
    bvr = np.concatenate([bq[h * 192 + 128:h * 192 + 192] for h in range(H)])[None, :]
    wmsa = np.asarray(L0_attn_W_msa, dtype=np.float32)
    bmsar = np.asarray(L0_attn_b_msa, dtype=np.float32)[None, :]
    gateT = np.ascontiguousarray(np.asarray(L0_ffn_gate_w, dtype=np.float32).T)
    w1 = np.asarray(L0_ffn_w1, dtype=np.float32)
    b1 = np.asarray(L0_ffn_b1, dtype=np.float32)
    b1c = np.ascontiguousarray(b1.reshape(E, 32, 128).transpose(0, 2, 1))
    w2 = np.asarray(L0_ffn_w2, dtype=np.float32)
    b2 = np.asarray(L0_ffn_b2, dtype=np.float32)
    consts = {
        "c_ident": np.eye(128, dtype=np.float32),
        "c_iotaC": np.broadcast_to(np.arange(CAP, dtype=np.float32)[None, :],
                                   (128, CAP)).copy(),
        "c_iotac3": np.stack([np.arange(128, dtype=np.float32) + 128 * ci
                              for ci in range(3)], axis=1).astype(np.float32),
        "c_ltriu": np.triu(np.ones((128, 128), dtype=np.float32), 1),
        "c_iota8": np.broadcast_to(np.tile(np.arange(8, dtype=np.float32), 16)[None, :],
                                   (128, 128)).copy(),
    }
    shared = {"wqk": np.ascontiguousarray(wqk), "bqkc": bqkc,
              "wv": np.ascontiguousarray(wv),
              "bvr": np.ascontiguousarray(bvr),
              "wmsa": np.ascontiguousarray(wmsa),
              "bmsar": np.ascontiguousarray(bmsar),
              "gateT": gateT, "w1": np.ascontiguousarray(w1), "b1c": b1c,
              "w2": np.ascontiguousarray(w2), "b2": np.ascontiguousarray(b2)}
    shared.update(consts)
    return shared


def kernel(L0_attn_z, L0_attn_W_qkv, L0_attn_b_qkv, L0_attn_W_msa, L0_attn_b_msa,
           L0_ffn_gate_w, L0_ffn_w1, L0_ffn_b1, L0_ffn_w2, L0_ffn_b2):
    z = np.asarray(L0_attn_z, dtype=np.float32)
    shared = make_host_inputs(L0_attn_W_qkv, L0_attn_b_qkv, L0_attn_W_msa,
                              L0_attn_b_msa, L0_ffn_gate_w, L0_ffn_w1, L0_ffn_b1,
                              L0_ffn_w2, L0_ffn_b2)
    nc = build_program()
    in_maps = []
    for c in range(B):
        m = dict(shared)
        m["x0"] = np.ascontiguousarray(z[c])
        pref = np.zeros((E, 1), dtype=np.float32)
        pref[:c, 0] = 1.0
        m["prefix"] = pref
        in_maps.append(m)
    res = run_bass_kernel_spmd(nc, in_maps, core_ids=list(range(B)))
    outs = [np.asarray(res.results[c]["out"], dtype=np.float32) for c in range(B)]
    return np.stack(outs, axis=0)


# version 28 (final)
# speedup vs baseline: 1.0364x; 1.0199x over previous
import contextlib
import numpy as np
import ml_dtypes
import concourse.bass as bass
import concourse.mybir as mybir
from concourse.bass_utils import run_bass_kernel_spmd

F32 = mybir.dt.float32
BF16 = mybir.dt.bfloat16
AX = mybir.AxisListType.X
ALU = mybir.AluOpType
AF = mybir.ActivationFunctionType

B = 8
T = 2048
NT = T // 128
D = 1024
ND = D // 128
H = 16
E = 8
DFF = 4096
CAP = 320
GCAP = 2560
EPS = 1e-5

USE_COLLECTIVE = True
DEBUG_DUMPS = False


class Plan:
    def __init__(self):
        self.ops = {k: [] for k in ("t", "v", "s", "g", "y")}
        self.n = {}
        self.sems = {}

    def w(self, eng, sem, val):
        if val is None or val <= 0:
            return
        h = self.sems[sem]
        self.ops[eng].append(lambda e, h=h, v=val: e.wait_ge(h, v))

    def op(self, eng, fn, inc=None, by=1):
        v, h = None, None
        if inc is not None:
            v = self.n[inc] = self.n.get(inc, 0) + by
            h = self.sems[inc]

        def run(e, fn=fn, h=h, by=by):
            inst = fn(e)
            if h is not None:
                inst.then_inc(h, by)
        self.ops[eng].append(run)
        return v

    def cur(self, sem):
        return self.n.get(sem, 0)

    def vself(self):
        self.w("v", "vs", self.cur("vs"))

    def sself(self):
        self.w("s", "ss", self.cur("ss"))


class Arena:
    def __init__(self, tensor, nelem):
        self.tensor = tensor
        self.free = [(0, nelem)]
        self.used = {}
        self.peak = 0
        self.curn = 0

    def alloc(self, name, elems):
        elems = (elems + 127) & ~127
        for i, (off, ln) in enumerate(self.free):
            if ln >= elems:
                self.used[name] = (off, elems)
                if ln == elems:
                    self.free.pop(i)
                else:
                    self.free[i] = (off + elems, ln - elems)
                self.curn += elems
                self.peak = max(self.peak, self.curn)
                return off
        raise MemoryError(f"arena OOM: {name} ({elems}) free={self.free}")

    def release(self, name):
        off, ln = self.used.pop(name)
        self.curn -= ln
        self.free.append((off, ln))
        self.free.sort()
        m = []
        for o, l in self.free:
            if m and m[-1][0] + m[-1][1] == o:
                m[-1] = (m[-1][0], m[-1][1] + l)
            else:
                m.append((o, l))
        self.free = m

    def ap(self, name, p0, p1, a, b):
        off, ln = self.used[name]
        assert 0 <= a <= b <= ln, (name, a, b, ln)
        return self.tensor[p0:p1, off + a: off + b]


def build_program():
    nc = bass.Bass("TRN2", target_bir_lowering=False)

    x0 = nc.dram_tensor("x0", [T, D], F32, kind="ExternalInput").ap()
    wqk = nc.dram_tensor("wqk", [D, 2048], F32, kind="ExternalInput").ap()
    bqkc = nc.dram_tensor("bqkc", [128, 16], F32, kind="ExternalInput").ap()
    wv = nc.dram_tensor("wv", [D, D], F32, kind="ExternalInput").ap()
    bvr = nc.dram_tensor("bvr", [1, D], F32, kind="ExternalInput").ap()
    wmsa = nc.dram_tensor("wmsa", [D, D], F32, kind="ExternalInput").ap()
    bmsar = nc.dram_tensor("bmsar", [1, D], F32, kind="ExternalInput").ap()
    gateT = nc.dram_tensor("gateT", [D, E], F32, kind="ExternalInput").ap()
    w1 = nc.dram_tensor("w1", [E, D, DFF], F32, kind="ExternalInput").ap()
    b1c = nc.dram_tensor("b1c", [E, 128, 32], F32, kind="ExternalInput").ap()
    w2 = nc.dram_tensor("w2", [E, DFF, D], F32, kind="ExternalInput").ap()
    b2 = nc.dram_tensor("b2", [E, D], F32, kind="ExternalInput").ap()
    prefix = nc.dram_tensor("prefix", [E, 1], F32, kind="ExternalInput").ap()
    c_ident = nc.dram_tensor("c_ident", [128, 128], F32, kind="ExternalInput").ap()
    c_iotaC = nc.dram_tensor("c_iotaC", [128, CAP], F32, kind="ExternalInput").ap()
    c_iotac3 = nc.dram_tensor("c_iotac3", [128, 3], F32, kind="ExternalInput").ap()
    c_ltriu = nc.dram_tensor("c_ltriu", [128, 128], F32, kind="ExternalInput").ap()
    c_iota8 = nc.dram_tensor("c_iota8", [128, 128], F32, kind="ExternalInput").ap()
    out = nc.dram_tensor("out", [T, D], F32, kind="ExternalOutput").ap()
    xlnd = nc.dram_tensor("xlnd", [T, D], F32)
    rbd = nc.dram_tensor("rbd", [2, 1024], BF16)
    rbd2 = nc.dram_tensor("rbd2", [2, 1024], F32)
    ped = nc.dram_tensor("ped", [128, 128], BF16)
    cc_in = nc.dram_tensor("cc_in", [1, E], F32)
    cc_out = nc.dram_tensor("cc_out", [E, E], F32)
    if DEBUG_DUMPS:
        dbg_qkT = nc.dram_tensor("dbg_qkT", [128, 16 * T], BF16)
        dbg_v = nc.dram_tensor("dbg_v", [128, NT * H * 65], BF16)
        dbg_x0T = nc.dram_tensor("dbg_x0T", [128, ND * T], BF16)

    pl = Plan()

    with contextlib.ExitStack() as ctx:
        EN = ctx.enter_context
        ARB_N = 82432
        ARF_N = 7168
        arb_t = EN(nc.sbuf_tensor("arb", [128, ARB_N], BF16))
        arf_t = EN(nc.sbuf_tensor("arf", [128, ARF_N], F32))
        arb = Arena(arb_t, ARB_N)
        arf = Arena(arf_t, ARF_N)

        s_id = EN(nc.sbuf_tensor("s_id", [128, 128], F32))
        s_iotaC = EN(nc.sbuf_tensor("s_iotaC", [128, CAP], F32))
        s_iotac3 = EN(nc.sbuf_tensor("s_iotac3", [128, 3], F32))
        s_iota8 = EN(nc.sbuf_tensor("s_iota8", [128, 128], F32))
        s_ltriuf = EN(nc.sbuf_tensor("s_ltriuf", [128, 128], F32))
        s_onesb = EN(nc.sbuf_tensor("s_onesb", [128, 128], BF16))
        s_onesf = EN(nc.sbuf_tensor("s_onesf", [1, 128], F32))
        s_onescf = EN(nc.sbuf_tensor("s_onescf", [128, 1], F32))
        s_bqkc = EN(nc.sbuf_tensor("s_bqkc", [128, 16], F32))
        s_bvr = EN(nc.sbuf_tensor("s_bvr", [1, D], BF16))
        s_bmsar = EN(nc.sbuf_tensor("s_bmsar", [1, D], BF16))
        s_gateT = EN(nc.sbuf_tensor("s_gateT", [128, ND * E], F32))
        s_b1c = EN(nc.sbuf_tensor("s_b1c", [128, E * 32], F32))
        s_b2bc = EN(nc.sbuf_tensor("s_b2bc", [128, D], F32))
        s_prefix = EN(nc.sbuf_tensor("s_prefix", [E, 1], F32))
        s_ag = EN(nc.sbuf_tensor("s_ag", [E, E], F32))
        s_toff = EN(nc.sbuf_tensor("s_toff", [1, E], F32))
        s_off = EN(nc.sbuf_tensor("s_off", [1, E], F32))
        s_offbc = EN(nc.sbuf_tensor("s_offbc", [128, E], F32))
        s_mx = EN(nc.sbuf_tensor("s_mx", [1, 16], F32))
        s_mxb = EN(nc.sbuf_tensor("s_mxb", [1, 1], BF16))
        s_negc = EN(nc.sbuf_tensor("s_negc", [128, 1], F32))
        s_M1 = EN(nc.sbuf_tensor("s_M1", [128, NT * E], BF16))
        s_M1f = EN(nc.sbuf_tensor("s_M1f", [128, NT * E], F32))
        s_ranks = EN(nc.sbuf_tensor("s_ranks", [128, NT * E], F32))
        s_cols = EN(nc.sbuf_tensor("s_cols", [128, NT * 4], F32))
        s_gs = EN(nc.sbuf_tensor("s_gs", [128, NT * 3], F32))
        s_lgtall = EN(nc.sbuf_tensor("s_lgtall", [128, NT * E], F32))
        s_ls = EN(nc.sbuf_tensor("s_ls", [128, NT * E], F32))
        s_les = EN(nc.sbuf_tensor("s_les", [128, NT * E], F32))
        s_gateTb = EN(nc.sbuf_tensor("s_gateTb", [128, 2 * ND * E], BF16))
        s_idb = EN(nc.sbuf_tensor("s_idb", [128, 128], BF16))
        s_stat = EN(nc.sbuf_tensor("s_stat", [128, 8], F32))
        s_pec = EN(nc.sbuf_tensor("s_pec", [128, 1], F32))
        s_eps = EN(nc.sbuf_tensor("s_eps", [128, 1], F32))
        s_cnt = EN(nc.sbuf_tensor("s_cnt", [1, E], F32))

        P0 = EN(nc.psum_tensor("P0", [128, 1024], F32))
        P1 = EN(nc.psum_tensor("P1", [128, 1024], F32))
        P2 = EN(nc.psum_tensor("P2", [128, 1024], F32))
        P3 = EN(nc.psum_tensor("P3", [128, 512], F32))
        P4 = EN(nc.psum_tensor("P4", [128, 512], F32))

        for nm in ("ts", "vs", "ss", "cs", "gs", "dC", "dT", "dB", "dXC",
                   "dQKa", "dQKb", "dWV", "dMSA", "dW1a", "dW1b", "dW1c", "dW2a",
                   "dW2b", "dW2c", "dSTG", "dSTGb", "dC2", "dB2", "dB2b", "dAG", "dx0c", "dx0d", "dx0a", "dx0b", "dxsa", "dxsb",
                   "dxla", "dxlb", "dOa", "dOb", "dBCa", "dBCb"):
            pl.sems[nm] = EN(nc.semaphore(nm))

        def yld(dst, src, sem):
            return pl.op("y", lambda e, d=dst, s=src: e.dma_start(out=d, in_=s),
                         inc=sem, by=16)

        def gld(dst, src, sem):
            return pl.op("g", lambda e, d=dst, s=src: e.dma_start(out=d, in_=s),
                         inc=sem, by=16)

        def MM(o, l, r, st, sp, inc=None):
            return pl.op("t", lambda e, o=o, l=l, r=r, st=st, sp=sp:
                         e.matmul(o, l, r, start=st, stop=sp), inc=inc)

        v_id = yld(s_id[:, :], c_ident, "dC2")

        arb.alloc("x0T", ND * T)
        arf.alloc("x0ld", 4 * 1024)
        X0SEM = ("dx0a", "dx0b", "dx0c", "dx0d")

        def x0_load(tt):
            ld = yld if (tt % 4) < 2 else gld
            return ld(
                arf.ap("x0ld", 0, 128, (tt % 4) * 1024, (tt % 4) * 1024 + 1024),
                x0[tt * 128:(tt + 1) * 128, :], X0SEM[tt % 4])
        x0ld_v = {}
        for tt in range(4):
            x0ld_v[tt] = x0_load(tt)

        yld(s_iotaC[:, :], c_iotaC, "dC")
        yld(s_iotac3[:, :], c_iotac3, "dC")
        yld(s_iota8[:, :], c_iota8, "dC")
        yld(s_ltriuf[:, :], c_ltriu, "dC")
        yld(s_bqkc[:, :], bqkc, "dC")
        yld(s_gateT.ap().rearrange("p (dj e) -> p dj e", e=E),
            gateT.rearrange("(dj p) e -> p dj e", p=128), "dC")
        yld(s_b1c.ap().rearrange("p (e m) -> p e m", m=32),
            b1c.rearrange("e p m -> p e m"), "dC")
        v_consts = yld(s_prefix[:, :], prefix, "dC")

        gld(s_bvr[:, :], bvr, "dB")
        gld(s_bmsar[:, :], bmsar, "dB")

        x0t_evac = {}
        pl.w("t", "dC2", v_id)
        for tt in range(NT):
            pl.w("t", X0SEM[tt % 4], x0ld_v[tt])
            for dj in range(ND):
                k2 = tt * ND + dj
                psl = (P3 if k2 % 2 == 0 else P4)[0:128, 0:128]
                if k2 >= 2:
                    pl.w("t", "vs", x0t_evac[k2 - 2])
                src_ap = arf.ap("x0ld", 0, 128, (tt % 4) * 1024 + dj * 128,
                                (tt % 4) * 1024 + dj * 128 + 128)
                tv = pl.op("t", lambda e, o=psl, s=src_ap:
                           e.transpose(o, s, s_id[:, :]), inc="ts")
                pl.w("v", "ts", tv)
                xdst = arb.ap("x0T", 0, 128, dj * T + tt * 128, dj * T + tt * 128 + 128)
                x0t_evac[k2] = pl.op("v", lambda e, d=xdst, p=psl:
                                     e.tensor_copy(d, p), inc="vs")
            if tt + 4 < NT:
                pl.w("y" if (tt % 4) < 2 else "g", "ts", pl.cur("ts"))
                x0ld_v[tt + 4] = x0_load(tt + 4)
        v_x0T = pl.cur("vs")
        ts_x0t = pl.cur("ts")
        arf.release("x0ld")

        pl.w("v", "dC", v_consts)
        pl.op("v", lambda e: e.memset(s_onesb[:, :], 1.0))
        pl.op("v", lambda e: e.memset(s_onesf[:, :], 1.0))
        pl.op("v", lambda e: e.memset(s_onescf[:, :], 1.0))
        pl.op("v", lambda e: e.memset(s_eps[:, :], EPS))
        pl.op("v", lambda e: e.memset(s_toff[:, :], 0.0))
        pl.op("v", lambda e: e.tensor_copy(s_idb[:, :], s_id[:, :]))
        pl.op("v", lambda e: e.tensor_copy(s_gateTb[:, 0:64], s_gateT[:, :]), inc="vs")
        pl.vself()
        pl.op("v", lambda e: e.tensor_sub(s_ls[:, 0:64], s_gateT[:, :],
                                          s_gateTb[:, 0:64]), inc="vs")
        pl.vself()
        v_prep = pl.op("v", lambda e: e.tensor_copy(s_gateTb[:, 64:128],
                                                    s_ls[:, 0:64]), inc="vs")

        arb.alloc("qkT", 16 * T)
        arb.alloc("v", NT * H * 65)
        arb.alloc("wqkbuf", 2 * ND * 128)
        arb.alloc("wvbuf", ND * 1024)

        vones_ap = arb.ap("v", 0, 128, 0, NT * H * 65).rearrange(
            "p (t c) -> p t c", c=65)[:, :, 64:65]
        v_vones = pl.op("v", lambda e, a=vones_ap: e.memset(a, 1.0), inc="vs")

        ct_order = []
        for j in range(8):
            ct_order += [j, 8 + j]

        def plan_wqk_load(i):
            dst = arb.ap("wqkbuf", 0, 128, (i % 2) * 1024, (i % 2) * 1024 + 1024)
            dst = dst.rearrange("p (dj c) -> p dj c", c=128)
            ct = ct_order[i]
            src = wqk[:, ct * 128:(ct + 1) * 128].rearrange("(dj p) c -> p dj c", p=128)
            sem = "dQKa" if i % 2 == 0 else "dQKb"
            return (sem, gld(dst, src, sem))

        wqk_v = [plan_wqk_load(0), plan_wqk_load(1)]
        wv_v = []
        for dj in range(ND):
            wv_v.append(gld(arb.ap("wvbuf", 0, 128, dj * 1024, (dj + 1) * 1024),
                            wv[dj * 128:(dj + 1) * 128, :], "dWV"))

        qk_slotsets = [[(P0, 0), (P0, 512), (P1, 0), (P1, 512)],
                       [(P2, 0), (P2, 512), (P3, 0), (P4, 0)]]
        qk_evac = {}
        qk_mm_done = {}
        pl.w("t", "vs", v_x0T)
        for i, ct in enumerate(ct_order):
            pl.w("t", wqk_v[i][0], wqk_v[i][1])
            sset = qk_slotsets[i % 2]
            if i >= 2:
                for tch in range(4):
                    pl.w("t", "vs", qk_evac[(i - 2) * 4 + tch])
            for dj in range(ND):
                lhs = arb.ap("wqkbuf", 0, 128, (i % 2) * 1024 + dj * 128,
                             (i % 2) * 1024 + dj * 128 + 128)
                for tch in range(4):
                    pt, pc = sset[tch]
                    rhs = arb.ap("x0T", 0, 128, dj * T + tch * 512,
                                 dj * T + tch * 512 + 512)
                    MM(pt[:, pc:pc + 512], lhs, rhs, dj == 0, dj == ND - 1,
                       inc="ts" if (dj == ND - 1 and tch == 3) else None)
            v_mm = pl.cur("ts")
            pl.w("v", "ts", v_mm)
            for tch in range(4):
                pt, pc = sset[tch]
                dst = arb.ap("qkT", 0, 128, ct * T + tch * 512, ct * T + tch * 512 + 512)
                qk_evac[i * 4 + tch] = pl.op(
                    "v", lambda e, d=dst, p=pt[:, pc:pc + 512], b=s_bqkc[:, ct:ct + 1]:
                    e.tensor_scalar(d, p, b, None, ALU.add), inc="vs")
            qk_mm_done[i] = pl.cur("ts")
            if i + 2 < 16:
                pl.w("g", "ts", qk_mm_done[i])
                wqk_v.append(plan_wqk_load(i + 2))

        pl.w("t", "dWV", wv_v[-1])
        pl.w("t", "dB", 32)
        pl.w("t", "vs", v_vones)
        for tch in range(4):
            pl.w("t", "vs", qk_evac[14 * 4 + tch])
        v_evac = {}
        for tt in range(NT):
            vpt = P0 if tt % 2 == 0 else P1
            if tt >= 2:
                pl.w("t", "vs", v_evac[tt - 2])
            for dj in range(ND):
                lhs = arb.ap("x0T", 0, 128, dj * T + tt * 128, dj * T + tt * 128 + 128)
                for nch in range(2):
                    rhs = arb.ap("wvbuf", 0, 128, dj * 1024 + nch * 512,
                                 dj * 1024 + nch * 512 + 512)
                    MM(vpt[:, nch * 512:nch * 512 + 512], lhs, rhs, dj == 0, False)
            for nch in range(2):
                MM(vpt[:, nch * 512:nch * 512 + 512], s_onesb[0:1, :],
                   s_bvr[0:1, nch * 512:nch * 512 + 512], False, True,
                   inc="ts" if nch == 1 else None)
            v_mm = pl.cur("ts")
            pl.w("v", "ts", v_mm)
            vsrc = vpt.ap().rearrange("p (h c) -> p h c", c=64)
            vdst = arb.ap("v", 0, 128, tt * H * 65, (tt + 1) * H * 65).rearrange(
                "p (h c) -> p h c", c=65)[:, :, 0:64]
            v_evac[tt] = pl.op("v", lambda e, d=vdst, s=vsrc:
                               e.tensor_copy(d, s), inc="vs")
        v_vdone = pl.cur("vs")
        ts_vdone = pl.cur("ts")
        if DEBUG_DUMPS:
            pl.w("y", "ts", ts_vdone)
            pl.w("y", "vs", v_vdone)
            yld2 = lambda d, s: pl.op("y", lambda e, d=d, s=s: e.dma_start(out=d, in_=s),
                                      inc="dC", by=16)
            yld2(dbg_x0T.ap(), arb.ap("x0T", 0, 128, 0, ND * T))
            yld2(dbg_qkT.ap(), arb.ap("qkT", 0, 128, 0, 16 * T))
            yld2(dbg_v.ap(), arb.ap("v", 0, 128, 0, NT * H * 65))
            pl.w("v", "dC", pl.cur("dC"))
        arb.release("x0T")
        arb.release("wqkbuf")
        arb.release("wvbuf")

        arb.alloc("sq", 2 * T)
        norm_done = {}
        for grp, base in (("q", 0), ("k", 8)):
            for j in range(8):
                buf = j % 2
                if j >= 2:
                    pl.w("v", "ts", norm_done[(grp, j - 2)])
                elif grp == "k":
                    pl.w("v", "ts", norm_done[("q", 6 + j)])
                src = arb.ap("qkT", 0, 128, (base + j) * T, (base + j) * T + T)
                dst = arb.ap("sq", 0, 128, buf * T, buf * T + T)
                pl.w("v", "vs", qk_evac[ct_order.index(base + j) * 4 + 3])
                sqv = pl.op("v", lambda e, d=dst, s=src: e.tensor_mul(d, s, s), inc="vs")
                pl.w("t", "vs", sqv)
                chunks = ([(P0, 0), (P0, 512), (P1, 0), (P1, 512)] if grp == "q"
                          else [(P2, 0), (P2, 512), (P3, 0), (P4, 0)])
                for ci, (pt, pc) in enumerate(chunks):
                    rhs = arb.ap("sq", 0, 128, buf * T + ci * 512, buf * T + ci * 512 + 512)
                    MM(pt[0:1, pc:pc + 512], s_onesb[:, 0:1], rhs, j == 0, j == 7,
                       inc="ts" if ci == 3 else None)
                norm_done[(grp, j)] = pl.cur("ts")
        arb.release("sq")
        pl.w("v", "ts", norm_done[("k", 7)])
        pl.op("v", lambda e: e.reduce_max(s_mx[0:1, 0:1], P0[0:1, 0:1024], axis=AX))
        pl.op("v", lambda e: e.reduce_max(s_mx[0:1, 1:2], P1[0:1, 0:1024], axis=AX))
        pl.op("v", lambda e: e.reduce_max(s_mx[0:1, 2:3], P2[0:1, 0:1024], axis=AX))
        pl.op("v", lambda e: e.reduce_max(s_mx[0:1, 3:4], P3[0:1, 0:512], axis=AX))
        pl.op("v", lambda e: e.reduce_max(s_mx[0:1, 4:5], P4[0:1, 0:512], axis=AX),
              inc="vs")
        pl.vself()
        pl.op("v", lambda e: e.tensor_max(s_mx[0:1, 5:6], s_mx[0:1, 0:1], s_mx[0:1, 1:2]))
        pl.op("v", lambda e: e.tensor_max(s_mx[0:1, 6:7], s_mx[0:1, 3:4], s_mx[0:1, 4:5]),
              inc="vs")
        pl.vself()
        pl.op("v", lambda e: e.tensor_max(s_mx[0:1, 7:8], s_mx[0:1, 6:7], s_mx[0:1, 2:3]),
              inc="vs")
        pl.vself()
        pl.op("v", lambda e: e.tensor_mul(s_mx[0:1, 8:9], s_mx[0:1, 5:6], s_mx[0:1, 7:8]),
              inc="vs")
        v_prod = pl.cur("vs")
        pl.w("s", "vs", v_prod)
        s_sqr = pl.op("s", lambda e: e.activation(s_mx[0:1, 9:10], s_mx[0:1, 8:9], AF.Sqrt),
                      inc="ss")
        pl.w("v", "ss", s_sqr)
        pl.op("v", lambda e: e.tensor_scalar_mul(s_mx[0:1, 10:11], s_mx[0:1, 9:10],
                                                 -0.125), inc="vs")
        pl.vself()
        pl.op("v", lambda e: e.tensor_copy(s_mxb[0:1, 0:1], s_mx[0:1, 10:11]), inc="vs")
        v_mxb = pl.cur("vs")
        pl.w("t", "vs", v_mxb)
        t_negbc = MM(P3[0:128, 0:1], s_onesb[0:1, :], s_mxb[0:1, 0:1], True, True,
                     inc="ts")
        pl.w("v", "ts", t_negbc)
        v_negc = pl.op("v", lambda e: e.tensor_copy(s_negc[:, 0:1], P3[0:128, 0:1]),
                       inc="vs")

        arb.alloc("attnT", ND * T)
        arb.alloc("exp", 6 * 1024)
        arb.alloc("rbc", 2 * 1024)
        arb.alloc("recip", 2 * 128)

        w1_v = {}
        w2_v = {}

        def load_w1(e_, mg, buf, wait_ts=None):
            pl.w("g", "ts", wait_ts)
            dst = arb.ap("w1buf", 0, 128, buf * ND * 512, (buf + 1) * ND * 512)
            dst = dst.rearrange("p (dj h) -> p dj h", h=512)
            src = w1[e_, :, mg * 512:(mg + 1) * 512].rearrange("(dj p) h -> p dj h", p=128)
            sem = ("dW1a", "dW1b", "dW1c")[buf]
            w1_v[(e_, mg)] = (sem, gld(dst, src, sem))

        def load_w2(e_, kg, buf, wait_ts=None):
            pl.w("g", "ts", wait_ts)
            dst = arb.ap("w2buf", 0, 128, buf * 2048, (buf + 1) * 2048)
            dst = dst.rearrange("p (kj d) -> p kj d", d=1024)
            srcw = w2[e_, kg * 256:(kg + 1) * 256, :].rearrange("(kj p) d -> p kj d",
                                                                p=128)
            sem = ("dW2a", "dW2b", "dW2c")[buf]
            w2_v[(e_, kg)] = (sem, gld(dst, srcw, sem))

        arf.alloc("rsrow", 2 * 1024)
        arf.alloc("rs8", 2 * 128)
        exp_last = {}
        expv = {}
        evac_par = {}
        first_exp = [True]

        def emit_S2(slot, h, qb, kt):
            bp = 64 * slot
            pt = P1 if slot else P0
            if slot in exp_last:
                pl.w("t", "ss", exp_last[slot])
            for nch in range(2):
                lhs = arb.ap("qkT", bp, bp + 64, (8 + h // 2) * T + kt * 128,
                             (8 + h // 2) * T + kt * 128 + 128)
                rhs = arb.ap("qkT", bp, bp + 64,
                             (h // 2) * T + qb * 1024 + nch * 512,
                             (h // 2) * T + qb * 1024 + nch * 512 + 512)
                MM(pt[:, nch * 512:nch * 512 + 512], lhs, rhs, True, True,
                   inc="ts" if nch == 1 else None)
            return pl.cur("ts")

        def emit_exp2(slot, smm, kt):
            pt = P1 if slot else P0
            ebuf = 3 * slot + (kt % 3)
            edst = arb.ap("exp", 0, 128, ebuf * 1024, ebuf * 1024 + 1024)
            pl.w("s", "ts", smm)
            if first_exp[0]:
                pl.w("s", "vs", v_negc)
                first_exp[0] = False
            exp_last[slot] = pl.op(
                "s", lambda e, d=edst, p=pt[:, 0:1024]:
                e.activation(d, p, AF.Exp, bias=s_negc[:, 0:1], scale=0.125),
                inc="ss")
            expv[(slot, kt)] = exp_last[slot]

        def emit_PV2(slot, h, par, kt, flush):
            if flush:
                pl.w("t", "ss", expv[(slot, kt)])
            if kt == 0:
                pl.w("t", "vs", evac_par.get(par, v_vdone))
            ebuf = 3 * slot + (kt % 3)
            lhsv = arb.ap("v", 0, 128, kt * H * 65 + h * 65,
                          kt * H * 65 + h * 65 + 65)
            for nch in range(2):
                rhs = arb.ap("exp", 0, 128, ebuf * 1024 + nch * 512,
                             ebuf * 1024 + nch * 512 + 512)
                out = (P2[0:65, nch * 512:nch * 512 + 512] if par == 0
                       else (P3 if nch == 0 else P4)[0:65, 0:512])
                MM(out, lhsv, rhs, kt == 0, kt == NT - 1,
                   inc="ts" if (kt == NT - 1 and nch == 1) else None)
            return pl.cur("ts") if kt == NT - 1 else None

        def emit_tails_pair(A, B):
            chains = []
            for idx, (h, qb, par, pvd) in enumerate((A, B)):
                chains.append((h, qb, par, pvd, "y" if par == 0 else "g",
                               "dBCa" if par == 0 else "dBCb"))
            vrs, dmae, vrc, vbcv = {}, {}, {}, {}
            for (h, qb, par, pvd, eng, sem) in sorted(chains, key=lambda c: -c[2]):
                rsr = arf.ap("rsrow", 0, 1, par * 1024, par * 1024 + 1024)
                pl.w("v", "ts", pvd)
                pl.vself()
                if par == 0:
                    vrs[par] = pl.op("v", lambda e, d=rsr:
                                     e.tensor_copy(d, P2[64:65, 0:1024]), inc="vs")
                else:
                    rs0 = arf.ap("rsrow", 0, 1, par * 1024, par * 1024 + 512)
                    rs1 = arf.ap("rsrow", 0, 1, par * 1024 + 512, par * 1024 + 1024)
                    pl.op("v", lambda e, d=rs0: e.tensor_copy(d, P3[64:65, 0:512]))
                    vrs[par] = pl.op("v", lambda e, d=rs1:
                                     e.tensor_copy(d, P4[64:65, 0:512]), inc="vs")
            for (h, qb, par, pvd, eng, sem) in chains:
                rsr = arf.ap("rsrow", 0, 1, par * 1024, par * 1024 + 1024)
                r8src = rsr.rearrange("p (j c) -> p j c", j=8)
                rs8 = arf.ap("rs8", 0, 8, par * 128, par * 128 + 128)
                r8dst = rs8.rearrange("p (x c) -> p x c", x=1)
                pl.w(eng, "vs", vrs[par])
                pl.op(eng, lambda e, s=r8src, d=r8dst: e.dma_start(out=d, in_=s),
                      inc=sem, by=16)
                dmae[par] = pl.cur(sem)

            def _rc(e, d, p):
                with nc.allow_low_precision(reason="bf16 softmax normalizer"):
                    return e.reciprocal(d, p)
            for (h, qb, par, pvd, eng, sem) in sorted(chains, key=lambda c: c[2]):
                rs8 = arf.ap("rs8", 0, 8, par * 128, par * 128 + 128)
                rc8 = arb.ap("recip", 0, 8, par * 128, par * 128 + 128)
                pl.w("v", sem, dmae[par])
                pl.vself()
                vrc[par] = pl.op("v", lambda e, d=rc8, s=rs8, f=_rc: f(e, d, s),
                                 inc="vs")
            for (h, qb, par, pvd, eng, sem) in chains:
                rc8 = arb.ap("recip", 0, 8, par * 128, par * 128 + 128)
                pl.w(eng, "vs", vrc[par])
                r8dst = bass.AP(rbd, par * 1024, [[128, 8], [1, 128]])
                pl.op(eng, lambda e, s=rc8, d=r8dst: e.dma_start(out=d, in_=s),
                      inc=sem, by=16)
                pl.w(eng, sem, pl.cur(sem))
                bsrc = bass.AP(rbd, par * 1024, [[0, 64], [1, 1024]])
                rbcd = arb.ap("rbc", 0, 64, par * 1024, par * 1024 + 1024)
                pl.op(eng, lambda e, s=bsrc, d=rbcd: e.dma_start(out=d, in_=s),
                      inc=sem, by=16)
                vbcv[par] = pl.cur(sem)
            for (h, qb, par, pvd, eng, sem) in sorted(chains, key=lambda c: c[2]):
                bp = (h % 2) * 64
                rbcd = arb.ap("rbc", 0, 64, par * 1024, par * 1024 + 1024)
                pl.w("v", sem, vbcv[par])
                adst = arb.ap("attnT", bp, bp + 64, (h // 2) * T + qb * 1024,
                              (h // 2) * T + qb * 1024 + 1024)
                if par == 0:
                    evac_par[par] = pl.op(
                        "v", lambda e, d=adst, r=rbcd: e.tensor_mul(
                            d, P2[0:64, 0:1024], r), inc="vs")
                else:
                    rb0 = arb.ap("rbc", 0, 64, par * 1024, par * 1024 + 512)
                    rb1 = arb.ap("rbc", 0, 64, par * 1024 + 512, par * 1024 + 1024)
                    ad0 = arb.ap("attnT", bp, bp + 64, (h // 2) * T + qb * 1024,
                                 (h // 2) * T + qb * 1024 + 512)
                    ad1 = arb.ap("attnT", bp, bp + 64,
                                 (h // 2) * T + qb * 1024 + 512,
                                 (h // 2) * T + qb * 1024 + 1024)
                    pl.op("v", lambda e, d=ad0, r=rb0: e.tensor_mul(
                        d, P3[0:64, 0:512], r))
                    evac_par[par] = pl.op(
                        "v", lambda e, d=ad1, r=rb1: e.tensor_mul(
                            d, P4[0:64, 0:512], r), inc="vs")

        WMSA_EARLY = 8
        arb.alloc("wmsabA", WMSA_EARLY * 1024)

        def wmsab_ap(dj, a, b):
            if dj < WMSA_EARLY:
                return arb.ap("wmsabA", 0, 128, dj * 1024 + a, dj * 1024 + b)
            d2 = dj - WMSA_EARLY
            return arb.ap("wmsabB", 0, 128, d2 * 1024 + a, d2 * 1024 + b)
        wmsa_v = []
        pl.w("g", "ts", pl.cur("ts"))
        pl.w("g", "vs", pl.cur("vs"))
        pl.w("g", "ss", pl.cur("ss"))
        for dj in range(WMSA_EARLY):
            wmsa_v.append(gld(wmsab_ap(dj, 0, 1024),
                              wmsa[dj * 128:(dj + 1) * 128, :], "dMSA"))

        for jp in range(ND):
            for sub in range(2):
                hA, qbA = 2 * jp, sub
                hB, qbB = 2 * jp + 1, 1 - sub
                parA, parB = qbA, qbB
                pvo = [(0, hA, parA), (1, hB, parB)]
                pvo.sort(key=lambda c: c[2])
                for kt in range(NT):
                    if kt >= 3:
                        for (sl, hh, pp) in pvo:
                            emit_PV2(sl, hh, pp, kt - 3, False)
                    if kt == NT - 1:
                        for (sl, hh, pp) in pvo:
                            emit_PV2(sl, hh, pp, NT - 3, False)
                    sA = emit_S2(0, hA, qbA, kt)
                    sB = emit_S2(1, hB, qbB, kt)
                    emit_exp2(0, sA, kt)
                    emit_exp2(1, sB, kt)
                for (sl, hh, pp) in pvo:
                    emit_PV2(sl, hh, pp, NT - 2, False)
                pvd = {}
                for (sl, hh, pp) in pvo:
                    pvd[sl] = emit_PV2(sl, hh, pp, NT - 1, True)
                pvdA, pvdB = pvd[0], pvd[1]
                emit_tails_pair((hA, qbA, parA, pvdA), (hB, qbB, parB, pvdB))
        v_attn_all = pl.cur("vs")
        ts_attn_all = pl.cur("ts")
        arb.release("qkT")
        arb.release("v")
        arb.release("exp")
        arb.release("rbc")
        arb.release("recip")

        arb.alloc("xlnb", NT * 1024)
        arb.alloc("w1buf", 3 * ND * 512)
        arb.alloc("w2buf", 3 * 2048)
        if ND > WMSA_EARLY:
            arb.alloc("wmsabB", (ND - WMSA_EARLY) * 1024)
        arf.release("rsrow")
        arf.release("rs8")
        arf.alloc("x0st", 2 * 1024)
        arf.alloc("stgA", 2 * 1024)
        arf.alloc("stgB", 1024)
        arf.alloc("lgT", 2048)

        pl.w("g", "vs", v_attn_all)
        pl.w("g", "ts", ts_attn_all)
        for dj in range(WMSA_EARLY, ND):
            wmsa_v.append(gld(wmsab_ap(dj, 0, 1024),
                              wmsa[dj * 128:(dj + 1) * 128, :], "dMSA"))
        load_w1(0, 0, 0)
        load_w1(0, 1, 1)
        load_w1(0, 2, 2)
        load_w2(0, 0, 0)
        load_w2(0, 1, 1)
        load_w2(0, 2, 2)

        x0_v = {}
        pl.w("y", "ts", ts_x0t)
        pl.w("y", "dBCa", pl.cur("dBCa"))
        pl.w("y", "dBCb", pl.cur("dBCb"))
        for tt in range(2):
            x0_v[tt] = yld(arf.ap("x0st", 0, 128, (tt % 2) * 1024, (tt % 2) * 1024 + 1024),
                           x0[tt * 128:(tt + 1) * 128, :],
                           "dx0a" if tt % 2 == 0 else "dx0b")

        ln1_add = {}
        ln1_done = {}
        xln_store = {}
        pl.w("t", "dMSA", wmsa_v[-1])
        pl.w("t", "vs", v_attn_all)
        for tt in range(NT):
            pt = P0 if tt % 2 == 0 else P1
            if tt >= 2:
                pl.w("t", "vs", ln1_add[tt - 2])
            for nch in range(2):
                for dj in range(ND):
                    lhs = arb.ap("attnT", 0, 128, dj * T + tt * 128, dj * T + tt * 128 + 128)
                    rhs = wmsab_ap(dj, nch * 512, nch * 512 + 512)
                    MM(pt[:, nch * 512:nch * 512 + 512], lhs, rhs, dj == 0, False)
                MM(pt[:, nch * 512:nch * 512 + 512], s_onesb[0:1, :],
                   s_bmsar[0:1, nch * 512:nch * 512 + 512], False, True,
                   inc="ts" if nch == 1 else None)
            msa_mm = pl.cur("ts")

            stg = arf.ap("stgA", 0, 128, (tt % 2) * 1024, (tt % 2) * 1024 + 1024)
            stgb = arf.ap("stgB", 0, 128, 0, 1024)
            x0s = arf.ap("x0st", 0, 128, (tt % 2) * 1024, (tt % 2) * 1024 + 1024)
            pl.w("v", "ts", msa_mm)
            pl.w("v", "dx0a" if tt % 2 == 0 else "dx0b", x0_v[tt])
            if tt >= 2:
                pl.w("v", "dxsa" if tt % 2 == 0 else "dxsb", xln_store[tt - 2])
            pl.vself()
            ln1_add[tt] = pl.op("v", lambda e, d=stgb, p=pt[:, 0:1024], x=x0s:
                                e.tensor_add(d, p, x), inc="vs")
            if tt + 2 < NT:
                pl.w("y", "vs", ln1_add[tt])
                x0_v[tt + 2] = yld(
                    arf.ap("x0st", 0, 128, (tt % 2) * 1024, (tt % 2) * 1024 + 1024),
                    x0[(tt + 2) * 128:(tt + 3) * 128, :],
                    "dx0a" if tt % 2 == 0 else "dx0b")
            pl.w("s", "vs", ln1_add[tt])
            s_sq2 = pl.op("s", lambda e, o=stg, y=stgb: e.activation(o, y, AF.Square),
                          inc="ss")
            pl.vself()
            pl.op("v", lambda e, y=stgb: e.reduce_sum(s_stat[:, 0:1], y, axis=AX))
            pl.w("v", "ss", s_sq2)
            pl.op("v", lambda e, y=stg: e.reduce_sum(s_stat[:, 4:5], y, axis=AX),
                  inc="vs")
            pl.vself()
            pl.op("v", lambda e: e.tensor_scalar_mul(s_stat[:, 1:2], s_stat[:, 0:1],
                                                     1.0 / D))
            pl.op("v", lambda e: e.tensor_scalar_mul(s_stat[:, 3:4], s_stat[:, 4:5],
                                                     1.0 / D), inc="vs")
            pl.vself()
            pl.op("v", lambda e: e.tensor_mul(s_stat[:, 2:3], s_stat[:, 1:2],
                                              s_stat[:, 1:2]), inc="vs")
            pl.vself()
            pl.op("v", lambda e: e.tensor_sub(s_stat[:, 4:5], s_stat[:, 3:4],
                                              s_stat[:, 2:3]), inc="vs")
            v_var = pl.cur("vs")
            pl.w("s", "vs", v_var)
            s_std = pl.op("s", lambda e: e.activation(s_stat[:, 5:6], s_stat[:, 4:5],
                                                      AF.Sqrt, bias=s_eps[:, 0:1],
                                                      scale=1.0), inc="ss")
            pl.w("v", "ss", s_std)
            pl.op("v", lambda e: e.reciprocal(s_stat[:, 6:7], s_stat[:, 5:6]), inc="vs")
            pl.vself()
            pl.op("v", lambda e, y=stgb, o=stg: e.tensor_scalar(
                o, y, s_stat[:, 1:2], s_stat[:, 6:7], ALU.subtract, ALU.mult),
                inc="vs")
            v_xln = pl.cur("vs")
            pl.vself()
            xlb = arb.ap("xlnb", 0, 128, tt * 1024, (tt + 1) * 1024)
            ln1_done[tt] = pl.op("v", lambda e, d=xlb, s=stg: e.tensor_copy(d, s),
                                 inc="vs")
            pl.w("y", "vs", v_xln)
            xln_store[tt] = pl.op(
                "y", lambda e, s=stg, tt=tt: e.dma_start(
                    out=xlnd.ap()[tt * 128:(tt + 1) * 128, :], in_=s),
                inc="dxsa" if tt % 2 == 0 else "dxsb", by=16)

        ts_ln_all = pl.cur("ts")

        arb.alloc("xlnT", 2 * T)
        P2bf = P2.bitcast(BF16)
        lgchunks = [(P0, 0), (P0, 512), (P1, 0), (P1, 512)]
        xt_evac = {}
        lgmm_done = {}
        for dj in range(ND):
            xslot = dj % 2
            for tt in range(NT):
                k2 = dj * NT + tt
                psl = P2bf.ap()[0:128, (k2 % 2) * 1024:(k2 % 2) * 1024 + 128]
                if dj == 0:
                    pl.w("t", "vs", ln1_done[tt])
                    if tt >= NT - 2:
                        pl.w("t", "vs", ln1_add[tt])
                if k2 >= 2:
                    pl.w("t", "ss", xt_evac[k2 - 2])
                if tt == 0 and dj >= 2:
                    pl.w("s", "ts", lgmm_done[dj - 2])
                xsrc = arb.ap("xlnb", 0, 128, tt * 1024 + dj * 128,
                              tt * 1024 + dj * 128 + 128)
                tv = pl.op("t", lambda e, o=psl, s=xsrc:
                           e.transpose(o, s, s_idb[:, :]), inc="ts")
                pl.w("s", "ts", tv)
                xd = arb.ap("xlnT", 0, 128, xslot * T + tt * 128,
                            xslot * T + tt * 128 + 128)
                xt_evac[k2] = pl.op("s", lambda e, d=xd, p=psl: e.copy(d, p),
                                    inc="ss")
            pl.w("t", "ss", xt_evac[dj * NT + NT - 1])
            for half in range(2):
                lhs = s_gateTb[:, half * 64 + dj * 8:half * 64 + dj * 8 + 8]
                for ci, (lpt, lpc) in enumerate(lgchunks):
                    rhs = arb.ap("xlnT", 0, 128, xslot * T + ci * 512,
                                 xslot * T + ci * 512 + 512)
                    MM(lpt[0:8, lpc:lpc + 512], lhs, rhs,
                       dj == 0 and half == 0, dj == ND - 1 and half == 1,
                       inc="ts" if ci == 3 else None)
            lgmm_done[dj] = pl.cur("ts")
        arb.release("xlnT")

        pl.w("s", "ts", lgmm_done[ND - 1])
        for ci, (lpt, lpc) in enumerate(lgchunks):
            lgd = arf.ap("lgT", 0, 8, ci * 512, (ci + 1) * 512)
            pl.op("s", lambda e, d=lgd, p=lpt[0:8, lpc:lpc + 512]: e.copy(d, p),
                  inc="ss" if ci == 3 else None)
        s_lgev = pl.cur("ss")
        tm_evac = {}
        for tt in range(NT):
            slot = P2[0:128, 256:264] if tt % 2 == 0 else P2[0:128, 768:776]
            if tt == 0:
                pl.w("t", "ss", s_lgev)
            if tt >= 2:
                pl.w("t", "ss", tm_evac[tt - 2])
            lgs = arf.ap("lgT", 0, 8, tt * 128, (tt + 1) * 128)
            tv = pl.op("t", lambda e, o=slot, s=lgs:
                       e.transpose(o, s, s_id[0:8, 0:8]), inc="ts")
            pl.w("s", "ts", tv)
            tm_evac[tt] = pl.op("s", lambda e, p=slot, tt=tt:
                                e.copy(s_lgtall[:, tt * 8:(tt + 1) * 8], p), inc="ss")

        pl.w("v", "ss", tm_evac[NT - 1])
        pl.vself()
        for tt in range(NT):
            pl.op("v", lambda e, tt=tt: e.reduce_max(
                s_gs[:, tt:tt + 1], s_lgtall[:, tt * 8:(tt + 1) * 8], axis=AX),
                inc="vs" if tt == NT - 1 else None)
        pl.vself()
        for tt in range(NT):
            pl.op("v", lambda e, tt=tt: e.tensor_scalar(
                s_ls[:, tt * 8:(tt + 1) * 8], s_lgtall[:, tt * 8:(tt + 1) * 8],
                s_gs[:, tt:tt + 1], None, ALU.subtract),
                inc="vs" if tt == NT - 1 else None)
        v_lsall = pl.cur("vs")
        pl.w("s", "vs", v_lsall)
        pl.sself()
        s_exw = pl.op("s", lambda e: e.activation(s_les[:, :], s_ls[:, :], AF.Exp),
                      inc="ss")
        pl.w("v", "ss", s_exw)
        for tt in range(NT):
            pl.op("v", lambda e, tt=tt: e.reduce_sum(
                s_gs[:, 16 + tt:17 + tt], s_les[:, tt * 8:(tt + 1) * 8], axis=AX),
                inc="vs" if tt == NT - 1 else None)
        pl.vself()
        pl.op("v", lambda e: e.reciprocal(s_cols[:, 32:48], s_gs[:, 16:32]), inc="vs")
        pl.op("v", lambda e: e.tensor_scalar(s_M1f[:, :], s_ls[:, :], 0.0, None,
                                             ALU.is_ge), inc="vs")
        pl.vself()
        pl.op("v", lambda e: e.scalar_tensor_tensor(
            s_ls[:, :], s_iota8[:, :], 9.0, s_M1f[:, :], ALU.subtract, ALU.mult),
            inc="vs")
        pl.vself()
        pl.op("v", lambda e: e.tensor_scalar_add(s_les[:, :], s_ls[:, :], 9.0),
              inc="vs")
        pl.vself()
        for tt in range(NT):
            pl.op("v", lambda e, tt=tt: e.tensor_reduce(
                s_cols[:, 16 + tt:17 + tt], s_les[:, tt * 8:(tt + 1) * 8],
                axis=AX, op=ALU.min), inc="vs" if tt == NT - 1 else None)
        pl.vself()
        for tt in range(NT):
            pl.op("v", lambda e, tt=tt: e.tensor_scalar(
                s_M1f[:, tt * 8:(tt + 1) * 8], s_iota8[:, tt * 8:(tt + 1) * 8],
                s_cols[:, 16 + tt:17 + tt], None, ALU.is_equal),
                inc="vs" if tt == NT - 1 else None)
        pl.vself()
        v_m1all = pl.op("v", lambda e: e.tensor_copy(s_M1[:, :], s_M1f[:, :]),
                        inc="vs")

        rp = P2[:, 768:776]
        cp = P2[0:1, 776:784]
        toff_v = {-1: v_m1all}
        pl.w("t", "vs", v_m1all)
        for tt in range(NT):
            pl.w("t", "vs", toff_v[tt - 1])
            pl.op("t", lambda e, tt=tt: e.matmul(rp, s_ltriuf[:, :],
                                                 s_M1f[:, tt * 8:tt * 8 + 8],
                                                 start=True, stop=False))
            pl.op("t", lambda e: e.matmul(rp, s_onesf[0:1, :], s_toff[0:1, :],
                                          start=False, stop=True), inc="ts")
            t_cnt = pl.op("t", lambda e, tt=tt: e.matmul(
                cp, s_onescf[:, 0:1], s_M1f[:, tt * 8:tt * 8 + 8],
                start=True, stop=True), inc="ts")
            pl.w("s", "ts", t_cnt)
            s_rke = pl.op("s", lambda e, tt=tt:
                          e.copy(s_ranks[:, tt * 8:tt * 8 + 8], rp), inc="ss")
            pl.w("v", "ss", s_rke)
            pl.w("v", "ts", t_cnt)
            pl.op("v", lambda e, tt=tt: e.tensor_mul(
                s_ranks[:, tt * 8:tt * 8 + 8], s_ranks[:, tt * 8:tt * 8 + 8],
                s_M1f[:, tt * 8:tt * 8 + 8]), inc="vs")
            pl.vself()
            pl.op("v", lambda e, tt=tt: e.reduce_sum(
                s_cols[:, tt:tt + 1], s_ranks[:, tt * 8:tt * 8 + 8], axis=AX))
            toff_v[tt] = pl.op("v", lambda e: e.tensor_add(
                s_toff[0:1, :], s_toff[0:1, :], cp), inc="vs")
        v_gate_all = pl.cur("vs")

        pl.vself()
        v_cntc = pl.op("v", lambda e: e.tensor_copy(s_cnt[0:1, :], s_toff[0:1, :]),
                       inc="vs")
        pl.w("g", "vs", v_cntc)
        g_in = pl.op("g", lambda e: e.dma_start(out=cc_in.ap(), in_=s_cnt[0:1, :]),
                     inc="dAG", by=16)
        pl.w("g", "dAG", g_in)
        if USE_COLLECTIVE:
            pl.op("g", lambda e: e.collective_compute(
                "AllGather", ALU.bypass, replica_groups=[list(range(B))],
                ins=[cc_in.ap()], outs=[cc_out.ap()]), inc="cs", by=1)
        else:
            for r in range(E):
                pl.op("g", lambda e, r=r: e.dma_start(out=cc_out.ap()[r:r + 1, :],
                                                      in_=cc_in.ap()),
                      inc="cs", by=16 if r == E - 1 else 16)
        pl.w("g", "cs", pl.cur("cs"))
        g_ag = pl.op("g", lambda e: e.dma_start(out=s_ag[:, :], in_=cc_out.ap()),
                     inc="dAG", by=16)

        ts_msa_all = pl.cur("ts")
        arb.release("attnT")
        arb.release("wmsabA")
        if ND > WMSA_EARLY:
            arb.release("wmsabB")
        arf.release("x0st")
        arf.release("lgT")
        arf.alloc("b2r", 2 * 1024)

        arb.alloc("EO", 24 * 1024)
        arb.alloc("Ht", 32 * CAP)
        arb.alloc("G", NT * CAP)
        arb.alloc("D0", ND * CAP)
        arb.alloc("D1", ND * CAP)

        dslots = [(P0, 0), (P0, 512), (P1, 0), (P1, 512), (P2, 0), (P2, 512),
                  (P3, 0), (P4, 0)]
        g_built = {}
        d_evac = {}
        h_evac = {}
        eo_evac = {}
        disp_last = {}
        v_b2 = None

        def emit_G(ee):
            for tt in range(NT):
                if tt == 0:
                    if ee == 0:
                        pl.w("v", "vs", v_gate_all)
                        pl.w("v", "ts", ts_msa_all)
                    else:
                        pl.w("v", "ts", disp_last[ee - 1])
                    pl.vself()
                gdst = arb.ap("G", 0, 128, tt * CAP, (tt + 1) * CAP)
                g_built[(ee, tt)] = pl.op(
                    "v", lambda e, d=gdst, tt=tt, ee=ee: e.tensor_scalar(
                        d, s_iotaC[:, :], s_cols[:, tt:tt + 1],
                        s_M1f[:, tt * 8 + ee:tt * 8 + ee + 1], ALU.is_equal, ALU.mult),
                    inc="vs")

        emit_G(0)
        for e_ in range(E):
            dbuf = "D0" if e_ % 2 == 0 else "D1"
            for dj in range(ND):
                pt, pc = dslots[dj]
                if e_ >= 1:
                    pl.w("t", "vs", d_evac[(e_ - 1, dj)])
                    if dj == 0:
                        pl.w("t", "vs", eo_evac[(e_ - 1, 2)])
                for tt in range(NT):
                    if dj == 0:
                        pl.w("t", "vs", g_built[(e_, tt)])
                        if e_ == 0 and tt == 0:
                            pl.w("t", "vs", ln1_done[NT - 1])
                    lhs = arb.ap("xlnb", 0, 128, tt * 1024 + dj * 128,
                                 tt * 1024 + dj * 128 + 128)
                    rhs = arb.ap("G", 0, 128, tt * CAP, (tt + 1) * CAP)
                    MM(pt[:, pc:pc + CAP], lhs, rhs, tt == 0, tt == NT - 1,
                       inc="ts" if tt == NT - 1 else None)
                dmm = pl.cur("ts")
                pl.w("v", "ts", dmm)
                ddst = arb.ap(dbuf, 0, 128, dj * CAP, (dj + 1) * CAP)
                d_evac[(e_, dj)] = pl.op(
                    "v", lambda e, d=ddst, p=pt[:, pc:pc + CAP]: e.tensor_copy(d, p),
                    inc="vs")
            disp_last[e_] = pl.cur("ts")

            hsets = [[(P0, 0), (P0, 512), (P1, 0), (P1, 512)],
                     [(P2, 0), (P2, 512), (P3, 0), (P4, 0)]]
            for mg in range(8):
                sset = hsets[mg % 2]
                gmg = e_ * 8 + mg
                pl.w("t", w1_v[(e_, mg)][0], w1_v[(e_, mg)][1])
                if mg >= 2:
                    for mi in range(4):
                        pl.w("t", "ss", h_evac[(e_, mg - 2, mi)])
                for mi in range(4):
                    pt, pc = sset[mi]
                    for dj in range(ND):
                        if mg == 0 and mi == 0:
                            pl.w("t", "vs", d_evac[(e_, dj)])
                        lhs = arb.ap("w1buf", 0, 128,
                                     (gmg % 3) * ND * 512 + dj * 512 + mi * 128,
                                     (gmg % 3) * ND * 512 + dj * 512 + mi * 128 + 128)
                        rhs = arb.ap(dbuf, 0, 128, dj * CAP, (dj + 1) * CAP)
                        MM(pt[:, pc:pc + CAP], lhs, rhs, dj == 0, dj == ND - 1,
                           inc="ts" if dj == ND - 1 else None)
                    hmm = pl.cur("ts")
                    m_t = mg * 4 + mi
                    hdst = arb.ap("Ht", 0, 128, m_t * CAP, (m_t + 1) * CAP)
                    pl.w("s", "ts", hmm)
                    h_evac[(e_, mg, mi)] = pl.op(
                        "s", lambda e, d=hdst, p=pt[:, pc:pc + CAP],
                        b=s_b1c[:, e_ * 32 + m_t:e_ * 32 + m_t + 1]:
                        e.activation(d, p, AF.Relu, bias=b, scale=1.0), inc="ss")
                nxt = gmg + 3
                if nxt < E * 8:
                    load_w1(nxt // 8, nxt % 8, gmg % 3, wait_ts=pl.cur("ts"))

            if e_ == 0:
                b2_v = {}
                for ee in range(2):
                    b2_v[ee] = yld(arf.ap("b2r", 0, 1, (ee % 2) * 1024,
                                          (ee % 2) * 1024 + 1024),
                                   b2[ee:ee + 1, :], "dB2" if ee % 2 == 0 else "dB2b")
            par = e_ % 2
            pl.w("t", "dB2" if par == 0 else "dB2b", b2_v[e_])
            pl.w("t", "ss", h_evac[(e_, 7, 3)])
            if e_ >= 1:
                pl.w("t", "vs", v_b2)
                pl.w("t", "vs", eo_evac[(e_ - 1, 2)])
            MM(P3[0:128, 0:512], s_onesf[0:1, :],
               arf.ap("b2r", 0, 1, par * 1024, par * 1024 + 512), True, True)
            t_b2 = MM(P4[0:128, 0:512], s_onesf[0:1, :],
                      arf.ap("b2r", 0, 1, par * 1024 + 512, par * 1024 + 1024),
                      True, True, inc="ts")
            if e_ + 2 < E:
                pl.w("y", "ts", t_b2)
                b2_v[e_ + 2] = yld(arf.ap("b2r", 0, 1, par * 1024, par * 1024 + 1024),
                                   b2[e_ + 2:e_ + 3, :],
                                   "dB2" if par == 0 else "dB2b")
            pl.w("v", "ts", t_b2)
            pl.op("v", lambda e: e.tensor_copy(s_b2bc[:, 0:512], P3[0:128, 0:512]))
            v_b2 = pl.op("v", lambda e: e.tensor_copy(s_b2bc[:, 512:1024],
                                                      P4[0:128, 0:512]), inc="vs")

            if e_ + 1 < E:
                emit_G(e_ + 1)

            eopt = [(P0, 128), (P1, 128), (P2, 64)]
            pl.w("t", "ss", h_evac[(e_, 7, 3)])
            if e_ >= 1:
                for ci in range(3):
                    pl.w("t", "vs", eo_evac[(e_ - 1, ci)])
            for kg in range(16):
                gkg = e_ * 16 + kg
                pl.w("t", w2_v[(e_, kg)][0], w2_v[(e_, kg)][1])
                for kj in range(2):
                    kt = kg * 2 + kj
                    for ci in range(3):
                        pt, crows = eopt[ci]
                        lhs = arb.ap("Ht", 0, 128, kt * CAP + ci * 128,
                                     kt * CAP + ci * 128 + crows)
                        for nch in range(2):
                            rhs = arb.ap("w2buf", 0, 128,
                                         (gkg % 3) * 2048 + kj * 1024 + nch * 512,
                                         (gkg % 3) * 2048 + kj * 1024 + nch * 512 + 512)
                            MM(pt[0:crows, nch * 512:nch * 512 + 512], lhs, rhs,
                               kt == 0, kt == 31,
                               inc="ts" if (kj == 1 and ci == 2 and nch == 1) else None)
                nxt = gkg + 3
                if nxt < E * 16:
                    load_w2(nxt // 16, nxt % 16, gkg % 3, wait_ts=pl.cur("ts"))
            eomm = pl.cur("ts")
            for ci in range(3):
                pt, crows = eopt[ci]
                pl.w("v", "ts", eomm)
                pl.w("v", "vs", v_b2)
                edst = arb.ap("EO", 0, crows, (e_ * 3 + ci) * 1024,
                              (e_ * 3 + ci) * 1024 + 1024)
                eo_evac[(e_, ci)] = pl.op(
                    "v", lambda e, d=edst, p=pt[0:crows, 0:1024], cr=crows:
                    e.tensor_add(d, p, s_b2bc[0:cr, 0:1024]), inc="vs")

        ts_moe_done = pl.cur("ts")
        arb.release("G")
        arb.release("D0")
        arb.release("D1")
        arb.release("Ht")
        arb.release("w1buf")
        arb.release("w2buf")
        arb.release("xlnb")
        arb.alloc("pebc", E * T)
        arb.alloc("CW", 3 * 24 * 128)
        arb.alloc("gsall", 128)
        arb.alloc("petr", 128)
        arf.release("b2r")
        arf.alloc("rrow", 2048)
        arf.alloc("rbcf", 2 * 1024)

        pl.w("t", "dAG", g_ag)
        pl.w("t", "vs", eo_evac[(E - 1, 2)])
        t_om = MM(P3[0:1, 0:8], s_prefix[:, :], s_ag[:, :], True, True, inc="ts")
        pl.w("v", "ts", t_om)
        pl.vself()
        v_off = pl.op("v", lambda e: e.tensor_copy(s_off[0:1, :], P3[0:1, 0:8]), inc="vs")
        pl.w("t", "vs", v_off)
        t_ob = MM(P3[0:128, 8:16], s_onesf[0:1, :], s_off[0:1, :], True, True, inc="ts")
        pl.w("v", "ts", t_ob)
        v_offbc = pl.op("v", lambda e: e.tensor_copy(s_offbc[:, 0:8], P3[0:128, 8:16]),
                        inc="vs")

        pl.vself()
        for tt in range(NT):
            pl.op("v", lambda e, tt=tt: e.tensor_mul(
                s_les[:, tt * 8:(tt + 1) * 8], s_offbc[:, 0:8],
                s_M1f[:, tt * 8:(tt + 1) * 8]), inc="vs" if tt == NT - 1 else None)
        pl.vself()
        for tt in range(NT):
            pl.op("v", lambda e, tt=tt: e.reduce_sum(
                s_gs[:, 32 + tt:33 + tt], s_les[:, tt * 8:(tt + 1) * 8], axis=AX),
                inc="vs" if tt == NT - 1 else None)
        pl.vself()
        pl.op("v", lambda e: e.tensor_add(s_ls[:, 0:16], s_cols[:, 0:16],
                                          s_gs[:, 32:48]), inc="vs")
        pl.vself()
        pl.op("v", lambda e: e.tensor_scalar(s_ls[:, 16:32], s_ls[:, 0:16],
                                             float(GCAP), None, ALU.is_lt), inc="vs")
        pl.vself()
        pl.op("v", lambda e: e.tensor_mul(s_cols[:, 48:64], s_cols[:, 32:48],
                                          s_ls[:, 16:32]), inc="vs")
        v_pk_all = pl.cur("vs")

        tr_evac = {}
        for tt in range(NT):
            psl = P2[0:1, (tt % 2) * 512:(tt % 2) * 512 + 128]
            if tt >= 2:
                pl.w("t", "ss", tr_evac[tt - 2])
            pl.w("t", "vs", v_pk_all if tt == 0 else None)
            tv = pl.op("t", lambda e, o=psl, tt=tt: e.transpose(
                o, s_cols[:, tt:tt + 1], s_id[:, :]), inc="ts")
            pl.w("s", "ts", tv)
            tr_evac[tt] = pl.op("s", lambda e, p=psl,
                                d=arf.ap("rrow", 0, 1, tt * 128, (tt + 1) * 128):
                                e.copy(d, p), inc="ss")
        pl.w("t", "ss", tr_evac[NT - 1])
        pl.w("t", "ss", tr_evac[NT - 2])
        rb_slots = [(P2, 0), (P2, 512), (P3, 0), (P4, 0)]
        for ci, (pt, pc) in enumerate(rb_slots):
            MM(pt[0:128, pc:pc + 512], s_onesf[0:1, :],
               arf.ap("rrow", 0, 1, ci * 512, (ci + 1) * 512), True, True,
               inc="ts" if ci == 3 else None)
        t_rbc = pl.cur("ts")
        pl.w("v", "ts", t_rbc)
        for ci, (pt, pc) in enumerate(rb_slots):
            pl.op("v", lambda e, p=pt[0:128, pc:pc + 512], ci=ci: e.tensor_copy(
                arf.ap("rbcf", 0, 128, ci * 512, (ci + 1) * 512), p),
                inc="vs" if ci == 3 else None)
        v_rbcf = pl.cur("vs")

        for ee in range(E):
            for tt in range(NT):
                col = ee * 16 + tt
                pl.op("v", lambda e, col=col, tt=tt, ee=ee: e.tensor_mul(
                    arb.ap("gsall", 0, 128, col, col + 1),
                    s_cols[:, 48 + tt:49 + tt],
                    s_M1f[:, tt * 8 + ee:tt * 8 + ee + 1]),
                    inc="vs" if col == 127 else None)
        v_gs = pl.cur("vs")
        pl.w("t", "vs", v_gs)
        t_petr = pl.op("t", lambda e: e.transpose(
            P2bf.ap()[0:128, 0:128], arb.ap("gsall", 0, 128, 0, 128),
            s_idb[:, :]), inc="ts")
        pl.w("s", "ts", t_petr)
        s_petr = pl.op("s", lambda e: e.copy(
            arb.ap("petr", 0, 128, 0, 128), P2bf.ap()[0:128, 0:128]), inc="ss")
        pl.w("y", "ss", s_petr)
        pdout = yld(ped.ap(), arb.ap("petr", 0, 128, 0, 128), "dBCa")
        pl.w("y", "dBCa", pdout)
        pl.w("g", "dBCa", pdout)
        pl.w("g", "ts", pl.cur("ts"))
        for ee in range(E):
            eng = "y" if ee % 2 == 0 else "g"
            sem = "dBCa" if ee % 2 == 0 else "dBCb"
            bsrc = bass.AP(ped, ee * 2048, [[0, 128], [1, 2048]])
            bdst = arb.ap("pebc", 0, 128, ee * T, ee * T + 2048)
            pl.op(eng, lambda e, s=bsrc, d=bdst: e.dma_start(out=d, in_=s),
                  inc=sem, by=16)
        pbca = pl.cur("dBCa")
        pbcb = pl.cur("dBCb")

        arf.release("rrow")
        arf.alloc("xlnld", 2 * 1024)
        xln_v = {}
        pl.w("y", "ts", t_rbc)
        for tt in range(2):
            pl.w("y", "dxsa" if tt % 2 == 0 else "dxsb", xln_store[tt])
            xln_v[tt] = yld(arf.ap("xlnld", 0, 128, (tt % 2) * 1024, (tt % 2) * 1024 + 1024),
                            xlnd.ap()[tt * 128:(tt + 1) * 128, :],
                            "dxla" if tt % 2 == 0 else "dxlb")

        cw_built = {}
        cw_read = {}
        ln2_add = {}
        out_v = {}

        def emit_cw_and_mm(tt):
            if tt >= 3:
                pl.w("v", "ts", cw_read[tt - 3])
            pl.vself()
            if tt == 0:
                pl.w("v", "dBCa", pbca)
                pl.w("v", "dBCb", pbcb)
            slot = (tt % 3) * 24 * 128
            for e_ in range(E):
                for ci in range(3):
                    idx = e_ * 3 + ci
                    crows = 128 if ci < 2 else 64
                    cdst = arb.ap("CW", 0, crows, slot + idx * 128,
                                  slot + idx * 128 + 128)
                    pin = arb.ap("pebc", 0, crows, e_ * T + tt * 128,
                                 e_ * T + (tt + 1) * 128)
                    rin = arf.ap("rbcf", 0, crows, tt * 128, (tt + 1) * 128)
                    pl.op("v", lambda e, d=cdst, r=rin, p=pin, ci=ci, cr=crows:
                          e.scalar_tensor_tensor(d, r, s_iotac3[0:cr, ci:ci + 1],
                                                 p, ALU.is_equal, ALU.mult),
                          inc="vs" if idx == 23 else None)
            cw_built[tt] = pl.cur("vs")

            pt = P0 if tt % 2 == 0 else P1
            pl.w("t", "vs", cw_built[tt])
            if tt >= 2:
                pl.w("t", "vs", ln2_add[tt - 2])
            first = True
            for e_ in range(E):
                for ci in range(3):
                    crows = 128 if ci < 2 else 64
                    lhs = arb.ap("CW", 0, crows, slot + (e_ * 3 + ci) * 128,
                                 slot + (e_ * 3 + ci) * 128 + 128)
                    last = (e_ == E - 1 and ci == 2)
                    for nch in range(2):
                        rhs = arb.ap("EO", 0, crows, (e_ * 3 + ci) * 1024 + nch * 512,
                                     (e_ * 3 + ci) * 1024 + nch * 512 + 512)
                        MM(pt[:, nch * 512:nch * 512 + 512], lhs, rhs, first,
                           last, inc="ts" if (last and nch == 1) else None)
                    first = False
            cw_read[tt] = pl.cur("ts")
            return pl.cur("ts")

        ans_mm = {}

        def emit_epilogue(tt):
            pt = P0 if tt % 2 == 0 else P1
            stg = arf.ap("stgA", 0, 128, (tt % 2) * 1024, (tt % 2) * 1024 + 1024)
            stgb = arf.ap("stgB", 0, 128, 0, 1024)
            xl = arf.ap("xlnld", 0, 128, (tt % 2) * 1024, (tt % 2) * 1024 + 1024)
            pl.w("v", "ts", ans_mm[tt])
            pl.w("v", "dxla" if tt % 2 == 0 else "dxlb", xln_v[tt])
            if tt == 0:
                pl.w("v", "dxsa", xln_store[NT - 2])
                pl.w("v", "dxsb", xln_store[NT - 1])
            if tt >= 2:
                pl.w("v", "dOa" if tt % 2 == 0 else "dOb", 16 * (tt // 2))
            pl.vself()
            ln2_add[tt] = pl.op("v", lambda e, d=stgb, p=pt[:, 0:1024], x=xl:
                                e.tensor_add(d, p, x), inc="vs")
            if tt + 2 < NT:
                pl.w("y", "vs", ln2_add[tt])
                pl.w("y", "dxsa" if tt % 2 == 0 else "dxsb", xln_store[tt + 2])
                xln_v[tt + 2] = yld(
                    arf.ap("xlnld", 0, 128, (tt % 2) * 1024, (tt % 2) * 1024 + 1024),
                    xlnd.ap()[(tt + 2) * 128:(tt + 3) * 128, :],
                    "dxla" if tt % 2 == 0 else "dxlb")
            pl.w("s", "vs", ln2_add[tt])
            s_sq2 = pl.op("s", lambda e, o=stg, y=stgb: e.activation(o, y, AF.Square),
                          inc="ss")
            pl.vself()
            pl.op("v", lambda e, y=stgb: e.reduce_sum(s_stat[:, 0:1], y, axis=AX))
            pl.w("v", "ss", s_sq2)
            pl.op("v", lambda e, y=stg: e.reduce_sum(s_stat[:, 4:5], y, axis=AX),
                  inc="vs")
            pl.vself()
            pl.op("v", lambda e: e.tensor_scalar_mul(s_stat[:, 1:2], s_stat[:, 0:1],
                                                     1.0 / D))
            pl.op("v", lambda e: e.tensor_scalar_mul(s_stat[:, 3:4], s_stat[:, 4:5],
                                                     1.0 / D), inc="vs")
            pl.vself()
            pl.op("v", lambda e: e.tensor_mul(s_stat[:, 2:3], s_stat[:, 1:2],
                                              s_stat[:, 1:2]), inc="vs")
            pl.vself()
            pl.op("v", lambda e: e.tensor_sub(s_stat[:, 4:5], s_stat[:, 3:4],
                                              s_stat[:, 2:3]), inc="vs")
            v_var = pl.cur("vs")
            pl.w("s", "vs", v_var)
            s_std = pl.op("s", lambda e: e.activation(s_stat[:, 5:6], s_stat[:, 4:5],
                                                      AF.Sqrt, bias=s_eps[:, 0:1],
                                                      scale=1.0), inc="ss")
            pl.w("v", "ss", s_std)
            pl.op("v", lambda e: e.reciprocal(s_stat[:, 6:7], s_stat[:, 5:6]), inc="vs")
            pl.vself()
            v_out = pl.op("v", lambda e, y=stgb, o=stg: e.tensor_scalar(
                o, y, s_stat[:, 1:2], s_stat[:, 6:7], ALU.subtract, ALU.mult),
                inc="vs")
            oeng = "y" if tt % 2 == 0 else "g"
            pl.w(oeng, "vs", v_out)
            out_v[tt] = pl.op(oeng, lambda e, s=stg, tt=tt: e.dma_start(
                out=out[tt * 128:(tt + 1) * 128, :], in_=s),
                inc="dOa" if tt % 2 == 0 else "dOb", by=16)

        for tt in range(NT):
            ans_mm[tt] = emit_cw_and_mm(tt)
            if tt >= 1:
                emit_epilogue(tt - 1)
        emit_epilogue(NT - 1)

        pl.w("y", "dOa", 16 * (NT // 2))
        pl.w("y", "dOb", 16 * (NT // 2))

        if arb.peak > ARB_N or arf.peak > ARF_N:
            raise MemoryError(f"arena peaks {arb.peak} {arf.peak}")
        blk = EN(nc.Block())

        @blk.tensor
        def _(t):
            for f in pl.ops["t"]:
                f(t)

        @blk.vector
        def _(v):
            for f in pl.ops["v"]:
                f(v)

        @blk.scalar
        def _(s):
            for f in pl.ops["s"]:
                f(s)

        @blk.gpsimd
        def _(g):
            for f in pl.ops["g"]:
                f(g)

        @blk.sync
        def _(y):
            for f in pl.ops["y"]:
                f(y)

    return nc


def make_host_inputs(L0_attn_W_qkv, L0_attn_b_qkv, L0_attn_W_msa, L0_attn_b_msa,
                     L0_ffn_gate_w, L0_ffn_w1, L0_ffn_b1, L0_ffn_w2, L0_ffn_b2):
    W = np.asarray(L0_attn_W_qkv, dtype=np.float32)
    bq = np.asarray(L0_attn_b_qkv, dtype=np.float32)
    wqk = np.concatenate([W[:, h * 192:h * 192 + 64] for h in range(H)]
                         + [W[:, h * 192 + 64:h * 192 + 128] for h in range(H)], axis=1)
    wv = np.concatenate([W[:, h * 192 + 128:h * 192 + 192] for h in range(H)], axis=1)
    bqk = np.concatenate([bq[h * 192:h * 192 + 64] for h in range(H)]
                         + [bq[h * 192 + 64:h * 192 + 128] for h in range(H)])
    bqkc = np.ascontiguousarray(bqk.reshape(16, 128).T)
    bvr = np.concatenate([bq[h * 192 + 128:h * 192 + 192] for h in range(H)])[None, :]
    wmsa = np.asarray(L0_attn_W_msa, dtype=np.float32)
    bmsar = np.asarray(L0_attn_b_msa, dtype=np.float32)[None, :]
    gateT = np.ascontiguousarray(np.asarray(L0_ffn_gate_w, dtype=np.float32).T)
    w1 = np.asarray(L0_ffn_w1, dtype=np.float32)
    b1 = np.asarray(L0_ffn_b1, dtype=np.float32)
    b1c = np.ascontiguousarray(b1.reshape(E, 32, 128).transpose(0, 2, 1))
    w2 = np.asarray(L0_ffn_w2, dtype=np.float32)
    b2 = np.asarray(L0_ffn_b2, dtype=np.float32)
    consts = {
        "c_ident": np.eye(128, dtype=np.float32),
        "c_iotaC": np.broadcast_to(np.arange(CAP, dtype=np.float32)[None, :],
                                   (128, CAP)).copy(),
        "c_iotac3": np.stack([np.arange(128, dtype=np.float32) + 128 * ci
                              for ci in range(3)], axis=1).astype(np.float32),
        "c_ltriu": np.triu(np.ones((128, 128), dtype=np.float32), 1),
        "c_iota8": np.broadcast_to(np.tile(np.arange(8, dtype=np.float32), 16)[None, :],
                                   (128, 128)).copy(),
    }
    shared = {"wqk": np.ascontiguousarray(wqk), "bqkc": bqkc,
              "wv": np.ascontiguousarray(wv),
              "bvr": np.ascontiguousarray(bvr),
              "wmsa": np.ascontiguousarray(wmsa),
              "bmsar": np.ascontiguousarray(bmsar),
              "gateT": gateT, "w1": np.ascontiguousarray(w1), "b1c": b1c,
              "w2": np.ascontiguousarray(w2), "b2": np.ascontiguousarray(b2)}
    shared.update(consts)
    return shared


def kernel(L0_attn_z, L0_attn_W_qkv, L0_attn_b_qkv, L0_attn_W_msa, L0_attn_b_msa,
           L0_ffn_gate_w, L0_ffn_w1, L0_ffn_b1, L0_ffn_w2, L0_ffn_b2):
    z = np.asarray(L0_attn_z, dtype=np.float32)
    shared = make_host_inputs(L0_attn_W_qkv, L0_attn_b_qkv, L0_attn_W_msa,
                              L0_attn_b_msa, L0_ffn_gate_w, L0_ffn_w1, L0_ffn_b1,
                              L0_ffn_w2, L0_ffn_b2)
    nc = build_program()
    in_maps = []
    for c in range(B):
        m = dict(shared)
        m["x0"] = np.ascontiguousarray(z[c])
        pref = np.zeros((E, 1), dtype=np.float32)
        pref[:c, 0] = 1.0
        m["prefix"] = pref
        in_maps.append(m)
    res = run_bass_kernel_spmd(nc, in_maps, core_ids=list(range(B)))
    outs = [np.asarray(res.results[c]["out"], dtype=np.float32) for c in range(B)]
    return np.stack(outs, axis=0)
